# version 01
# baseline (speedup 1.0000x reference)
import numpy as np
from contextlib import ExitStack
import concourse.bass as bass
import concourse.mybir as mybir
from concourse.bass_utils import run_bass_kernel_spmd
from concourse.alu_op_type import AluOpType as ALU

F32 = mybir.dt.float32
BF16 = mybir.dt.bfloat16
F16 = mybir.dt.float16
U32 = mybir.dt.uint32
I32 = mybir.dt.int32
I16 = mybir.dt.int16
AF = mybir.ActivationFunctionType
AX = mybir.AxisListType

ENGINES = ("sync", "vector", "scalar", "gpsimd", "tensor")


class Ck:
    __slots__ = ("sem", "val")

    def __init__(self, sem, val):
        self.sem = sem
        self.val = val


class Prog:
    def __init__(self):
        self.nc = bass.Bass(target_bir_lowering=False)
        self.ctx = ExitStack()
        self.records = {e: [] for e in ENGINES}
        self.cnt = {}
        self.sem_names = []
        self.watermark = {e: {} for e in ENGINES}
        self._built = False
        self._sbuf_bytes = 0
        for e in ENGINES:
            self._new_sem("@" + e)

    def _new_sem(self, name):
        if name not in self.cnt:
            self.cnt[name] = 0
            self.sem_names.append(name)
        return name

    def dram(self, name, shape, dtype, kind="Internal", **kw):
        return self.nc.dram_tensor(name, list(shape), dtype, kind=kind, **kw).ap()

    def sbuf(self, name, shape, dtype):
        t = self.ctx.enter_context(self.nc.sbuf_tensor(name, list(shape), dtype))
        free_elems = 1
        for s in shape[1:]:
            free_elems *= s
        self._sbuf_bytes += free_elems * mybir.dt.size(dtype)
        return t

    def psum(self, name, shape, dtype=F32):
        return self.ctx.enter_context(self.nc.psum_tensor(name, list(shape), dtype))

    def wait(self, eng, waits):
        wm = self.watermark[eng]
        best = {}
        for ck in waits:
            if ck is None:
                continue
            if ck.sem not in best or ck.val > best[ck.sem].val:
                best[ck.sem] = ck
        for sem, ck in best.items():
            if wm.get(sem, -1) >= ck.val:
                continue
            wm[sem] = ck.val
            self.records[eng].append(("wait", sem, ck.val))

    SERIAL = ("vector", "scalar", "gpsimd")

    def op(self, eng, fn, waits=(), inc=1, stream=None, _is_dma=False):
        w = tuple(waits)
        if not _is_dma and eng in self.SERIAL:
            assert inc or stream, "serialized engines need inc>0 compute ops"
            w = w + (Ck("@" + eng, self.cnt["@" + eng]),)
        self.wait(eng, w)
        sem = "@" + eng if stream is None else self._new_sem(stream)
        self.records[eng].append(("op", fn, inc, sem))
        if inc:
            self.cnt[sem] += inc
            return Ck(sem, self.cnt[sem])
        return None

    def dma(self, queue, out, in_, waits=(), stream=None, **kw):
        if stream is None:
            tname = None
            for ap in (out, in_):
                t = getattr(ap, "tensor", None)
                if t is not None and getattr(t, "name", None) is not None:
                    try:
                        sp = ap.space
                    except Exception:
                        sp = None
                    if str(sp).endswith("SBUF"):
                        tname = t.name
                        break
            if tname is None:
                tname = out.tensor.name
            stream = "dma_" + tname
        return self.op(queue, lambda e: e.dma_start(out=out, in_=in_, **kw),
                       waits, inc=16, stream=stream, _is_dma=True)

    def ck(self, eng_or_sem):
        sem = "@" + eng_or_sem if eng_or_sem in ENGINES else eng_or_sem
        return Ck(sem, self.cnt[sem])

    def ccol(self, val, dtype=F32):
        key = (float(val), dtype)
        if not hasattr(self, "_ccols"):
            self._ccols = {}
        if key not in self._ccols:
            t = self.sbuf(f"ccol{len(self._ccols)}", [128, 1], dtype)
            self.op("vector", (lambda t=t, val=val: lambda e: e.memset(
                t[:], float(val)))())
            self._ccols[key] = (t, self.ck("vector"))
        return self._ccols[key]

    def build(self):
        assert not self._built
        self._built = True
        assert len(self.sem_names) <= 100, \
            f"too many semaphores: {len(self.sem_names)}"
        with ExitStack() as sctx:
            sems = {}
            for name in self.sem_names:
                sems[name] = sctx.enter_context(self.nc.semaphore(name))
            block = sctx.enter_context(self.nc.Block())
            for e in ENGINES:
                recs = self.records[e]
                if not recs:
                    continue
                dec = getattr(block, e)

                def make_body(recs_):
                    def body(engobj):
                        for r in recs_:
                            if r[0] == "wait":
                                engobj.wait_ge(sems[r[1]], r[2])
                            else:
                                _, fn, inc, sem = r
                                inst = fn(engobj)
                                if inc:
                                    assert inst is not None
                                    inst.then_inc(sems[sem], inc)
                    return body

                dec(make_body(recs))
        return self.nc


def ceil_div(a, b):
    return (a + b - 1) // b


"""Layer emitters for the megakernel. All emit onto a Prog (mkcore)."""
import numpy as np

T = 2048
D = 1024
NT = T // 128
NC = D // 128


class Consts:

    @staticmethod
    def host_arrays():
        import numpy as _np
        import ml_dtypes
        zlo = _np.zeros((128, 128), _np.float32)
        for i in range(64):
            zlo[64 + i, i] = 1.0
        return {"c_ident": _np.eye(128, dtype=ml_dtypes.bfloat16),
                "c_identh": _np.eye(128, dtype=_np.float16),
                "c_identlo": zlo.astype(ml_dtypes.bfloat16)}

    def __init__(self, P, f16=False):
        self.ident = P.sbuf("c_ident_s", [128, 128], BF16)
        self.ident_hbm = P.dram("c_ident", [128, 128], BF16, kind="ExternalInput")
        self.ck_ident = P.dma("sync", self.ident[:], self.ident_hbm)
        if f16:
            self.identh = P.sbuf("c_identh_s", [128, 128], F16)
            self.identh_hbm = P.dram("c_identh", [128, 128], F16, kind="ExternalInput")
            self.ck_identh = P.dma("sync", self.identh[:], self.identh_hbm)
        self.identlo = None

    def load_identlo(self, P):
        if self.identlo is None:
            self.identlo = P.sbuf("c_identlo_s", [128, 128], BF16)
            self.identlo_hbm = P.dram("c_identlo", [128, 128], BF16,
                                      kind="ExternalInput")
            self.ck_identlo = P.dma("sync", self.identlo[:], self.identlo_hbm)
        return self.identlo, self.ck_identlo


def mm_accum(P, psum_ap, pairs, waits=(), tail_waits=()):
    n = len(pairs)
    ck = None
    for i, (l, r) in enumerate(pairs):
        w = tuple(waits) if i == 0 else ()
        if i == n - 1:
            w = w + tuple(tail_waits)
        ck = P.op("tensor",
                  (lambda l=l, r=r, i=i: lambda e: e.matmul(
                      out=psum_ap, lhsT=l, rhs=r, start=(i == 0), stop=(i == n - 1)))(),
                  waits=w, inc=(1 if i == n - 1 else 0))
    return ck


class Transposer:

    def __init__(self, P, consts, name="tp", dtype=BF16):
        self.P = P
        self.c = consts
        self.dtype = dtype
        self.ident = consts.identh if dtype == F16 else consts.ident
        self.ck_id = consts.ck_identh if dtype == F16 else consts.ck_ident
        self.ps = [P.psum(f"{name}_ps{i}", [128, 128], dtype) for i in range(2)]
        self.evac_ck = [None, None]
        self.i = 0

    def t(self, src_ap, dst_ap, waits=(), evac_eng="scalar"):
        P = self.P
        k = self.i % 2
        self.i += 1
        sp, sf = src_ap.shape[0], src_ap.shape[1]
        w = tuple(waits) + ((self.ck_id,) if self.i <= 2 else ())
        if self.evac_ck[k] is not None:
            w = w + (self.evac_ck[k],)
        try:
            base = src_ap.base_partition()
        except Exception:
            base = 0
        if base == 64:
            idl, idl_ck = self.c.load_identlo(P)
            ident_ap = idl[64:128, 0:sp]
            w = w + (idl_ck,)
        else:
            assert base == 0, f"transpose src base {base}"
            ident_ap = self.ident[0:sp, 0:sp]
        tck = P.op("tensor", (lambda k=k, src_ap=src_ap, sp=sp, sf=sf,
                              ident_ap=ident_ap: lambda e: e.transpose(
            out=self.ps[k][0:sf, 0:sp], in_=src_ap,
            identity=ident_ap))(), waits=w)
        if evac_eng == "scalar":
            eck = P.op("scalar", (lambda k=k, dst_ap=dst_ap, sp=sp, sf=sf:
                                  lambda e: e.activation(
                out=dst_ap, in_=self.ps[k][0:sf, 0:sp], func=AF.Copy))(),
                waits=(tck,))
        else:
            eck = P.op("vector", (lambda k=k, dst_ap=dst_ap, sp=sp, sf=sf:
                                  lambda e: e.tensor_copy(
                dst_ap, self.ps[k][0:sf, 0:sp]))(), waits=(tck,))
        self.evac_ck[k] = eck
        return eck


def transpose_tiles(P, tp, src_tiles, dst_chunks, waits_per_tile=None,
                    nt=NT, nch=NC):
    cks = []
    for c in range(nch):
        last = None
        for t in range(nt):
            w = ()
            if waits_per_tile is not None and waits_per_tile[t] is not None:
                w = (waits_per_tile[t],)
            last = tp.t(src_tiles[t][:, c * 128:(c + 1) * 128],
                        dst_chunks[c][:, t * 128:(t + 1) * 128], waits=w)
        cks.append(last)
    return cks


def emit_ln_stats_tile(P, x_ap, dump_ap, stat, t, waits=(), nt=NT):
    P.op("vector", lambda e: e.tensor_scalar(
        out=dump_ap, in0=x_ap, scalar1=1.0, scalar2=0.0,
        op0=ALU.mult, op1=ALU.add, accum_out=stat[:, t:t + 1]),
        waits=tuple(waits))
    P.op("vector", lambda e: e.scalar_tensor_tensor(
        out=dump_ap, in0=x_ap, scalar=1.0, in1=x_ap,
        op0=ALU.bypass, op1=ALU.mult, accum_out=stat[:, nt + t:nt + t + 1]))
    return P.ck("vector")


def emit_ln_finalize(P, stat, nt=NT, dvar=D, eps=1e-5):
    sumx = stat[:, 0:nt]
    sumx2 = stat[:, nt:2 * nt]
    mu = stat[:, 2 * nt:3 * nt]
    rstd = stat[:, 3 * nt:4 * nt]
    P.op("vector", lambda e: e.tensor_scalar(
        out=mu, in0=sumx, scalar1=1.0 / dvar, scalar2=None, op0=ALU.mult),
        waits=(P.ck("vector"),))
    P.op("vector", lambda e: e.tensor_tensor(
        out=rstd, in0=mu, in1=mu, op=ALU.mult), waits=(P.ck("vector"),))
    P.op("vector", lambda e: e.scalar_tensor_tensor(
        out=rstd, in0=sumx2, scalar=1.0 / dvar, in1=rstd,
        op0=ALU.mult, op1=ALU.subtract), waits=(P.ck("vector"),))
    epst, epsck = P.ccol(float(eps))
    P.op("scalar", lambda e: e.activation(
        out=rstd, in_=rstd, func=AF.Sqrt, bias=epst[:, 0:1], scale=1.0),
        waits=(P.ck("vector"), epsck))
    P.op("vector", lambda e: e.reciprocal(out=rstd, in_=rstd),
         waits=(P.ck("scalar"),))
    return P.ck("vector")


def ln_affine(P, x_ap, out_ap, stat, t, waits=(), nt=NT):
    mu = stat[:, 2 * nt + t:2 * nt + t + 1]
    rstd = stat[:, 3 * nt + t:3 * nt + t + 1]
    P.op("vector", lambda e: e.tensor_scalar(
        out=out_ap, in0=x_ap, scalar1=mu, scalar2=rstd,
        op0=ALU.subtract, op1=ALU.mult), waits=tuple(waits))
    return P.ck("vector")


def emit_ln(P, x_tiles, xhat_tiles, stat, x_cks=None, nt=NT, dvar=D, eps=1e-5):
    sumx = stat[:, 0:nt]
    sumx2 = stat[:, nt:2 * nt]
    mu = stat[:, 2 * nt:3 * nt]
    rstd = stat[:, 3 * nt:4 * nt]
    for t in range(nt):
        w = (x_cks[t],) if x_cks else ()
        P.op("vector", (lambda t=t: lambda e: e.tensor_scalar(
            out=xhat_tiles[t][:], in0=x_tiles[t][:], scalar1=1.0, scalar2=0.0,
            op0=ALU.mult, op1=ALU.add, accum_out=sumx[:, t:t + 1]))(), waits=w)
        P.op("vector", (lambda t=t: lambda e: e.scalar_tensor_tensor(
            out=xhat_tiles[t][:], in0=x_tiles[t][:], scalar=1.0, in1=x_tiles[t][:],
            op0=ALU.bypass, op1=ALU.mult, accum_out=sumx2[:, t:t + 1]))())
    ck0 = P.ck("vector")
    P.op("vector", lambda e: e.tensor_scalar(
        out=mu, in0=sumx, scalar1=1.0 / dvar, scalar2=None, op0=ALU.mult),
        waits=(ck0,))
    ckmu = P.ck("vector")
    P.op("vector", lambda e: e.tensor_tensor(
        out=rstd, in0=mu, in1=mu, op=ALU.mult), waits=(ckmu,))
    P.op("vector", lambda e: e.scalar_tensor_tensor(
        out=rstd, in0=sumx2, scalar=1.0 / dvar, in1=rstd,
        op0=ALU.mult, op1=ALU.subtract), waits=(P.ck("vector"),))
    ckv = P.ck("vector")
    P.op("scalar", lambda e: e.activation(
        out=rstd, in_=rstd, func=AF.Sqrt, bias=float(eps), scale=1.0),
        waits=(ckv,))
    cksq = P.ck("scalar")
    P.op("vector", lambda e: e.reciprocal(out=rstd, in_=rstd), waits=(cksq,))
    ckr = P.ck("vector")
    tile_cks = []
    for t in range(nt):
        P.op("vector", (lambda t=t: lambda e: e.tensor_scalar(
            out=xhat_tiles[t][:], in0=x_tiles[t][:],
            scalar1=mu[:, t:t + 1], scalar2=rstd[:, t:t + 1],
            op0=ALU.subtract, op1=ALU.mult))(), waits=(ckr,))
        tile_cks.append(P.ck("vector"))
    return tile_cks, P.ck("vector")


QUANT_MIDS = [0.25, 0.75, 1.25, 1.75, 2.5, 3.5, 5.0]


def emit_quant(P, src_ap, dst_ap, scr, width, waits=(), sign_eng="scalar"):
    nb = width // 16
    np_ = src_ap.shape[0]
    mx = scr["mx"][0:np_, 0:nb]
    inv = scr["inv"][0:np_, 0:nb]
    a = scr["a"][0:np_, 0:width]
    q = scr["q"][0:np_, 0:width]
    sgn = scr["sgn"][0:np_, 0:width]
    src3 = src_ap.rearrange("p (w c) -> p w c", c=16)
    P.op("vector", lambda e: e.tensor_reduce(
        out=mx, in_=src3, axis=AX.X, op=ALU.max,
        apply_absolute_value=True), waits=tuple(waits))
    P.op("vector", lambda e: e.tensor_scalar(
        out=mx, in0=mx, scalar1=1.0 / 6.0, scalar2=1e-12,
        op0=ALU.mult, op1=ALU.max), waits=(P.ck("vector"),))
    P.op("vector", lambda e: e.reciprocal(out=inv, in_=mx),
         waits=(P.ck("vector"),))
    ck_inv = P.ck("vector")
    P.op(sign_eng, lambda e: e.activation(out=sgn, in_=src_ap, func=AF.Sign),
         waits=tuple(waits))
    ck_sgn = P.ck(sign_eng)
    inv3 = inv.rearrange("p (w o) -> p w o", o=1)
    from concourse.bass import broadcast_tensor_aps
    xn3, invb = broadcast_tensor_aps(src3, inv3)
    P.op("vector", (lambda: lambda e: e.tensor_tensor(
        out=a.rearrange("p (w c) -> p w c", c=16), in0=xn3, in1=invb,
        op=ALU.mult))(), waits=(ck_inv,))
    P.op("vector", lambda e: e.scalar_tensor_tensor(
        out=a, in0=a, scalar=-1.0, in1=a, op0=ALU.mult, op1=ALU.max),
        waits=(P.ck("vector"),))
    cka = P.ck("vector")
    P.op("vector", lambda e: e.tensor_scalar(
        out=q, in0=a, scalar1=0.25, scalar2=None, op0=ALU.is_gt), waits=(cka,))
    for m in (0.75, 1.25, 1.75):
        P.op("vector", (lambda m=m: lambda e: e.scalar_tensor_tensor(
            out=q, in0=a, scalar=m, in1=q, op0=ALU.is_gt, op1=ALU.add))(),
            waits=(P.ck("vector"),))
    u = dst_ap
    P.op("vector", lambda e: e.tensor_scalar(
        out=u, in0=a, scalar1=2.5, scalar2=None, op0=ALU.is_gt), waits=(cka,))
    for m in (3.5, 5.0, 5.0):
        P.op("vector", (lambda m=m: lambda e: e.scalar_tensor_tensor(
            out=u, in0=a, scalar=m, in1=u, op0=ALU.is_gt, op1=ALU.add))(),
            waits=(P.ck("vector"),))
    P.op("vector", lambda e: e.scalar_tensor_tensor(
        out=q, in0=q, scalar=0.5, in1=u, op0=ALU.mult, op1=ALU.add),
        waits=(P.ck("vector"),))
    q3 = q.rearrange("p (w c) -> p w c", c=16)
    mx3 = mx.rearrange("p (w o) -> p w o", o=1)
    q3b, mxb = broadcast_tensor_aps(q3, mx3)
    P.op("vector", (lambda: lambda e: e.tensor_tensor(
        out=q3, in0=q3b, in1=mxb, op=ALU.mult))(), waits=(P.ck("vector"),))
    P.op("vector", lambda e: e.tensor_tensor(
        out=dst_ap, in0=q, in1=sgn, op=ALU.mult),
        waits=(P.ck("vector"), ck_sgn))
    return P.ck("vector")


def load_bias_cols(P, tp, hbm_vec, width, name, scr_f32=None):
    ncol = width // 128
    row = P.sbuf(f"{name}_row", [1, width], F32)
    rowb = P.sbuf(f"{name}_rowb", [1, width], BF16)
    colsb = P.sbuf(f"{name}_colsb", [128, ncol], BF16)
    cols = P.sbuf(f"{name}_cols", [128, ncol], F32)
    ck = P.dma("sync", row[:], hbm_vec.rearrange("(o w) -> o w", o=1))
    P.op("vector", lambda e: e.tensor_copy(rowb[:], row[:]), waits=(ck,))
    ckb = P.ck("vector")
    last = None
    for c in range(ncol):
        last = tp.t(rowb[0:1, c * 128:(c + 1) * 128], colsb[:, c:c + 1], waits=(ckb,))
    P.op("vector", lambda e: e.tensor_copy(cols[:], colsb[:]), waits=(last,))
    return cols, P.ck("vector")


def bcast_row(P, consts, hbm_vec, width, name, dtype=BF16, ps=None,
              ps_free=None):
    row = P.sbuf(f"{name}_row", [1, width], F32)
    rowb = P.sbuf(f"{name}_rowb", [1, width], BF16)
    ones = P.sbuf(f"{name}_ones", [1, 128], BF16)
    out = P.sbuf(f"{name}_bc", [128, width], dtype)
    ck = P.dma("sync", row[:], hbm_vec.rearrange("(o w) -> o w", o=1))
    P.op("vector", lambda e: e.tensor_copy(rowb[:], row[:]), waits=(ck,))
    P.op("vector", lambda e: e.memset(ones[:], 1.0))
    ckb = P.ck("vector")
    extra0 = tuple(c for c in (ps_free or ()) if c is not None)
    cks = []
    nps = ceil_div(width, 512)
    if ps is None:
        ps = P.psum(f"{name}_ps", [128, min(width, 512)], F32)
    last = None
    for i in range(nps):
        w0 = i * 512
        w1 = min(width, w0 + 512)
        mmck = P.op("tensor", (lambda w0=w0, w1=w1: lambda e: e.matmul(
            out=ps[:, 0:w1 - w0], lhsT=ones[:], rhs=rowb[:, w0:w1],
            start=True, stop=True))(), waits=((ckb,) + extra0) if i == 0 else (P.ck("vector"),))
        P.op("vector", (lambda w0=w0, w1=w1: lambda e: e.tensor_copy(
            out[:, w0:w1], ps[:, 0:w1 - w0]))(), waits=(mmck,))
        last = P.ck("vector")
    return out, last


"""Generic projection emitters (feature-major outputs)."""
import numpy as np


def emit_proj_rowsO(P, tp, name, xhatT, xhatT_ck, w_hbm, bias_col, bias_ck,
                    out_cb, O=D, scale=1.0, tok=T, extra_w=()):
    NO = O // 128
    NS = tok // 512
    wrow = P.sbuf(f"{name}_wrow", [128, D], BF16)
    wTc = P.sbuf(f"{name}_wTc", [128, D], BF16)
    stage = [P.sbuf(f"{name}_stg{i}", [128, 512], BF16) for i in range(2)]
    ps = [P.psum(f"{name}_ps{i}", [128, 512], F32) for i in range(2)]
    st_ck = [None, None]
    ps_ck = [None, None]
    prev_w = None
    prev_T = None
    si = 0
    for oc in range(NO):
        w = ((prev_T,) if prev_T else ()) + tuple(extra_w)
        ldck = P.dma("gpsimd", wrow[:], w_hbm[oc * 128:(oc + 1) * 128, :],
                     waits=w)
        tcks = []
        for c in range(NC):
            eck = tp.t(wrow[:, c * 128:(c + 1) * 128],
                       wTc[:, c * 128:(c + 1) * 128], waits=(ldck,))
            tcks.append(eck)
        prev_T = tcks[-1]
        for ts in range(NS):
            k = si % 2
            si += 1
            mw = (tcks[-1],) + tuple(xhatT_ck) + \
                 ((ps_ck[k],) if ps_ck[k] else ())
            mck = None
            for c in range(NC):
                mck = P.op("tensor", (lambda c=c, ts=ts, k=k: lambda e: e.matmul(
                    out=ps[k][:], lhsT=wTc[:, c * 128:(c + 1) * 128],
                    rhs=xhatT[c][:, ts * 512:(ts + 1) * 512],
                    start=(c == 0), stop=(c == NC - 1)))(),
                    waits=mw if c == 0 else (), inc=(1 if c == NC - 1 else 0))
            ew = (mck,) + ((st_ck[k],) if st_ck[k] else ())
            if bias_col is not None:
                ew = ew + ((bias_ck,) if bias_ck else ())
                P.op("vector", (lambda oc=oc, k=k: lambda e: e.tensor_scalar(
                    out=stage[k][:], in0=ps[k][:], scalar1=float(scale),
                    scalar2=bias_col[:, oc:oc + 1], op0=ALU.mult,
                    op1=ALU.add))(), waits=ew)
            elif scale != 1.0:
                P.op("vector", (lambda k=k: lambda e: e.tensor_scalar(
                    out=stage[k][:], in0=ps[k][:], scalar1=float(scale),
                    scalar2=None, op0=ALU.mult))(), waits=ew)
            else:
                P.op("vector", (lambda k=k: lambda e: e.tensor_copy(
                    stage[k][:], ps[k][:]))(), waits=ew)
            eck = P.ck("vector")
            ps_ck[k] = eck
            st_ck[k] = out_cb(oc, ts, stage[k][:], eck)
    return


def emit_proj_rowsD(P, name, xhatT, xhatT_ck, w_tiles, w_ck, bias_col, bias_ck,
                    out_cb, O=D, scale=1.0, tok=T):
    NO = O // 128
    NS = tok // 512
    stage = [P.sbuf(f"{name}_stg{i}", [128, 512], BF16) for i in range(2)]
    ps = [P.psum(f"{name}_ps{i}", [128, 512], F32) for i in range(2)]
    st_ck = [None, None]
    ps_ck = [None, None]
    si = 0
    for oc in range(NO):
        for ts in range(NS):
            k = si % 2
            si += 1
            mw = tuple(w_ck) + tuple(xhatT_ck) + \
                 ((ps_ck[k],) if ps_ck[k] else ())
            mck = None
            for c in range(NC):
                mck = P.op("tensor", (lambda c=c, ts=ts, oc=oc, k=k:
                                      lambda e: e.matmul(
                    out=ps[k][:], lhsT=w_tiles[c][:, oc * 128:(oc + 1) * 128],
                    rhs=xhatT[c][:, ts * 512:(ts + 1) * 512],
                    start=(c == 0), stop=(c == NC - 1)))(),
                    waits=mw if c == 0 else (), inc=(1 if c == NC - 1 else 0))
            prev_st = st_ck[k]
            if prev_st is not None and not isinstance(prev_st, tuple):
                prev_st = (prev_st,)
            ew = (mck,) + (prev_st if prev_st else ())
            if bias_col is not None:
                ew = ew + ((bias_ck,) if bias_ck else ())
                P.op("vector", (lambda oc=oc, k=k: lambda e: e.tensor_scalar(
                    out=stage[k][:], in0=ps[k][:], scalar1=float(scale),
                    scalar2=bias_col[:, oc:oc + 1], op0=ALU.mult,
                    op1=ALU.add))(), waits=ew)
            else:
                P.op("vector", (lambda k=k: lambda e: e.tensor_scalar(
                    out=stage[k][:], in0=ps[k][:], scalar1=float(scale),
                    scalar2=None, op0=ALU.mult))(), waits=ew)
            eck = P.ck("vector")
            ps_ck[k] = eck
            st_ck[k] = out_cb(oc, ts, stage[k][:], eck)
    return P.ck("tensor")


def emit_ln_xhatT(P, tp, name, xres_in, tok=T, keep_tiles=False,
                  skip_transpose=False, dtype=BF16):
    nt = tok // 128
    x_s = [P.sbuf(f"{name}_x{i}", [128, D], F32) for i in range(2)]
    dump = P.sbuf(f"{name}_dump", [128, D], BF16)
    stat = P.sbuf(f"{name}_stat", [128, 4 * nt], F32)
    if keep_tiles:
        xh_tiles = [P.sbuf(f"{name}_xh{t}", [128, D], dtype)
                    for t in range(nt)]
    else:
        xh_tiles = [P.sbuf(f"{name}_xh", [128, D], dtype)] * nt
    xhatT = [P.sbuf(f"{name}_xT{c}", [128, tok], dtype) for c in range(NC)] \
        if not skip_transpose else None
    used = [None, None]
    for t in range(nt):
        k = t % 2
        w = ((used[k],) if used[k] else ())
        ck = P.dma("sync", x_s[k][:], xres_in[t * 128:(t + 1) * 128, :],
                   waits=w)
        used[k] = emit_ln_stats_tile(P, x_s[k][:], dump[:], stat, t,
                                     waits=(ck,), nt=nt)
    ln_ck = emit_ln_finalize(P, stat, nt=nt)
    prev_q = None
    tile_cks = []
    for t in range(nt):
        k = t % 2
        w = ((used[k],) if used[k] else ())
        ck = P.dma("sync", x_s[k][:], xres_in[t * 128:(t + 1) * 128, :],
                   waits=w)
        aw = (ck, ln_ck)
        if not keep_tiles and prev_q is not None:
            aw = aw + (prev_q,)
        ack = ln_affine(P, x_s[k][:], xh_tiles[t][:], stat, t, waits=aw,
                        nt=nt)
        used[k] = ack
        tile_cks.append(ack)
        if not skip_transpose:
            tcks = []
            for c in range(NC):
                eck = tp.t(xh_tiles[t][:, c * 128:(c + 1) * 128],
                           xhatT[c][:, t * 128:(t + 1) * 128], waits=(ack,))
                tcks.append(eck)
            prev_q = tcks[-1]
    ready = (P.ck("scalar"),) if not skip_transpose else (P.ck("vector"),)
    if keep_tiles:
        return xhatT, ready, xh_tiles, tile_cks
    return xhatT, ready


"""L0_attn_a: Nystromformer attention (64 landmarks, Moore-Penrose pinv)."""
import numpy as np

NH = 16
HS = 64
M = 64
ML = 32
ITERS = 6
HT = NH * M


def ident_tiled_host():
    import ml_dtypes
    i64 = np.eye(64, dtype=np.float32)
    return np.tile(i64, (1, 16)).astype(ml_dtypes.bfloat16)


def make_sb_cb(P, q_sb, out_cb):
    def cb(oc, ts, src_ap, ck):
        P.op("vector", (lambda oc=oc, ts=ts, src_ap=src_ap:
                        lambda e: e.tensor_copy(
            q_sb[oc][:, ts * 512:(ts + 1) * 512], src_ap))(), waits=(ck,))
        cck = P.ck("vector")
        dck = out_cb(oc, ts, src_ap, ck)
        return (cck, dck)
    return cb


def emit_l0attnA_1(P, consts, tp, wk_hbm, wq_hbm, wv_hbm, xres_in,
                   q_out, k_out, v_out, ql_out, kl_out):
    xhatT, xt_ck = emit_ln_xhatT(P, tp, "na", xres_in)
    w_tiles = [P.sbuf(f"na_w{c}", [128, D], BF16) for c in range(NC)]
    q_sb = [P.sbuf(f"na_q{c}", [128, T], BF16) for c in range(NC)]
    lm = P.sbuf("na_lm", [128, ML], F32)
    projs = [("q", wq_hbm, q_out, 0.125, ql_out),
             ("k", wk_hbm, k_out, 1.0, kl_out),
             ("v", wv_hbm, v_out, 1.0, None)]
    prev_w = [None] * NC
    prev_proj = None
    all_stores = []
    for (nm, w_hbm, out_hbm, scale, l_out) in projs:
        w_cks = []
        for c in range(NC):
            w = ((prev_w[c],) if prev_w[c] else ()) + \
                ((prev_proj,) if prev_proj else ())
            prev_w[c] = P.dma("gpsimd", w_tiles[c][:],
                              w_hbm[c * 128:(c + 1) * 128, :], waits=w)
            w_cks.append(prev_w[c])

        def out_cb(oc, ts, src_ap, ck, out_hbm=out_hbm):
            dck = P.dma("sync", out_hbm[oc * 128:(oc + 1) * 128,
                                        ts * 512:(ts + 1) * 512],
                        src_ap, waits=(ck,))
            all_stores.append(dck)
            return dck

        cb = make_sb_cb(P, q_sb, out_cb) if l_out is not None else out_cb
        prev_proj = emit_proj_rowsD(P, f"na_{nm}", xhatT, xt_ck, w_tiles,
                                    tuple(w_cks), None, None, cb, scale=scale)
        if l_out is not None:
            lw = (P.ck("vector"),)
            prev_lm = None
            for c in range(NC):
                P.op("vector", (lambda c=c: lambda e: e.tensor_reduce(
                    out=lm[:, 0:ML],
                    in_=q_sb[c][:].rearrange("p (w c2) -> p w c2", c2=64),
                    axis=AX.X, op=ALU.add))(),
                    waits=lw + ((prev_lm,) if prev_lm else ()))
                P.op("vector", lambda e: e.tensor_scalar(
                    out=lm[:, 0:ML], in0=lm[:, 0:ML], scalar1=1.0 / 64,
                    scalar2=None, op0=ALU.mult))
                prev_lm = P.dma("sync", l_out[c * 128:(c + 1) * 128, :],
                                lm[:, 0:ML], waits=(P.ck("vector"),))
                all_stores.append(prev_lm)
    P.wait("sync", tuple(all_stores))


def emit_l0attnA_2(P, consts, tp, q_hbm, k_hbm, v_hbm, qlg_hbm, klg_hbm,
                   identt_hbm, s1T_out, zsT_out, mzs_out, stages=5):
    from concourse.bass import broadcast_tensor_aps
    qlg = P.sbuf("n2_qlg", [HS, HT], BF16)
    klg = P.sbuf("n2_klg", [HS, HT], BF16)
    qlf = P.sbuf("n2_qlf", [HS, M], F32)
    for h in range(NH):
        c1 = P.dma("sync", qlf[:], qlg_hbm[h * HS:(h + 1) * HS, :],
                   waits=(P.ck("vector"),))
        P.op("vector", (lambda h=h: lambda e: e.tensor_copy(
            qlg[:, h * M:(h + 1) * M], qlf[:]))(), waits=(c1,))
        c2 = P.dma("sync", qlf[:], klg_hbm[h * HS:(h + 1) * HS, :],
                   waits=(P.ck("vector"),))
        P.op("vector", (lambda h=h: lambda e: e.tensor_copy(
            klg[:, h * M:(h + 1) * M], qlf[:]))(), waits=(c2,))
    qlg_ck = P.ck("vector")
    identt = P.sbuf("n2_id", [M, HT], BF16)
    id_ck = P.dma("sync", identt[:], identt_hbm)

    ps_b = P.psum("n2_psb", [M, 512], F32)
    ps_p = P.psum("n2_psp", [M, 512], F32)
    s1ps = P.psum("n2_s1ps", [128, HT], F32)
    ps_s3 = [P.psum(f"n2_ps3{i}", [M, 512], F32) for i in range(2)]

    s2n = P.sbuf("n2_s2n", [M, HT], BF16)
    s2nT = P.sbuf("n2_s2nT", [M, HT], BF16)
    sm = P.sbuf("n2_sm", [M, 8], F32)
    prev_blk = None
    for blk in range(2):
        mck = None
        for hh in range(8):
            h = blk * 8 + hh
            w = ()
            if hh == 0:
                w = (qlg_ck,) + ((prev_blk,) if prev_blk else ())
            mck = P.op("tensor", (lambda h=h, hh=hh: lambda e: e.matmul(
                out=ps_b[:, hh * M:(hh + 1) * M],
                lhsT=qlg[:, h * M:(h + 1) * M],
                rhs=klg[:, h * M:(h + 1) * M],
                start=True, stop=True))(), waits=w)
        base = blk * 512
        ps3 = ps_b[:].rearrange("p (w c2) -> p w c2", c2=M)
        mx3 = sm[:, 0:8].rearrange("p (w o) -> p w o", o=1)
        P.op("vector", (lambda ps3=ps3: lambda e: e.tensor_reduce(
            out=sm[:, 0:8], in_=ps3, axis=AX.X, op=ALU.max))(), waits=(mck,))
        a3, b3 = broadcast_tensor_aps(ps3, mx3)
        P.op("vector", (lambda a3=a3, b3=b3: lambda e: e.tensor_tensor(
            out=ps_b[:], in0=a3, in1=b3, op=ALU.subtract))(),
            waits=(P.ck("vector"),))
        P.op("scalar", lambda e: e.activation(
            out=ps_b[:], in_=ps_b[:], func=AF.Exp), waits=(P.ck("vector"),))
        P.op("vector", (lambda ps3=ps3: lambda e: e.tensor_reduce(
            out=sm[:, 0:8], in_=ps3, axis=AX.X, op=ALU.add))(),
            waits=(P.ck("scalar"),))
        P.op("vector", lambda e: e.reciprocal(out=sm[:, 0:8],
                                              in_=sm[:, 0:8]),
             waits=(P.ck("vector"),))
        a3b, b3b = broadcast_tensor_aps(ps3, mx3)
        P.op("vector", (lambda base=base, a3b=a3b, b3b=b3b:
                        lambda e: e.tensor_tensor(
            out=s2n[:, base:base + 512], in0=a3b, in1=b3b, op=ALU.mult))(),
            waits=(P.ck("vector"),))
        s2_ck = P.ck("vector")
        for hh in range(8):
            o = base + hh * M
            tp.t(s2n[:, o:o + M], s2nT[:, o:o + M], waits=(s2_ck,))
        prev_blk = P.ck("scalar")
    s2nT_ck = P.ck("scalar")
    s2n_ck = P.ck("vector")

    z = P.sbuf("n2_z", [M, HT], BF16)
    zT = P.sbuf("n2_zT", [M, HT], BF16)
    t1 = P.sbuf("n2_t1", [M, HT], BF16)
    u = P.sbuf("n2_u", [M, HT], BF16)
    xz = P.sbuf("n2_xz", [M, HT], BF16)
    xzT = P.sbuf("n2_xzT", [M, HT], BF16)
    ones64 = P.sbuf("n2_ones", [M, 1], BF16)
    ones_row = P.sbuf("n2_onesr", [1, M], BF16)
    P.op("vector", lambda e: e.memset(ones64[:], 1.0))
    P.op("vector", lambda e: e.memset(ones_row[:], 1.0))
    ones_ck = P.ck("vector")
    rs = P.sbuf("n2_rs", [M, NH], F32)
    rsb = P.sbuf("n2_rsb", [M, NH], BF16)
    rsT = P.sbuf("n2_rsT", [1, M], BF16)
    cs = P.sbuf("n2_cs", [1, HT], F32)
    csm = P.sbuf("n2_csm", [1, 2 * NH], F32)
    csm_b = P.sbuf("n2_csmb", [1, NH], BF16)
    rec64 = P.sbuf("n2_rec", [M, NH], F32)
    P.op("vector", lambda e: e.tensor_reduce(
        out=rs[:], in_=s2n[:].rearrange("p (w c2) -> p w c2", c2=M),
        axis=AX.X, op=ALU.add), waits=(s2n_ck,))
    P.op("vector", lambda e: e.tensor_copy(rsb[:], rs[:]),
         waits=(P.ck("vector"),))
    rsb_ck = P.ck("vector")
    for half in range(2):
        csck = P.op("tensor", (lambda half=half: lambda e: e.matmul(
            out=ps_b[0:1, 0:512], lhsT=ones64[:],
            rhs=s2n[:, half * 512:(half + 1) * 512],
            start=True, stop=True))(),
            waits=(ones_ck, s2n_ck, P.ck("vector")))
        P.op("vector", (lambda half=half: lambda e: e.tensor_copy(
            cs[0:1, half * 512:(half + 1) * 512], ps_b[0:1, 0:512]))(),
            waits=(csck,))
    P.op("vector", lambda e: e.tensor_reduce(
        out=csm[0:1, 0:NH],
        in_=cs[0:1, :].rearrange("p (w c2) -> p w c2", c2=M),
        axis=AX.X, op=ALU.max), waits=(P.ck("vector"),))
    for h in range(NH):
        tp.t(rsb[:, h:h + 1], rsT[0:1, :], waits=(rsb_ck, P.ck("vector")))
        P.op("vector", (lambda h=h: lambda e: e.tensor_reduce(
            out=csm[0:1, NH + h:NH + h + 1], in_=rsT[0:1, :], axis=AX.X,
            op=ALU.max))(), waits=(P.ck("scalar"),))
    P.op("vector", lambda e: e.tensor_tensor(
        out=csm[0:1, 0:NH], in0=csm[0:1, 0:NH], in1=csm[0:1, NH:2 * NH],
        op=ALU.mult), waits=(P.ck("vector"),))
    P.op("vector", lambda e: e.reciprocal(
        out=csm[0:1, 0:NH], in_=csm[0:1, 0:NH]), waits=(P.ck("vector"),))
    P.op("vector", lambda e: e.tensor_copy(csm_b[:], csm[0:1, 0:NH]),
         waits=(P.ck("vector"),))
    cb_ck = P.ck("vector")
    for h in range(NH):
        bck = P.op("tensor", (lambda h=h: lambda e: e.matmul(
            out=ps_b[0:M, 0:1], lhsT=ones_row[:], rhs=csm_b[:, h:h + 1],
            start=True, stop=True))(),
            waits=(cb_ck,) if h == 0 else (P.ck("vector"),))
        P.op("vector", (lambda h=h: lambda e: e.tensor_copy(
            rec64[:, h:h + 1], ps_b[0:M, 0:1]))(), waits=(bck,))
    rec_ck = P.ck("vector")
    for (src, dst, wck) in ((s2nT, z, s2nT_ck), (s2n, zT, s2n_ck)):
        for h in range(NH):
            P.op("vector", (lambda src=src, dst=dst, h=h:
                            lambda e: e.tensor_scalar(
                out=dst[:, h * M:(h + 1) * M], in0=src[:, h * M:(h + 1) * M],
                scalar1=rec64[:, h:h + 1], scalar2=None, op0=ALU.mult))(),
                waits=(rec_ck, wck))
    z_ck = P.ck("vector")

    def heads_mm_evac(lhs_sb, rhs_sb, out_sb, waits, scale=None):
        last = None
        for blk in range(2):
            mck = None
            for hh in range(8):
                h = blk * 8 + hh
                w = ()
                if blk == 0 and hh == 0:
                    w = tuple(waits)
                if blk == 1 and hh == 0:
                    w = (P.ck("vector"),)
                mck = P.op("tensor", (lambda h=h, hh=hh, lhs_sb=lhs_sb,
                                      rhs_sb=rhs_sb: lambda e: e.matmul(
                    out=ps_p[:, hh * M:(hh + 1) * M],
                    lhsT=lhs_sb[:, h * M:(h + 1) * M],
                    rhs=rhs_sb[:, h * M:(h + 1) * M],
                    start=True, stop=True))(), waits=w)
            if scale is None:
                P.op("vector", (lambda blk=blk, out_sb=out_sb:
                                lambda e: e.tensor_copy(
                    out_sb[:, blk * 512:(blk + 1) * 512], ps_p[:]))(),
                    waits=(mck,))
            else:
                P.op("vector", (lambda blk=blk, out_sb=out_sb, scale=scale:
                                lambda e: e.tensor_scalar(
                    out=out_sb[:, blk * 512:(blk + 1) * 512], in0=ps_p[:],
                    scalar1=float(scale), scalar2=None, op0=ALU.mult))(),
                    waits=(mck,))
            last = P.ck("vector")
        return last

    def isub(dst, const, srcT):
        P.op("vector", (lambda dst=dst, const=const, srcT=srcT:
                        lambda e: e.scalar_tensor_tensor(
            out=dst[:], in0=identt[:], scalar=float(const), in1=srcT[:],
            op0=ALU.mult, op1=ALU.subtract))(),
            waits=(P.ck("vector"), id_ck))
        return P.ck("vector")

    for it in range(ITERS):
        w0 = (z_ck,) if it == 0 else (P.ck("vector"),)
        heads_mm_evac(s2nT, z, xz, w0)
        heads_mm_evac(z, s2nT, xzT, (P.ck("vector"),))
        isub(t1, 7.0, xz)
        heads_mm_evac(xzT, t1, u, (P.ck("vector"),))
        isub(t1, 15.0, u)
        heads_mm_evac(xzT, t1, u, (P.ck("vector"),))
        isub(t1, 13.0, u)
        heads_mm_evac(zT, t1, u, (P.ck("vector"),))
        heads_mm_evac(t1, zT, zT, (P.ck("vector"),), scale=0.25)
        P.op("vector", lambda e: e.tensor_scalar(
            out=z[:], in0=u[:], scalar1=0.25, scalar2=None, op0=ALU.mult),
            waits=(P.ck("vector"),))
    zs_f = P.sbuf("n2_zsf", [M, HT], F32)
    P.op("vector", lambda e: e.tensor_copy(zs_f[:], z[:]),
         waits=(P.ck("vector"),))
    zst = P.dma("sync", zsT_out, zs_f[:], waits=(P.ck("vector"),))

    s1sc = P.sbuf("n2_s1sc", [128, HT], F32)
    s1n = P.sbuf("n2_s1n", [128, HT], BF16)
    wm = P.sbuf("n2_wm", [128, NH], F32)
    qsl = [P.sbuf(f"n2_qsl{i}", [HS, 128], BF16) for i in range(4)]
    qsl_ck = [None] * 4
    qsl_used = [None] * 4
    stg = [P.sbuf(f"n2_stg{i}", [128, 128], BF16) for i in range(2)]
    stg_ck = [None, None]
    s1_store = []
    prev_s1ps = None
    qi = 0
    for t in range(NT):
        mcks = []
        for h in range(NH):
            qk = qi % 4
            qi += 1
            lw = ((qsl_used[qk],) if qsl_used[qk] else ())
            qsl_ck[qk] = P.dma("sync", qsl[qk][:],
                               q_hbm[h * HS:(h + 1) * HS,
                                     t * 128:(t + 1) * 128], waits=lw)
            w = (qsl_ck[qk], qlg_ck)
            if h == 0:
                w = w + ((prev_s1ps,) if prev_s1ps else ())
            mck = P.op("tensor", (lambda h=h, qk=qk: lambda e: e.matmul(
                out=s1ps[:, h * M:(h + 1) * M],
                lhsT=qsl[qk][:], rhs=klg[:, h * M:(h + 1) * M],
                start=True, stop=True))(), waits=w)
            qsl_used[qk] = mck
            mcks.append(mck)
        ps3 = s1ps[:].rearrange("p (w c2) -> p w c2", c2=M)
        sc3 = s1sc[:].rearrange("p (w c2) -> p w c2", c2=M)
        mx3 = wm[:].rearrange("p (w o) -> p w o", o=1)
        P.op("vector", (lambda ps3=ps3: lambda e: e.tensor_reduce(
            out=wm[:], in_=ps3, axis=AX.X, op=ALU.max))(), waits=(mcks[-1],))
        a3, b3 = broadcast_tensor_aps(ps3, mx3)
        P.op("vector", (lambda a3=a3, b3=b3, sc3=sc3:
                        lambda e: e.tensor_tensor(
            out=sc3, in0=a3, in1=b3, op=ALU.subtract))(),
            waits=(P.ck("vector"),))
        prev_s1ps = P.ck("vector")
        P.op("scalar", lambda e: e.activation(
            out=s1sc[:], in_=s1sc[:], func=AF.Exp), waits=(prev_s1ps,))
        P.op("vector", (lambda sc3=sc3: lambda e: e.tensor_reduce(
            out=wm[:], in_=sc3, axis=AX.X, op=ALU.add))(),
            waits=(P.ck("scalar"),))
        P.op("vector", lambda e: e.reciprocal(out=wm[:], in_=wm[:]),
             waits=(P.ck("vector"),))
        a3b, b3b = broadcast_tensor_aps(sc3, mx3)
        P.op("vector", (lambda a3b=a3b, b3b=b3b: lambda e: e.tensor_tensor(
            out=s1n[:], in0=a3b, in1=b3b, op=ALU.mult))(),
            waits=(P.ck("vector"),))
        s1n_ck = P.ck("vector")
        for c in range(8):
            k = (t * 8 + c) % 2
            ew = (s1n_ck,) + ((stg_ck[k],) if stg_ck[k] else ())
            eck = tp.t(s1n[:, c * 128:(c + 1) * 128], stg[k][:], waits=ew)
            stg_ck[k] = P.dma("sync",
                              s1T_out[c * 128:(c + 1) * 128,
                                      t * 128:(t + 1) * 128],
                              stg[k][:], waits=(eck,))
            s1_store.append(stg_ck[k])

    khs = [P.sbuf(f"n2_khs{i}", [HS, T], BF16) for i in range(2)]
    vhs = [P.sbuf(f"n2_vhs{i}", [HS, T], BF16) for i in range(2)]
    kh_ck = [None, None]
    vh_ck = [None, None]
    kh_used = [None, None]
    vh_used = [None, None]
    e3 = P.sbuf("n2_e3", [M, T], BF16)
    e3T = P.sbuf("n2_e3T", [128, NT * M], BF16)
    vT = P.sbuf("n2_vT", [128, NT * M], BF16)
    sm3 = P.sbuf("n2_sm3", [M, 8], F32)
    mzs = P.sbuf("n2_mzs", [M, NH * 66], F32)
    prev_e3 = None
    prev_z = None
    for h in range(NH):
        hk = h % 2
        kh_ck[hk] = P.dma("gpsimd", khs[hk][:],
                          k_hbm[h * HS:(h + 1) * HS, :],
                          waits=((kh_used[hk],) if kh_used[hk] else ()))
        vh_ck[hk] = P.dma("gpsimd", vhs[hk][:],
                          v_hbm[h * HS:(h + 1) * HS, :],
                          waits=((vh_used[hk],) if vh_used[hk] else ()))
        mk3 = [None, None]
        for sl in range(4):
            k2 = sl % 2
            w = (kh_ck[hk], qlg_ck)
            if sl >= 2:
                w = w + (Ck("@scalar", P.cnt["@scalar"]),)
            if sl == 0 and prev_z:
                w = w + (prev_z,)
            mk3[k2] = P.op("tensor", (lambda h=h, sl=sl, k2=k2, hk=hk:
                                      lambda e: e.matmul(
                out=ps_s3[k2][:], lhsT=qlg[:, h * M:(h + 1) * M],
                rhs=khs[hk][:, sl * 512:(sl + 1) * 512],
                start=True, stop=True))(), waits=w)
            P.op("vector", (lambda sl=sl, k2=k2: lambda e: e.tensor_reduce(
                out=sm3[:, sl:sl + 1], in_=ps_s3[k2][:], axis=AX.X,
                op=ALU.max))(), waits=(mk3[k2],))
            P.op("scalar", (lambda sl=sl, k2=k2, h=h: lambda e: e.activation(
                out=e3[:, sl * 512:(sl + 1) * 512], in_=ps_s3[k2][:],
                func=AF.Copy))(), waits=(P.ck("vector"),))
        kh_used[hk] = P.ck("scalar")
        P.op("vector", lambda e: e.tensor_reduce(
            out=sm3[:, 4:5], in_=sm3[:, 0:4], axis=AX.X, op=ALU.max),
            waits=(P.ck("vector"), P.ck("scalar")))
        P.op("vector", lambda e: e.tensor_scalar(
            out=sm3[:, 5:6], in0=sm3[:, 4:5], scalar1=-1.0, scalar2=None,
            op0=ALU.mult), waits=(P.ck("vector"),))
        negm_ck = P.ck("vector")
        for sl in range(4):
            P.op("scalar", (lambda sl=sl: lambda e: e.activation(
                out=e3[:, sl * 512:(sl + 1) * 512],
                in_=e3[:, sl * 512:(sl + 1) * 512],
                func=AF.Exp, bias=sm3[:, 5:6], scale=1.0,
                accum_out=sm3[:, 6:7]))(),
                waits=(negm_ck,) if sl == 0 else (P.ck("vector"),))
            if sl == 0:
                P.op("vector", lambda e: e.tensor_copy(
                    sm3[:, 7:8], sm3[:, 6:7]), waits=(P.ck("scalar"),))
            else:
                P.op("vector", lambda e: e.tensor_tensor(
                    out=sm3[:, 7:8], in0=sm3[:, 7:8], in1=sm3[:, 6:7],
                    op=ALU.add), waits=(P.ck("scalar"),))
        e3_ck = (P.ck("scalar"), P.ck("vector"))
        tcks = []
        for c in range(NT):
            ew = e3_ck + ((prev_e3,) if prev_e3 and c == 0 else ())
            tp.t(e3[:, c * 128:(c + 1) * 128],
                 e3T[:, c * M:(c + 1) * M], waits=ew)
            eck2 = tp.t(vhs[hk][:, c * 128:(c + 1) * 128],
                        vT[:, c * M:(c + 1) * M], waits=(vh_ck[hk],))
            tcks.append(eck2)
        tr_ck = tcks[-1]
        zmck = None
        for c in range(NT):
            zmck = P.op("tensor", (lambda c=c: lambda e: e.matmul(
                out=ps_b[0:M, 0:M], lhsT=e3T[:, c * M:(c + 1) * M],
                rhs=vT[:, c * M:(c + 1) * M],
                start=(c == 0), stop=(c == NT - 1)))(),
                waits=(tr_ck,) if c == 0 else (), inc=1)
        prev_e3 = zmck
        vh_used[hk] = zmck
        P.op("vector", (lambda h=h: lambda e: e.tensor_copy(
            mzs[:, h * 66:h * 66 + 64], ps_b[0:M, 0:M]))(), waits=(zmck,))
        P.op("vector", (lambda h=h: lambda e: e.tensor_copy(
            mzs[:, h * 66 + 64:h * 66 + 65], sm3[:, 4:5]))(),
            waits=(P.ck("vector"),))
        P.op("vector", (lambda h=h: lambda e: e.tensor_copy(
            mzs[:, h * 66 + 65:h * 66 + 66], sm3[:, 7:8]))(),
            waits=(P.ck("vector"),))
        prev_z = P.ck("vector")
    mzst = P.dma("sync", mzs_out, mzs[:],
                 waits=(prev_z,) if prev_z else (P.ck("vector"),))
    P.wait("sync", (mzst, zst) + tuple(s1_store[-2:]))


def emit_l0attnA_3(P, consts, tp, s1T_hbm, zsT_hbm, mzs_own_hbm,
                   mzs_pair_hbm, wo_hbm, bo_hbm, xres_in, xres_out):
    mo = P.sbuf("n3_mo", [M, NH * 66], F32)
    mp = P.sbuf("n3_mp", [M, NH * 66], F32)
    c1 = P.dma("sync", mo[:], mzs_own_hbm)
    c2 = P.dma("sync", mp[:], mzs_pair_hbm)
    zs = P.sbuf("n3_zs", [M, HT], F32)
    zsb = P.sbuf("n3_zsb", [M, HT], BF16)
    c3 = P.dma("sync", zs[:], zsT_hbm)
    P.op("vector", lambda e: e.tensor_copy(zsb[:], zs[:]), waits=(c3,))
    zs_ck = P.ck("vector")
    sm = P.sbuf("n3_sm", [M, 8], F32)
    z3 = P.sbuf("n3_z3", [M, HT], BF16)
    for h in range(NH):
        o = h * 66
        P.op("vector", (lambda o=o: lambda e: e.tensor_tensor(
            out=sm[:, 0:1], in0=mo[:, o + 64:o + 65],
            in1=mp[:, o + 64:o + 65], op=ALU.max))(), waits=(c1, c2))
        P.op("vector", lambda e: e.tensor_scalar(
            out=sm[:, 1:2], in0=sm[:, 0:1], scalar1=-1.0, scalar2=None,
            op0=ALU.mult), waits=(P.ck("vector"),))
        P.op("scalar", (lambda o=o: lambda e: e.activation(
            out=sm[:, 2:3], in_=mo[:, o + 64:o + 65], func=AF.Exp,
            bias=sm[:, 1:2], scale=1.0))(), waits=(P.ck("vector"),))
        P.op("scalar", (lambda o=o: lambda e: e.activation(
            out=sm[:, 3:4], in_=mp[:, o + 64:o + 65], func=AF.Exp,
            bias=sm[:, 1:2], scale=1.0))(), waits=(P.ck("scalar"),))
        P.op("vector", (lambda o=o: lambda e: e.tensor_tensor(
            out=sm[:, 4:5], in0=mo[:, o + 65:o + 66], in1=sm[:, 2:3],
            op=ALU.mult))(), waits=(P.ck("scalar"),))
        P.op("vector", (lambda o=o: lambda e: e.tensor_tensor(
            out=sm[:, 5:6], in0=mp[:, o + 65:o + 66], in1=sm[:, 3:4],
            op=ALU.mult))(), waits=(P.ck("vector"),))
        P.op("vector", lambda e: e.tensor_tensor(
            out=sm[:, 4:5], in0=sm[:, 4:5], in1=sm[:, 5:6], op=ALU.add),
            waits=(P.ck("vector"),))
        P.op("vector", lambda e: e.reciprocal(out=sm[:, 4:5],
                                              in_=sm[:, 4:5]),
             waits=(P.ck("vector"),))
        P.op("vector", (lambda o=o: lambda e: e.tensor_scalar(
            out=mo[:, o:o + 64], in0=mo[:, o:o + 64], scalar1=sm[:, 2:3],
            scalar2=None, op0=ALU.mult))(), waits=(P.ck("vector"),))
        P.op("vector", (lambda o=o: lambda e: e.scalar_tensor_tensor(
            out=mo[:, o:o + 64], in0=mp[:, o:o + 64], scalar=sm[:, 3:4],
            in1=mo[:, o:o + 64], op0=ALU.mult, op1=ALU.add))(),
            waits=(P.ck("vector"),))
        P.op("vector", (lambda o=o, h=h: lambda e: e.tensor_scalar(
            out=z3[:, h * M:(h + 1) * M], in0=mo[:, o:o + 64],
            scalar1=sm[:, 4:5], scalar2=None, op0=ALU.mult))(),
            waits=(P.ck("vector"),))
    z3_ck = P.ck("vector")
    ps_w = P.psum("n3_psw", [M, HT], F32)
    Wt = P.sbuf("n3_W", [M, HT], BF16)
    wmck = None
    for h in range(NH):
        wmck = P.op("tensor", (lambda h=h: lambda e: e.matmul(
            out=ps_w[:, h * M:(h + 1) * M], lhsT=zsb[:, h * M:(h + 1) * M],
            rhs=z3[:, h * M:(h + 1) * M], start=True, stop=True))(),
            waits=(zs_ck, z3_ck) if h == 0 else ())
    P.op("vector", lambda e: e.tensor_copy(Wt[:], ps_w[:]), waits=(wmck,))
    W_ck = P.ck("vector")
    s1h = [P.sbuf(f"n3_s1{i}", [M, T], BF16) for i in range(2)]
    s1_ck = [None, None]
    attn = [P.sbuf(f"n3_at{c}", [128, T], BF16) for c in range(NC)]
    ps_o = [P.psum(f"n3_pso{i}", [M, 512], F32) for i in range(2)]
    pso_used = [None, None]
    oi = 0
    for h in range(NH):
        hk = h % 2
        w = ((s1_ck[hk],) if s1_ck[hk] else ())
        lck = P.dma("sync", s1h[hk][:], s1T_hbm[h * M:(h + 1) * M, :],
                    waits=w)
        last = None
        for sl in range(4):
            ko = oi % 2
            oi += 1
            mck = P.op("tensor", (lambda h=h, sl=sl, hk=hk, ko=ko:
                                  lambda e: e.matmul(
                out=ps_o[ko][:], lhsT=Wt[:, h * M:(h + 1) * M],
                rhs=s1h[hk][:, sl * 512:(sl + 1) * 512],
                start=True, stop=True))(),
                waits=(lck, W_ck) + ((pso_used[ko],) if pso_used[ko]
                                     else ()))
            P.op("scalar", (lambda h=h, sl=sl, ko=ko: lambda e: e.activation(
                out=attn[h // 2][(h % 2) * 64:(h % 2) * 64 + 64,
                                 sl * 512:(sl + 1) * 512],
                in_=ps_o[ko][:], func=AF.Copy))(), waits=(mck,))
            pso_used[ko] = P.ck("scalar")
            last = pso_used[ko]
        s1_ck[hk] = last
    attn_ck = (P.ck("scalar"),)
    wo_t = [P.sbuf(f"n3_wo{c}", [128, D], BF16) for c in range(NC)]
    wo_cks = [P.dma("gpsimd", wo_t[c][:], wo_hbm[c * 128:(c + 1) * 128, :])
              for c in range(NC)]
    bo_bc, bo_ck = bcast_row(P, consts, bo_hbm, D, "n3_bo", dtype=F32)
    x_s = [P.sbuf(f"n3_x{i}", [128, D], F32) for i in range(2)]
    ps_f = [P.psum(f"n3_psf{i}", [128, 512], F32) for i in range(2)]
    x_used = [None, None]
    psf_used = [None, None]
    for t in range(NT):
        k = t % 2
        xck = P.dma("sync", x_s[k][:], xres_in[t * 128:(t + 1) * 128, :],
                    waits=((x_used[k],) if x_used[k] else ()))
        for half in range(2):
            mw = tuple(wo_cks) + attn_ck + \
                 ((psf_used[half],) if psf_used[half] else ())
            mck = None
            for c in range(NC):
                mck = P.op("tensor", (lambda c=c, t=t, half=half:
                                      lambda e: e.matmul(
                    out=ps_f[half][:],
                    lhsT=attn[c][:, t * 128:(t + 1) * 128],
                    rhs=wo_t[c][:, half * 512:(half + 1) * 512],
                    start=(c == 0), stop=(c == NC - 1)))(),
                    waits=mw if c == 0 else (), inc=(1 if c == NC - 1 else 0))
            P.op("vector", (lambda k=k, half=half: lambda e: e.tensor_tensor(
                out=x_s[k][:, half * 512:(half + 1) * 512],
                in0=x_s[k][:, half * 512:(half + 1) * 512],
                in1=ps_f[half][:], op=ALU.add))(), waits=(mck, xck))
            P.op("vector", (lambda k=k, half=half: lambda e: e.tensor_tensor(
                out=x_s[k][:, half * 512:(half + 1) * 512],
                in0=x_s[k][:, half * 512:(half + 1) * 512],
                in1=bo_bc[:, half * 512:(half + 1) * 512], op=ALU.add))(),
                waits=(bo_ck,))
            psf_used[half] = P.ck("vector")
        x_used[k] = P.dma("sync", xres_out[t * 128:(t + 1) * 128, :],
                          x_s[k][:], waits=(P.ck("vector"),))
    P.wait("sync", (x_used[0], x_used[1]))


"""L0_ffn: product-key memory FFN (top-1 expert per head via argmax+argmax)."""
import numpy as np
import concourse.bass as bass

HEADS = 8
NKEYS = 256
DKEY = 512
NEXP = 65536


def emit_l0ffn(P, consts, tp, wq_hbm, keys_hbm, wdown_hbm, wup_hbm,
               xres_in, xres_out, dbg=None):
    xhatT, xt_ck, xh_tiles, xh_cks = emit_ln_xhatT(
        P, tp, "pk", xres_in, keep_tiles=True, dtype=F16)

    qph = [P.sbuf(f"pk_qph{c}", [128, T], F16) for c in range(4)]
    wqs = [P.sbuf(f"pk_wqs{i}", [128, 512], F16) for i in range(NC)]
    keysb = P.sbuf("pk_keysb", [128, 2 * DKEY], F16)
    keysT = P.sbuf("pk_keysT", [128, 4 * NKEYS], F16)
    ps_q = [P.psum(f"pk_psq{i}", [128, 512], F32) for i in range(2)]
    ps_s = [P.psum(f"pk_pss{i}", [128, NKEYS], F32) for i in range(2)]
    sx = [P.sbuf(f"pk_sx{h}", [128, NT], F32) for h in range(HEADS)]
    sy = [P.sbuf(f"pk_sy{h}", [128, NT], F32) for h in range(HEADS)]
    ix = [P.sbuf(f"pk_ix{h}", [128, NT], F32) for h in range(HEADS)]
    iy = [P.sbuf(f"pk_iy{h}", [128, NT], F32) for h in range(HEADS)]
    m8 = P.sbuf("pk_m8", [128, 8], F32)
    i8 = P.sbuf("pk_i8", [128, 8], U32)
    i8f = P.sbuf("pk_i8f", [128, 8], F32)
    wq_ck = [None] * NC
    psq_used = [None, None]
    pss_used = [None, None]
    qph_used = None
    keys_used = None
    m8_used = None
    prev_qready = None
    for p in range(2):
        for h in range(HEADS):
            o0 = p * HEADS * DKEY + h * DKEY
            for c in range(NC):
                w = ((wq_ck[c],) if wq_ck[c] else ()) +                     ((prev_qready,) if prev_qready else ())
                wq_ck[c] = P.dma("gpsimd", wqs[c][:],
                                 wq_hbm[c * 128:(c + 1) * 128, o0:o0 + DKEY],
                                 waits=w)
            kw = ((keys_used,) if keys_used else ())
            kck = P.dma("gpsimd", keysb[:].rearrange("p (r f) -> p r f", r=2),
                        keys_hbm[h, :, p, :].rearrange(
                            "(k pp) f -> pp k f", pp=128), waits=kw)
            tk = []
            for kc in range(2):
                for dc in range(4):
                    eck = tp.t(keysb[:, kc * 512 + dc * 128:
                                     kc * 512 + (dc + 1) * 128],
                               keysT[:, dc * NKEYS + kc * 128:
                                     dc * NKEYS + (kc + 1) * 128],
                               waits=(kck,))
                    tk.append(eck)
            keysT_ck = tk[-1]
            for oc in range(4):
                for ts in range(4):
                    k = ts % 2
                    mw = tuple(wq_ck) + tuple(xt_ck) + \
                         ((psq_used[k],) if psq_used[k] else ())
                    mck = None
                    for c in range(NC):
                        mck = P.op("tensor", (lambda c=c, oc=oc, ts=ts, k=k:
                                              lambda e: e.matmul(
                            out=ps_q[k][:],
                            lhsT=wqs[c][:, oc * 128:(oc + 1) * 128],
                            rhs=xhatT[c][:, ts * 512:(ts + 1) * 512],
                            start=(c == 0), stop=(c == NC - 1)))(),
                            waits=mw if c == 0 else (),
                            inc=(1 if c == NC - 1 else 0))
                    ew = (mck,) + ((qph_used,) if qph_used else ())
                    P.op("scalar", (lambda oc=oc, ts=ts, k=k:
                                    lambda e: e.activation(
                        out=qph[oc][:, ts * 512:(ts + 1) * 512],
                        in_=ps_q[k][:], func=AF.Copy))(), waits=ew)
                    psq_used[k] = P.ck("scalar")
            qready = P.ck("scalar")
            prev_qready = qready
            sxa = sx[h] if p == 0 else sy[h]
            ixa = ix[h] if p == 0 else iy[h]
            for t in range(NT):
                k = t % 2
                mw = (qready, keysT_ck) + \
                     ((pss_used[k],) if pss_used[k] else ())
                mck = None
                for dc in range(4):
                    mck = P.op("tensor", (lambda dc=dc, t=t, k=k:
                                          lambda e: e.matmul(
                        out=ps_s[k][:],
                        lhsT=qph[dc][:, t * 128:(t + 1) * 128],
                        rhs=keysT[:, dc * NKEYS:(dc + 1) * NKEYS],
                        start=(dc == 0), stop=(dc == 3)))(),
                        waits=mw if dc == 0 else (), inc=(1 if dc == 3 else 0))
                vw = (mck,) + ((m8_used,) if m8_used else ())
                P.op("vector", (lambda k=k: lambda e: e.max(
                    out=m8[:], in_=ps_s[k][:]))(), waits=vw)
                P.op("vector", (lambda k=k: lambda e: e.max_index(
                    out=i8[:], in_max=m8[:], in_values=ps_s[k][:]))(),
                    waits=(P.ck("vector"),))
                P.op("vector", lambda e: e.tensor_copy(i8f[:], i8[:]),
                     waits=(P.ck("vector"),))
                P.op("vector", (lambda sxa=sxa, t=t: lambda e: e.tensor_copy(
                    sxa[:, t:t + 1], m8[:, 0:1]))(), waits=(P.ck("vector"),))
                P.op("vector", (lambda ixa=ixa, t=t: lambda e: e.tensor_copy(
                    ixa[:, t:t + 1], i8f[:, 0:1]))(), waits=(P.ck("vector"),))
                m8_used = P.ck("vector")
                pss_used[k] = m8_used
            qph_used = m8_used
            keys_used = keysT_ck
    idxu = [P.sbuf(f"pk_idxu{h}", [128, NT], U32) for h in range(HEADS)]
    sc = [P.sbuf(f"pk_sc{h}", [128, NT], F32) for h in range(HEADS)]
    idx_cks = []
    sc_cks = []
    for h in range(HEADS):
        P.op("vector", (lambda h=h: lambda e: e.tensor_tensor(
            out=sc[h][:], in0=sx[h][:], in1=sy[h][:], op=ALU.add))(),
            waits=(P.ck("vector"),))
        P.op("vector", (lambda h=h: lambda e: e.tensor_scalar(
            out=sc[h][:], in0=sc[h][:], scalar1=0.0, scalar2=None,
            op0=ALU.max))(), waits=(P.ck("vector"),))
        sc_cks.append(P.ck("vector"))
        P.op("vector", (lambda h=h: lambda e: e.scalar_tensor_tensor(
            out=ix[h][:], in0=ix[h][:], scalar=float(NKEYS), in1=iy[h][:],
            op0=ALU.mult, op1=ALU.add))(), waits=(P.ck("vector"),))
        P.op("vector", (lambda h=h: lambda e: e.tensor_copy(
            idxu[h][:], ix[h][:]))(), waits=(P.ck("vector"),))
        idx_cks.append(P.ck("vector"))

    gat = [P.sbuf(f"pk_gat{i}", [128, D], F32) for i in range(4)]
    gat_ck = [None] * 4
    gat_used = [None] * 4
    hv = [P.sbuf(f"pk_hv{h}", [128, NT], F32) for h in range(HEADS)]
    dmp = P.sbuf("pk_dmp", [128, D], F16)
    gi = 0
    for t in range(NT):
        for h in range(HEADS):
            g = gi % 4
            gi += 1
            gw = (idx_cks[h],) + ((gat_used[g],) if gat_used[g] else ())
            gat_ck[g] = P.op("gpsimd", (lambda g=g, h=h, t=t:
                                        lambda e: e.indirect_dma_start(
                out=gat[g][:], out_offset=None, in_=wdown_hbm,
                in_offset=bass.IndirectOffsetOnAxis(
                    ap=idxu[h][:, t:t + 1], axis=0)))(), waits=gw, inc=16,
                stream=f"gat{g}", _is_dma=True)
            P.op("vector", (lambda g=g, h=h, t=t: lambda e:
                            e.scalar_tensor_tensor(
                out=dmp[:], in0=gat[g][:], scalar=1.0,
                in1=xh_tiles[t][:], op0=ALU.bypass, op1=ALU.mult,
                accum_out=hv[h][:, t:t + 1]))(),
                waits=(gat_ck[g], xh_cks[t]))
            gat_used[g] = P.ck("vector")
    wgt = hv
    for h in range(HEADS):
        P.op("scalar", (lambda h=h: lambda e: e.activation(
            out=sx[h][:], in_=hv[h][:], func=AF.Gelu))(),
            waits=(P.ck("vector"), sc_cks[h]))
        P.op("vector", (lambda h=h: lambda e: e.tensor_tensor(
            out=wgt[h][:], in0=sx[h][:], in1=sc[h][:], op=ALU.mult))(),
            waits=(P.ck("scalar"),))
    wgt_ck = P.ck("vector")

    x_s = [P.sbuf(f"pk_xr{i}", [128, D], F32) for i in range(2)]
    x_used = [None, None]
    gat_used = [None] * 4
    last_store = None
    for t in range(NT):
        k = t % 2
        xck = P.dma("sync", x_s[k][:], xres_in[t * 128:(t + 1) * 128, :],
                    waits=((x_used[k],) if x_used[k] else ()))
        rck = xck
        for h in range(HEADS):
            g = gi % 4
            gi += 1
            gw = (idx_cks[h],) + ((gat_used[g],) if gat_used[g] else ())
            gck = P.op("gpsimd", (lambda g=g, h=h, t=t:
                                  lambda e: e.indirect_dma_start(
                out=gat[g][:], out_offset=None, in_=wup_hbm,
                in_offset=bass.IndirectOffsetOnAxis(
                    ap=idxu[h][:, t:t + 1], axis=0)))(), waits=gw, inc=16,
                stream=f"gat{g}", _is_dma=True)
            P.op("vector", (lambda g=g, h=h, t=t, k=k: lambda e:
                            e.scalar_tensor_tensor(
                out=x_s[k][:], in0=gat[g][:], scalar=wgt[h][:, t:t + 1],
                in1=x_s[k][:], op0=ALU.mult, op1=ALU.add))(),
                waits=(gck, rck, wgt_ck))
            rck = P.ck("vector")
            gat_used[g] = rck
        last_store = P.dma("sync", xres_out[t * 128:(t + 1) * 128, :],
                           x_s[k][:], waits=(rck,))
        x_used[k] = last_store
    dcks = []
    if dbg is not None:
        for h in range(HEADS):
            dcks.append(P.dma("sync", dbg["ix"][h], ix[h][:], waits=(P.ck("vector"),)))
            dcks.append(P.dma("sync", dbg["sc"][h], sc[h][:], waits=(P.ck("vector"),)))
            dcks.append(P.dma("sync", dbg["wgt"][h], wgt[h][:], waits=(P.ck("vector"),)))
    P.wait("sync", (last_store, x_used[0], x_used[1]) + tuple(dcks))


"""L0_attn_b: RoPE + relu linear attention with rmsnorms."""
import numpy as np

NH = 16
HS = 64
RMS_EPS = 1.1920929e-07
EPS_LIN = 1e-06
THETA = 10000.0
O3 = 3 * D


def rope_tables_host(parity):
    pos = np.concatenate([np.arange(512) + 512 * (parity + 2 * i)
                          for i in range(4)])
    inv_freq = 1.0 / THETA ** (np.arange(0, HS, 2, dtype=np.float64) / HS)
    fr = np.outer(pos, inv_freq)
    cos = np.cos(fr).astype(np.float32)
    sin = np.sin(fr).astype(np.float32)
    return (np.tile(cos, (1, NH)).astype(np.float32),
            np.tile(sin, (1, NH)).astype(np.float32))


def emit_l0attnB_1(P, consts, tp, wqkv_hbm, bqkv_hbm, gq_hbm, gk_hbm,
                   cos_hbm, sin_hbm, xres_in, qfT_hbm, vk_hbm, dbg=None):
    xhatT, xt_ck = emit_ln_xhatT(P, tp, "ab", xres_in)
    wT = [P.sbuf(f"ab_wT{c}", [128, O3], BF16) for c in range(NC)]
    wrow = P.sbuf("ab_wrow", [128, D], BF16)
    prev_T = None
    for oc in range(O3 // 128):
        w = ((prev_T,) if prev_T else ())
        ldck = P.dma("gpsimd", wrow[:], wqkv_hbm[oc * 128:(oc + 1) * 128, :],
                     waits=w)
        for c in range(NC):
            prev_T = tp.t(wrow[:, c * 128:(c + 1) * 128],
                          wT[c][:, oc * 128:(oc + 1) * 128], waits=(ldck,))
    wT_ck = (P.ck("scalar"),)
    ps_p = [P.psum(f"ab_psp{i}", [128, 512], F32) for i in range(2)]
    bbc, bbc_ck = bcast_row(P, consts, bqkv_hbm, O3, "ab_bb", dtype=F32,
                            ps=ps_p[0])
    gqbc, gq_ck = bcast_row(P, consts, gq_hbm, D, "ab_gq", dtype=F32,
                            ps=ps_p[0], ps_free=(bbc_ck,))
    gkbc, gk_ck = bcast_row(P, consts, gk_hbm, D, "ab_gk", dtype=F32,
                            ps=ps_p[0], ps_free=(gq_ck,))
    cosS = [P.sbuf(f"ab_cosS{i}", [128, 512], F32) for i in range(2)]
    sinS = [P.sbuf(f"ab_sinS{i}", [128, 512], F32) for i in range(2)]
    rope_used = [None, None]

    qkv = P.sbuf("ab_qkv", [128, O3], F32)
    qf = P.sbuf("ab_qf", [128, D], BF16)
    kf = P.sbuf("ab_kf", [128, D], BF16)
    vp = P.sbuf("ab_vp", [128, NH * (HS + 1)], BF16)
    rr = P.sbuf("ab_rr", [128, D], F32)
    stt = P.sbuf("ab_stt", [128, 8], F32)
    dmp = P.sbuf("ab_dmp", [128, D], BF16)
    vk_sb = P.sbuf("ab_vksb", [64, NH * (HS + 1)], F32)
    ps_vk = [P.psum(f"ab_psvk{i}", [64, HS + 1], F32) for i in range(2)]
    stage = [P.sbuf(f"ab_stg{i}", [128, 128], BF16) for i in range(2)]
    epsc, epsck = P.ccol(RMS_EPS)
    P.op("vector", lambda e: e.memset(vk_sb[:], 0.0))
    vk_zck = P.ck("vector")
    vk_used = vk_zck
    psp_used = [gk_ck, None]
    psvk_used = [None, None]
    st_used = [None, None]
    qkv_used = None
    si = 0
    vki = 0
    for t in range(NT):
        rk = t % 2
        rw = ((rope_used[rk],) if rope_used[rk] else ())
        cos_ck = P.dma("sync", cosS[rk][:], cos_hbm[t * 128:(t + 1) * 128, :],
                       waits=rw)
        sin_ck = P.dma("sync", sinS[rk][:],
                       sin_hbm[t * 128:(t + 1) * 128, :], waits=rw)
        rope_ck = (cos_ck, sin_ck)
        for s in range(6):
            k = s % 2
            mw = wT_ck + tuple(xt_ck) + \
                 ((psp_used[k],) if psp_used[k] else ())
            mck = None
            for c in range(NC):
                mck = P.op("tensor", (lambda c=c, s=s, t=t, k=k:
                                      lambda e: e.matmul(
                    out=ps_p[k][:], lhsT=xhatT[c][:, t * 128:(t + 1) * 128],
                    rhs=wT[c][:, s * 512:(s + 1) * 512],
                    start=(c == 0), stop=(c == NC - 1)))(),
                    waits=mw if c == 0 else (), inc=(1 if c == NC - 1 else 0))
            ew = (mck, bbc_ck)
            P.op("vector", (lambda s=s, k=k: lambda e: e.tensor_tensor(
                out=qkv[:, s * 512:(s + 1) * 512], in0=ps_p[k][:],
                in1=bbc[:, s * 512:(s + 1) * 512], op=ALU.add))(), waits=ew)
            psp_used[k] = P.ck("vector")
        qkv_ck = P.ck("vector")
        if dbg is not None and t == 0:
            P.dma("sync", dbg["qkv"], qkv[:], waits=(qkv_ck,))
        for j, (g_bc, g_ck) in enumerate(((gqbc, gq_ck), (gkbc, gk_ck))):
            seg = qkv[:, j * D:(j + 1) * D]
            P.op("vector", (lambda seg=seg, j=j: lambda e:
                            e.scalar_tensor_tensor(
                out=dmp[:], in0=seg, scalar=1.0, in1=seg,
                op0=ALU.bypass, op1=ALU.mult,
                accum_out=stt[:, j:j + 1]))(), waits=(qkv_ck,))
            P.op("scalar", (lambda j=j: lambda e: e.activation(
                out=stt[:, 4 + j:5 + j], in_=stt[:, j:j + 1], func=AF.Sqrt,
                bias=epsc[:, 0:1], scale=1.0 / D))(),
                waits=(P.ck("vector"), epsck))
            P.op("vector", (lambda j=j: lambda e: e.reciprocal(
                out=stt[:, 4 + j:5 + j], in_=stt[:, 4 + j:5 + j]))(),
                waits=(P.ck("scalar"),))
            P.op("vector", (lambda seg=seg, j=j: lambda e: e.tensor_scalar(
                out=seg, in0=seg, scalar1=stt[:, 4 + j:5 + j], scalar2=None,
                op0=ALU.mult))(), waits=(P.ck("vector"),))
            P.op("vector", (lambda seg=seg, g_bc=g_bc: lambda e:
                            e.tensor_tensor(
                out=seg, in0=seg, in1=g_bc[:], op=ALU.mult))(),
                waits=(P.ck("vector"), g_ck))
        rms_ck = P.ck("vector")
        if dbg is not None and t == 0:
            P.dma("sync", dbg["rms"], qkv[:], waits=(rms_ck,))
        qf_war = st_used[1] if si > 0 else None
        for j, dst in enumerate((qf, kf)):
            seg = qkv[:, j * D:(j + 1) * D]
            ev = seg.rearrange("p (w two) -> p w two", two=2)[:, :, 0:1] \
                .rearrange("p w o -> p (w o)")
            od = seg.rearrange("p (w two) -> p w two", two=2)[:, :, 1:2] \
                .rearrange("p w o -> p (w o)")
            r_ev = rr[:, 0:512]
            r_od = rr[:, 512:1024]
            P.op("vector", (lambda ev=ev, rk=rk, r_ev=r_ev: lambda e:
                            e.tensor_tensor(out=r_ev, in0=ev,
                                            in1=cosS[rk][:], op=ALU.mult))(),
                 waits=(rms_ck, P.ck("vector"), st_used[0],
                        st_used[1]) + rope_ck)
            P.op("vector", (lambda od=od, rk=rk, r_od=r_od: lambda e:
                            e.tensor_tensor(out=r_od, in0=od,
                                            in1=cosS[rk][:], op=ALU.mult))())
            dm1 = dmp[:, 0:512]
            P.op("vector", (lambda od=od, rk=rk, dm1=dm1: lambda e:
                            e.tensor_tensor(out=dm1, in0=od,
                                            in1=sinS[rk][:], op=ALU.mult))())
            P.op("vector", (lambda r_ev=r_ev, dm1=dm1: lambda e:
                            e.tensor_tensor(out=r_ev, in0=r_ev, in1=dm1,
                                            op=ALU.subtract))(),
                 waits=(P.ck("vector"),))
            P.op("vector", (lambda ev=ev, rk=rk, dm1=dm1: lambda e:
                            e.tensor_tensor(out=dm1, in0=ev,
                                            in1=sinS[rk][:], op=ALU.mult))(),
                 waits=(P.ck("vector"),))
            P.op("vector", (lambda r_od=r_od, dm1=dm1: lambda e:
                            e.tensor_tensor(out=r_od, in0=r_od, in1=dm1,
                                            op=ALU.add))(),
                 waits=(P.ck("vector"),))
            dev = dst[:].rearrange("p (w two) -> p w two", two=2)[:, :, 0:1] \
                .rearrange("p w o -> p (w o)")
            dod = dst[:].rearrange("p (w two) -> p w two", two=2)[:, :, 1:2] \
                .rearrange("p w o -> p (w o)")
            P.op("vector", (lambda dev=dev, r_ev=r_ev: lambda e:
                            e.tensor_scalar(out=dev, in0=r_ev, scalar1=0.0,
                                            scalar2=None, op0=ALU.max))(),
                 waits=(P.ck("vector"),))
            P.op("vector", (lambda dod=dod, r_od=r_od: lambda e:
                            e.tensor_scalar(out=dod, in0=r_od, scalar1=0.0,
                                            scalar2=None, op0=ALU.max))(),
                 waits=(P.ck("vector"),))
        qfkf_ck = P.ck("vector")
        rope_used[rk] = qfkf_ck
        if dbg is not None and t == 0:
            P.dma("sync", dbg["qf"], qf[:], waits=(qfkf_ck,))
            P.dma("sync", dbg["kf"], kf[:], waits=(qfkf_ck,))
        P.op("vector", (lambda t=t: lambda e: e.tensor_copy(
            vp[:].rearrange("p (h f) -> p h f", f=HS + 1)[:, :, 0:HS],
            qkv[:, 2 * D:3 * D].rearrange("p (h f) -> p h f", f=HS)))(),
            waits=(qkv_ck, P.ck("vector")))
        P.op("vector", lambda e: e.memset(
            vp[:].rearrange("p (h f) -> p h f", f=HS + 1)[:, :, HS:HS + 1],
            1.0), waits=(P.ck("vector"),))
        vp_ck = P.ck("vector")
        for h in range(NH):
            kv = vki % 2
            vki += 1
            mw = (qfkf_ck, vp_ck) + \
                 ((psvk_used[kv],) if psvk_used[kv] else ())
            mck = P.op("tensor", (lambda h=h, kv=kv: lambda e: e.matmul(
                out=ps_vk[kv][:], lhsT=kf[:, h * HS:(h + 1) * HS],
                rhs=vp[:, h * (HS + 1):(h + 1) * (HS + 1)],
                start=True, stop=True))(), waits=mw)
            P.op("vector", (lambda h=h, kv=kv: lambda e: e.tensor_tensor(
                out=vk_sb[:, h * (HS + 1):(h + 1) * (HS + 1)],
                in0=vk_sb[:, h * (HS + 1):(h + 1) * (HS + 1)],
                in1=ps_vk[kv][:], op=ALU.add))(),
                waits=(mck, vk_used))
            vk_used = P.ck("vector")
            psvk_used[kv] = vk_used
        for c in range(NC):
            k = si % 2
            si += 1
            ew = (qfkf_ck,) + ((st_used[k],) if st_used[k] else ())
            eck = tp.t(qf[:, c * 128:(c + 1) * 128], stage[k][:], waits=ew)
            st_used[k] = P.dma("sync",
                               qfT_hbm[c * 128:(c + 1) * 128,
                                       t * 128:(t + 1) * 128],
                               stage[k][:], waits=(eck,))
        qkv_used = None
    vkst = P.dma("sync", vk_hbm, vk_sb[:], waits=(vk_used,))
    P.wait("sync", (vkst,) + tuple(st_used))


def emit_l0attnB_2(P, consts, tp, qfT_hbm, vk_own_hbm, vk_pair_hbm,
                   wout_hbm, bout_hbm, gout_hbm, xres_in, xres_out):
    qfT_h = [P.sbuf(f"b2_qf{h}", [HS, T], BF16) for h in range(NH)]
    qf_cks = [P.dma("sync", qfT_h[h][:], qfT_hbm[h * HS:(h + 1) * HS, :])
              for h in range(NH)]
    vko = P.sbuf("b2_vko", [64, NH * (HS + 1)], F32)
    vkp = P.sbuf("b2_vkp", [64, NH * (HS + 1)], F32)
    vkb = P.sbuf("b2_vkb", [64, NH * (HS + 1)], BF16)
    c1 = P.dma("sync", vko[:], vk_own_hbm)
    c2 = P.dma("sync", vkp[:], vk_pair_hbm)
    P.op("vector", lambda e: e.tensor_tensor(
        out=vko[:], in0=vko[:], in1=vkp[:], op=ALU.add), waits=(c1, c2))
    P.op("vector", lambda e: e.tensor_copy(vkb[:], vko[:]),
         waits=(P.ck("vector"),))
    vk_ck = P.ck("vector")
    woT = [P.sbuf(f"b2_woT{c}", [128, D], BF16) for c in range(NC)]
    wrow = P.sbuf("b2_wrow", [128, D], BF16)
    prev_T = None
    for oc in range(NC):
        w = ((prev_T,) if prev_T else ())
        ldck = P.dma("gpsimd", wrow[:], wout_hbm[oc * 128:(oc + 1) * 128, :],
                     waits=w)
        for c in range(NC):
            prev_T = tp.t(wrow[:, c * 128:(c + 1) * 128],
                          woT[c][:, oc * 128:(oc + 1) * 128], waits=(ldck,))
    woT_ck = (P.ck("scalar"),)
    bout_bc, bout_ck = bcast_row(P, consts, bout_hbm, D, "b2_bo", dtype=F32)
    gout_bc, gout_ck = bcast_row(P, consts, gout_hbm, D, "b2_go", dtype=F32)
    epsc, epsck = P.ccol(RMS_EPS)
    attn = P.sbuf("b2_attn", [128, D], BF16)
    attnT = P.sbuf("b2_attnT", [128, D], BF16)
    res = P.sbuf("b2_res", [128, NH * (HS + 1)], F32)
    z = P.sbuf("b2_z", [128, D], F32)
    dmp = P.sbuf("b2_dmp", [128, D], BF16)
    stt = P.sbuf("b2_stt", [128, 4], F32)
    x_s = [P.sbuf(f"b2_x{i}", [128, D], F32) for i in range(2)]
    ps_r = [P.psum(f"b2_psr{i}", [128, HS + 1], F32) for i in range(2)]
    ps_w = [P.psum(f"b2_psw{i}", [128, 512], F32) for i in range(2)]
    x_used = [None, None]
    psr_used = [None, None]
    psw_used = [None, None]
    attnT_used = None
    res_used = None
    last_store = None
    ri = 0
    for t in range(NT):
        k = t % 2
        xck = P.dma("sync", x_s[k][:], xres_in[t * 128:(t + 1) * 128, :],
                    waits=((x_used[k],) if x_used[k] else ()))
        for h in range(NH):
            kr = ri % 2
            ri += 1
            mw = (qf_cks[h], vk_ck) + \
                 ((psr_used[kr],) if psr_used[kr] else ())
            mck = P.op("tensor", (lambda h=h, t=t, kr=kr: lambda e: e.matmul(
                out=ps_r[kr][:], lhsT=qfT_h[h][:, t * 128:(t + 1) * 128],
                rhs=vkb[:, h * (HS + 1):(h + 1) * (HS + 1)],
                start=True, stop=True))(), waits=mw)
            P.op("vector", (lambda h=h, kr=kr: lambda e: e.tensor_copy(
                res[:, h * (HS + 1):(h + 1) * (HS + 1)], ps_r[kr][:]))(),
                waits=(mck,) + ((res_used,) if res_used and h == 0 else ()))
            psr_used[kr] = P.ck("vector")
        res_ck = P.ck("vector")
        for h in range(NH):
            P.op("vector", (lambda h=h: lambda e: e.tensor_scalar(
                out=stt[:, 0:1], in0=res[:, h * 65 + 64:h * 65 + 65],
                scalar1=float(EPS_LIN), scalar2=None, op0=ALU.add))(),
                waits=(res_ck, P.ck("vector")))
            P.op("vector", lambda e: e.reciprocal(
                out=stt[:, 1:2], in_=stt[:, 0:1]), waits=(P.ck("vector"),))
            P.op("vector", (lambda h=h: lambda e: e.tensor_scalar(
                out=attn[:, h * 64:(h + 1) * 64],
                in0=res[:, h * 65:h * 65 + 64],
                scalar1=stt[:, 1:2], scalar2=None, op0=ALU.mult))(),
                waits=(P.ck("vector"),))
        attn_ck = P.ck("vector")
        res_used = attn_ck
        tcks = []
        for c in range(NC):
            ew = (attn_ck,) + ((attnT_used,) if attnT_used and c == 0 else ())
            eck = tp.t(attn[:, c * 128:(c + 1) * 128],
                       attnT[:, c * 128:(c + 1) * 128], waits=ew)
            tcks.append(eck)
        for half in range(2):
            mw = woT_ck + (tcks[-1],) + \
                 ((psw_used[half],) if psw_used[half] else ())
            mck = None
            for c in range(NC):
                mck = P.op("tensor", (lambda c=c, half=half:
                                      lambda e: e.matmul(
                    out=ps_w[half][:], lhsT=attnT[:, c * 128:(c + 1) * 128],
                    rhs=woT[c][:, half * 512:(half + 1) * 512],
                    start=(c == 0), stop=(c == NC - 1)))(),
                    waits=mw if c == 0 else (), inc=(1 if c == NC - 1 else 0))
            P.op("vector", (lambda half=half: lambda e: e.tensor_tensor(
                out=z[:, half * 512:(half + 1) * 512], in0=ps_w[half][:],
                in1=bout_bc[:, half * 512:(half + 1) * 512], op=ALU.add))(),
                waits=(mck, bout_ck))
            psw_used[half] = P.ck("vector")
        attnT_used = P.ck("tensor")
        z_ck = P.ck("vector")
        P.op("vector", lambda e: e.scalar_tensor_tensor(
            out=dmp[:], in0=z[:], scalar=1.0, in1=z[:],
            op0=ALU.bypass, op1=ALU.mult, accum_out=stt[:, 2:3]),
            waits=(z_ck,))
        P.op("scalar", lambda e: e.activation(
            out=stt[:, 3:4], in_=stt[:, 2:3], func=AF.Sqrt,
            bias=epsc[:, 0:1], scale=1.0 / D),
            waits=(P.ck("vector"), epsck))
        P.op("vector", lambda e: e.reciprocal(
            out=stt[:, 3:4], in_=stt[:, 3:4]), waits=(P.ck("scalar"),))
        P.op("vector", lambda e: e.tensor_scalar(
            out=z[:], in0=z[:], scalar1=stt[:, 3:4], scalar2=None,
            op0=ALU.mult), waits=(P.ck("vector"),))
        P.op("vector", (lambda: lambda e: e.tensor_tensor(
            out=z[:], in0=z[:], in1=gout_bc[:], op=ALU.mult))(),
            waits=(P.ck("vector"), gout_ck))
        P.op("vector", (lambda k=k: lambda e: e.tensor_tensor(
            out=x_s[k][:], in0=x_s[k][:], in1=z[:], op=ALU.add))(),
            waits=(P.ck("vector"), xck))
        last_store = P.dma("sync", xres_out[t * 128:(t + 1) * 128, :],
                           x_s[k][:], waits=(P.ck("vector"),))
        x_used[k] = last_store
    P.wait("sync", (x_used[0], x_used[1]))


"""L1_attn_a: causal masked-score attention (no softmax)."""
import numpy as np

NH = 16
HS = 64
LFULL = 4096


def causal_masks_host():
    import ml_dtypes
    m = np.zeros((128, 4, 512), np.float32)
    for j in range(4):
        for kk in range(128):
            m[kk, j, :] = (np.arange(512) >= kk + 128 * j)
    return m.reshape(128, 2048).astype(ml_dtypes.bfloat16)


def emit_qkv_heads(P, consts, tp, xres_in, w_hbms, b_hbms, out_hbms,
                   vt_idx=None, vt_hbm=None, scale=1.0, scale_idx=None):
    xhatT, xt_ck = emit_ln_xhatT(P, tp, "qk", xres_in)
    tst = [P.sbuf(f"qk_tst{i}", [128, 128], BF16) for i in range(2)]
    tst_ck = [None, None]
    state = {"si": 0}
    all_store_cks = []
    for i, (w_hbm, b_hbm, out_hbm) in enumerate(zip(w_hbms, b_hbms, out_hbms)):
        if b_hbm is not None:
            bias_col, bias_ck = load_bias_cols(P, tp, b_hbm, D, f"qk_b{i}")
        else:
            bias_col, bias_ck = None, None
        sc = scale if (scale_idx is None or i in scale_idx) else 1.0

        def out_cb(oc, ts, src_ap, ck, i=i, out_hbm=out_hbm):
            dck = P.dma("sync", out_hbm[oc * 128:(oc + 1) * 128,
                                        ts * 512:(ts + 1) * 512],
                        src_ap, waits=(ck,))
            all_store_cks.append(dck)
            if vt_idx is not None and i == vt_idx:
                for j in range(4):
                    k = state["si"] % 2
                    state["si"] += 1
                    ew = (ck,) + ((tst_ck[k],) if tst_ck[k] else ())
                    eck = tp.t(src_ap[:, j * 128:(j + 1) * 128],
                               tst[k][:], waits=ew)
                    tst_ck[k] = P.dma("sync",
                                      vt_hbm[ts * 512 + j * 128:
                                             ts * 512 + (j + 1) * 128,
                                             oc * 128:(oc + 1) * 128],
                                      tst[k][:], waits=(eck,))
                    all_store_cks.append(tst_ck[k])
            return dck

        emit_proj_rowsO(P, tp, f"qk_p{i}", xhatT, xt_ck, w_hbm, bias_col,
                        bias_ck, out_cb, scale=sc)
    P.wait("sync", tuple(all_store_cks))


def emit_l1attnA_core(P, consts, tp, parity, q_hbm, k_hbm, vt_hbm, cmask_hbm,
                      wo_hbm, wob_hbm, xres_in, xres_out):
    q_sb = [P.sbuf(f"aa_q{h}", [64, T], BF16) for h in range(NH)]
    q_cks = [P.dma("sync", q_sb[h][:], q_hbm[h * HS:(h + 1) * HS, :])
             for h in range(NH)]
    cmask = P.sbuf("aa_cmask", [128, 2048], BF16)
    cm_ck = P.dma("sync", cmask[:], cmask_hbm)
    attn = [P.sbuf(f"aa_at{c}", [128, T], BF16) for c in range(NC)]
    kh = [P.sbuf(f"aa_kh{i}", [64, LFULL], BF16) for i in range(2)]
    vth = [P.sbuf(f"aa_vth{i}", [128, 32 * 64], BF16) for i in range(2)]
    s_bf = [P.sbuf(f"aa_sbf{i}", [128, 512], BF16) for i in range(2)]
    ps_s = [P.psum(f"aa_pss{i}", [128, 512], F32) for i in range(2)]
    ps_o = [P.psum(f"aa_pso{i}", [64, 512], F32) for i in range(2)]
    kh_ck = [None, None]
    kh_used = [None, None]
    vth_ck = [None, None]
    vth_used = [None, None]
    s_ck = [None, None]
    s_used = [None, None]
    pss_used = [None, None]
    pso_used = [None, None]
    si = 0
    oi = 0
    vt3 = [v[:].rearrange("p (kc f) -> p kc f", f=64) for v in vth]
    for h in range(NH):
        hk = h % 2
        w = ((kh_used[hk],) if kh_used[hk] else ())
        kh_ck[hk] = P.dma("gpsimd", kh[hk][:],
                          k_hbm[h * HS:(h + 1) * HS, :], waits=w)
        w = ((vth_used[vk],) if vth_used[(vk := h % 2)] else ())
        vth_ck[hk] = P.dma("gpsimd", vt3[hk],
                           vt_hbm.rearrange("(kc p) f -> p kc f", p=128)
                           [:, :, h * 64:(h + 1) * 64], waits=w)
        hc, hr = h // 2, (h % 2) * 64
        for bq in range(4):
            gb = parity + 2 * bq
            ko = oi % 2
            oi += 1
            nkc = 4 * (gb + 1)
            last_pv = None
            for kc in range(nkc):
                k = si % 2
                si += 1
                mw = (kh_ck[hk], q_cks[h]) + \
                     ((pss_used[k],) if pss_used[k] else ())
                mck = P.op("tensor", (lambda hk=hk, kc=kc, bq=bq, k=k, h=h:
                                      lambda e: e.matmul(
                    out=ps_s[k][:], lhsT=kh[hk][:, kc * 128:(kc + 1) * 128],
                    rhs=q_sb[h][:, bq * 512:(bq + 1) * 512],
                    start=True, stop=True))(), waits=mw)
                ew = (mck,) + ((s_used[k],) if s_used[k] else ())
                j = kc - 4 * gb
                if j >= 0:
                    P.op("vector", (lambda k=k, j=j: lambda e: e.tensor_tensor(
                        out=s_bf[k][:], in0=ps_s[k][:],
                        in1=cmask[:, j * 512:(j + 1) * 512], op=ALU.mult))(),
                        waits=ew + (cm_ck,))
                    s_ck[k] = P.ck("vector")
                    pss_used[k] = s_ck[k]
                else:
                    P.op("scalar", (lambda k=k: lambda e: e.activation(
                        out=s_bf[k][:], in_=ps_s[k][:], func=AF.Copy))(),
                        waits=ew)
                    s_ck[k] = P.ck("scalar")
                    pss_used[k] = s_ck[k]
                pw = (s_ck[k], vth_ck[hk]) + \
                     (((pso_used[ko],) if pso_used[ko] else ()) if kc == 0 else ())
                pvck = P.op("tensor", (lambda hk=hk, kc=kc, k=k, ko=ko,
                                       nkc=nkc: lambda e: e.matmul(
                    out=ps_o[ko][:], lhsT=vt3[hk][:, kc, :],
                    rhs=s_bf[k][:], start=(kc == 0),
                    stop=(kc == nkc - 1)))(), waits=pw, inc=1)
                s_used[k] = pvck
                last_pv = pvck
            P.op("scalar", (lambda hc=hc, hr=hr, bq=bq, ko=ko:
                            lambda e: e.activation(
                out=attn[hc][hr:hr + 64, bq * 512:(bq + 1) * 512],
                in_=ps_o[ko][:], func=AF.Copy))(), waits=(last_pv,))
            pso_used[ko] = P.ck("scalar")
        kh_used[hk] = last_pv
        vth_used[hk] = last_pv
    attn_ck = (P.ck("scalar"), P.ck("vector"))

    woT = [P.sbuf(f"aa_woT{c}", [128, D], BF16) for c in range(NC)]
    wrow = P.sbuf("aa_wrow", [128, D], BF16)
    prev_T = None
    for oc in range(NC):
        w = ((prev_T,) if prev_T else ())
        ldck = P.dma("gpsimd", wrow[:], wo_hbm[oc * 128:(oc + 1) * 128, :],
                     waits=w)
        for c in range(NC):
            prev_T = tp.t(wrow[:, c * 128:(c + 1) * 128],
                          woT[c][:, oc * 128:(oc + 1) * 128], waits=(ldck,))
    woT_ck = (P.ck("scalar"),)
    wob_bc, wob_ck = bcast_row(P, consts, wob_hbm, D, "aa_wob", dtype=F32,
                               ps=ps_s[1], ps_free=(pss_used[1],))
    x_s = [P.sbuf(f"aa_x{i}", [128, D], F32) for i in range(2)]
    ps_w = ps_s
    psw_used = [pss_used[0], wob_ck]
    x_used = [None, None]
    last_store = None
    for t in range(NT):
        k = t % 2
        xck = P.dma("sync", x_s[k][:], xres_in[t * 128:(t + 1) * 128, :],
                    waits=((x_used[k],) if x_used[k] else ()))
        for half in range(2):
            mw = woT_ck + attn_ck + \
                 ((psw_used[half],) if psw_used[half] else ())
            mck = None
            for c in range(NC):
                mck = P.op("tensor", (lambda c=c, t=t, half=half:
                                      lambda e: e.matmul(
                    out=ps_w[half][:],
                    lhsT=attn[c][:, t * 128:(t + 1) * 128],
                    rhs=woT[c][:, half * 512:(half + 1) * 512],
                    start=(c == 0), stop=(c == NC - 1)))(),
                    waits=mw if c == 0 else (), inc=(1 if c == NC - 1 else 0))
            P.op("vector", (lambda k=k, half=half: lambda e: e.tensor_tensor(
                out=x_s[k][:, half * 512:(half + 1) * 512],
                in0=x_s[k][:, half * 512:(half + 1) * 512],
                in1=ps_w[half][:], op=ALU.add))(),
                waits=(mck, xck, wob_ck))
            P.op("vector", (lambda k=k, half=half: lambda e: e.tensor_tensor(
                out=x_s[k][:, half * 512:(half + 1) * 512],
                in0=x_s[k][:, half * 512:(half + 1) * 512],
                in1=wob_bc[:, half * 512:(half + 1) * 512], op=ALU.add))())
            psw_used[half] = P.ck("vector")
        last_store = P.dma("sync", xres_out[t * 128:(t + 1) * 128, :],
                           x_s[k][:], waits=(P.ck("vector"),))
        x_used[k] = last_store
    P.wait("sync", (x_used[0], x_used[1]))


"""L1_ffn (fp4 quant-dequant SwiGLU) emitters."""
import numpy as np

I_FF = 2816
N_SL = 6
SLICES = [(s * 512, min(I_FF, (s + 1) * 512)) for s in range(N_SL)]
NCH_FF = I_FF // 128


def emit_l1ffn_wprep(P, consts, tp, w_gate, w_up, w_down, o_gT, o_uT, o_dT,
                     gu_rows, d_rows):
    scr = {
        "mx": P.sbuf("wq_mx", [128, I_FF // 16], F32),
        "inv": P.sbuf("wq_inv", [128, I_FF // 16], F32),
        "a": P.sbuf("wq_a", [128, I_FF], BF16),
        "q": P.sbuf("wq_q", [128, I_FF], BF16),
        "sgn": P.sbuf("wq_sgn", [128, I_FF], BF16),
    }
    wt = P.sbuf("wq_wt", [128, I_FF], BF16)
    wq = P.sbuf("wq_wq", [128, I_FF], BF16)
    stage = [P.sbuf(f"wq_st{i}", [128, 128], BF16) for i in range(2)]
    st_ck = [None, None]
    si = 0
    prev = None
    for (w_hbm, o_hbm, rows, width) in ((w_gate, o_gT, gu_rows, D),
                                        (w_up, o_uT, gu_rows, D),
                                        (w_down, o_dT, d_rows, I_FF)):
        for r0 in range(0, rows, 128):
            r1 = min(rows, r0 + 128)
            nr = r1 - r0
            w = ((prev,) if prev else ())
            ldck = P.dma("gpsimd", wt[0:nr, 0:width], w_hbm[r0:r1, :],
                         waits=w)
            qck = emit_quant(P, wt[0:nr, 0:width], wq[0:nr, 0:width], scr,
                             width, waits=(ldck,))
            prev = qck
            for c0 in range(0, width, 128):
                k = si % 2
                si += 1
                ew = (qck,) + ((st_ck[k],) if st_ck[k] else ())
                eck = tp.t(wq[0:nr, c0:c0 + 128], stage[k][0:128, 0:nr],
                           waits=ew)
                st_ck[k] = P.dma("sync", o_hbm[c0:c0 + 128, r0:r1],
                                 stage[k][0:128, 0:nr], waits=(eck,))
    P.op("sync", lambda e: e.nop(), waits=[c for c in st_ck if c], inc=0)


def emit_l1ffn(P, consts, tp, xres_in, wgT_h, wuT_h, wdT_h, xres_out,
               group=4):
    x_s = [P.sbuf(f"f1_x{i}", [128, D], F32) for i in range(2)]
    xhat = P.sbuf("f1_xh", [128, D], BF16)
    dump = P.sbuf("f1_dump", [128, D], BF16)
    stat = P.sbuf("f1_stat", [128, 4 * NT], F32)
    used = [None, None]
    for t in range(NT):
        k = t % 2
        w = ((used[k],) if used[k] else ())
        ck = P.dma("sync", x_s[k][:], xres_in[t * 128:(t + 1) * 128, :],
                   waits=w)
        used[k] = emit_ln_stats_tile(P, x_s[k][:], dump[:], stat, t,
                                     waits=(ck,))
    ln_ck = emit_ln_finalize(P, stat)
    qscr = {
        "mx": P.sbuf("f1_qmx", [128, I_FF // 16], F32),
        "inv": P.sbuf("f1_qinv", [128, I_FF // 16], F32),
        "a": P.sbuf("f1_qa", [128, I_FF], BF16),
        "q": P.sbuf("f1_qq", [128, I_FF], BF16),
        "sgn": P.sbuf("f1_qsgn", [128, I_FF], BF16),
    }
    xq = P.sbuf("f1_xq", [128, D], BF16)
    xqT = [P.sbuf(f"f1_xqT{c}", [128, T], BF16) for c in range(NC)]
    prev_q = None
    for t in range(NT):
        k = t % 2
        w = ((used[k],) if used[k] else ())
        ck = P.dma("sync", x_s[k][:], xres_in[t * 128:(t + 1) * 128, :],
                   waits=w)
        ack = ln_affine(P, x_s[k][:], xhat[:], stat, t,
                        waits=(ck, ln_ck) + ((prev_q,) if prev_q else ()))
        used[k] = ack
        qck = emit_quant(P, xhat[:], xq[:], qscr, D, waits=(ack,))
        tcks = []
        for c in range(NC):
            eck = tp.t(xq[:, c * 128:(c + 1) * 128],
                       xqT[c][:, t * 128:(t + 1) * 128], waits=(qck,))
            tcks.append(eck)
        prev_q = tcks[-1]
    xqT_ready = (P.ck("scalar"),)

    wg_s = [P.sbuf(f"f1_wg{i}", [128, 512], BF16) for i in range(NC)]
    wu_s = [P.sbuf(f"f1_wu{i}", [128, 512], BF16) for i in range(NC)]
    wd_res = [P.sbuf(f"f1_wd{j}", [128, D], BF16) for j in range(NCH_FF)]
    wd_cks = [P.dma("gpsimd", wd_res[j][:], wdT_h[j * 128:(j + 1) * 128, :])
              for j in range(NCH_FF)]
    h_g = [P.sbuf(f"f1_h{i}", [128, I_FF], BF16) for i in range(group)]
    sg = P.sbuf("f1_sg", [128, 512], F32)
    hq = P.sbuf("f1_hq", [128, I_FF], BF16)
    hqT = P.sbuf("f1_hqT", [128, I_FF], BF16)
    ps_g = P.psum("f1_psg", [128, 512], F32)
    ps_u = P.psum("f1_psu", [128, 512], F32)
    ps_d = [P.psum(f"f1_psd{i}", [128, 512], F32) for i in range(2)]
    hq_scr = {
        "mx": qscr["mx"], "inv": qscr["inv"], "a": qscr["a"],
        "q": qscr["q"], "sgn": qscr["sgn"],
    }
    prev_slice_w = None
    prev_hq = None
    prev_sck = None
    prev_hck = None
    prev_resd = [None, None]
    last_store = None
    for g0 in range(0, NT, group):
        gtiles = range(g0, min(NT, g0 + group))
        for s, (o0, o1) in enumerate(SLICES):
            ow = o1 - o0
            lw = []
            for c in range(NC):
                w = (prev_slice_w,) if prev_slice_w else ()
                lw.append(P.dma("gpsimd", wg_s[c][:, 0:ow],
                                wgT_h[c * 128:(c + 1) * 128, o0:o1], waits=w))
                lw.append(P.dma("gpsimd", wu_s[c][:, 0:ow],
                                wuT_h[c * 128:(c + 1) * 128, o0:o1], waits=w))
            for t in gtiles:
                ti = t - g0
                gw = tuple(lw) + xqT_ready + ((prev_sck,) if prev_sck else ())
                ck_g = mm_accum(P, ps_g[:, 0:ow],
                                [(xqT[c][:, t * 128:(t + 1) * 128],
                                  wg_s[c][:, 0:ow]) for c in range(NC)],
                                waits=gw)
                uw = ((prev_hck,) if prev_hck else ())
                ck_u = mm_accum(P, ps_u[:, 0:ow],
                                [(xqT[c][:, t * 128:(t + 1) * 128],
                                  wu_s[c][:, 0:ow]) for c in range(NC)],
                                waits=uw)
                sw = (ck_g,) + ((prev_hck,) if prev_hck else ())
                sck = P.op("scalar", (lambda ow=ow: lambda e: e.activation(
                    out=sg[:, 0:ow], in_=ps_g[:, 0:ow], func=AF.Silu))(),
                    waits=sw)
                hck = P.op("vector", (lambda ti=ti, o0=o0, o1=o1, ow=ow:
                                      lambda e: e.tensor_tensor(
                    out=h_g[ti][:, o0:o1], in0=sg[:, 0:ow],
                    in1=ps_u[:, 0:ow], op=ALU.mult))(), waits=(sck, ck_u))
                prev_sck, prev_hck = sck, hck
            prev_slice_w = prev_hck
        for t in gtiles:
            ti = t - g0
            k = t % 2
            xw = ((used[k],) if used[k] else ())
            xck = P.dma("sync", x_s[k][:], xres_in[t * 128:(t + 1) * 128, :],
                        waits=xw)
            w = ((prev_hq,) if prev_hq else ()) + (prev_hck,)
            qck = emit_quant(P, h_g[ti][:], hq[:], hq_scr, I_FF, waits=w)
            tcks = []
            for j in range(NCH_FF):
                eck = tp.t(hq[:, j * 128:(j + 1) * 128],
                           hqT[:, j * 128:(j + 1) * 128], waits=(qck,))
                tcks.append(eck)
            prev_hq = tcks[-1]
            for half in range(2):
                dw = (tcks[-1],) + tuple(wd_cks) + \
                     ((prev_resd[half],) if prev_resd[half] else ())
                ck_d = mm_accum(
                    P, ps_d[half][:],
                    [(hqT[:, j * 128:(j + 1) * 128],
                      wd_res[j][:, half * 512:(half + 1) * 512])
                     for j in range(NCH_FF)],
                    waits=dw)
                P.op("vector", (lambda k=k, half=half: lambda e: e.tensor_tensor(
                    out=x_s[k][:, half * 512:(half + 1) * 512],
                    in0=x_s[k][:, half * 512:(half + 1) * 512],
                    in1=ps_d[half][:], op=ALU.add))(), waits=(ck_d, xck))
                prev_resd[half] = P.ck("vector")
            rck = P.ck("vector")
            last_store = P.dma("sync", xres_out[t * 128:(t + 1) * 128, :],
                               x_s[k][:], waits=(rck,))
            used[k] = last_store
    P.wait("sync", (last_store, used[0], used[1]))


"""L1_attn_b: top-64-threshold softmax attention."""
import numpy as np

NH = 16
HS = 64
LF = 4096
NKC = LF // 128
NG = 4


def emit_l1attnB_1(P, consts, tp, w_hbms, b_hbms, out_hbms, vt_hbm,
                   xres_in):
    xhatT, xt_ck = emit_ln_xhatT(P, tp, "eb", xres_in)
    w_tiles = [P.sbuf(f"eb_w{c}", [128, D], BF16) for c in range(NC)]
    tst = [P.sbuf(f"eb_tst{i}", [128, 128], BF16) for i in range(2)]
    tst_ck = [None, None]
    state = {"si": 0}
    prev_w = [None] * NC
    prev_proj = None
    all_stores = []
    for i, (w_hbm, b_hbm, out_hbm, scale) in enumerate(
            zip(w_hbms, b_hbms, out_hbms, (0.125, 1.0, 1.0))):
        w_cks = []
        for c in range(NC):
            w = ((prev_w[c],) if prev_w[c] else ()) + \
                ((prev_proj,) if prev_proj else ())
            prev_w[c] = P.dma("gpsimd", w_tiles[c][:],
                              w_hbm[c * 128:(c + 1) * 128, :], waits=w)
            w_cks.append(prev_w[c])
        bias_col, bias_ck = load_bias_cols(P, tp, b_hbm, D, f"eb_b{i}")
        if scale != 1.0:
            P.op("vector", (lambda bias_col=bias_col, scale=scale:
                            lambda e: e.tensor_scalar(
                out=bias_col[:], in0=bias_col[:], scalar1=float(scale),
                scalar2=None, op0=ALU.mult))(), waits=(bias_ck,))
            bias_ck = P.ck("vector")

        def out_cb(oc, ts, src_ap, ck, i=i, out_hbm=out_hbm):
            dcks = []
            if out_hbm is not None:
                dck = P.dma("sync", out_hbm[oc * 128:(oc + 1) * 128,
                                            ts * 512:(ts + 1) * 512],
                            src_ap, waits=(ck,))
                all_stores.append(dck)
                dcks.append(dck)
            if i == 2:
                for j in range(4):
                    k = state["si"] % 2
                    state["si"] += 1
                    ew = (ck,) + ((tst_ck[k],) if tst_ck[k] else ())
                    eck = tp.t(src_ap[:, j * 128:(j + 1) * 128],
                               tst[k][:], waits=ew)
                    tst_ck[k] = P.dma("sync",
                                      vt_hbm[ts * 512 + j * 128:
                                             ts * 512 + (j + 1) * 128,
                                             oc * 128:(oc + 1) * 128],
                                      tst[k][:], waits=(eck,))
                    all_stores.append(tst_ck[k])
                    dcks.append(tst_ck[k])
            return tuple(dcks)

        prev_proj = emit_proj_rowsD(P, f"eb_p{i}", xhatT, xt_ck, w_tiles,
                                    tuple(w_cks), bias_col, bias_ck, out_cb,
                                    scale=scale)
    P.wait("sync", tuple(all_stores))


def emit_l1attnB_2(P, consts, tp, q_hbm, kf_hbm, vtf_hbm, wo_hbm, bo_hbm,
                   xres_in, xres_out):
    qh = [P.sbuf(f"e2_q{h}", [HS, T], BF16) for h in range(NH)]
    q_cks = [P.dma("sync", qh[h][:], q_hbm[h * HS:(h + 1) * HS, :])
             for h in range(NH)]
    kaug = [P.sbuf(f"e2_ka{i}", [66, LF], BF16) for i in range(2)]
    ka_ck = [None, None]
    ka_used = [None, None]
    neg_init = []
    for i in range(2):
        P.op("vector", (lambda i=i: lambda e: e.memset(
            kaug[i][64:66, :], -1.0))())
        neg_init.append(P.ck("vector"))
    vth = [P.sbuf(f"e2_vt{i}", [128, NKC * HS], BF16) for i in range(2)]
    vt_ck = [None, None]
    vt_used = [None, None]
    vaug = P.sbuf("e2_va", [128, NKC * (HS + 1)], BF16)
    va3 = vaug[:].rearrange("p (k f) -> p k f", f=HS + 1)
    S1 = P.sbuf("e2_S1", [128, LF], F32)
    S2 = P.sbuf("e2_S2", [128, LF], F32)
    m8a = P.sbuf("e2_m8", [128, 8], F32)
    tsp = P.sbuf("e2_tsp", [128, 2], BF16)
    tf = P.sbuf("e2_tf", [128, 1], F32)
    qaug = P.sbuf("e2_qa", [66, 512], BF16)
    em = P.sbuf("e2_em", [128, 512], BF16)
    pmat = P.sbuf("e2_pm", [128, 512], BF16)
    rb32 = P.sbuf("e2_rb32", [1, 512], F32)
    rb = P.sbuf("e2_rb", [1, 512], BF16)
    dnr = P.sbuf("e2_dnr", [128, 512], BF16)
    ones1 = P.sbuf("e2_o1", [1, 64], BF16)
    P.op("vector", lambda e: e.memset(ones1[:], 1.0))
    ones_ck = P.ck("vector")
    attn = [P.sbuf(f"e2_at{c}", [128, T], BF16) for c in range(NC)]
    ps_s = [P.psum(f"e2_pss{i}", [128, 512], F32) for i in range(4)]
    ps_pv = [P.psum(f"e2_pv{i}", [65, 512], F32) for i in range(2)]
    pss_used = [None] * 4
    pv_used = [None, None]
    prev_rmm = None
    for h in range(NH):
        hk = h % 2
        w = ((ka_used[hk],) if ka_used[hk] else ()) + (neg_init[hk],)
        ka_ck[hk] = P.dma("gpsimd", kaug[hk][0:64, :],
                          kf_hbm[h * HS:(h + 1) * HS, :], waits=w)
        w = ((vt_used[hk],) if vt_used[hk] else ())
        vt_ck[hk] = P.dma("gpsimd",
                          vth[hk][:].rearrange("p (k f) -> p k f", f=HS),
                          vtf_hbm.rearrange("(k p) f -> p k f", p=128)
                          [:, :, h * HS:(h + 1) * HS], waits=w)
        P.op("vector", (lambda hk=hk: lambda e: e.tensor_copy(
            va3[:, :, 0:HS],
            vth[hk][:].rearrange("p (k f) -> p k f", f=HS)))(),
            waits=(vt_ck[hk],) + tuple(c for c in pv_used if c))
        P.op("vector", lambda e: e.memset(va3[:, :, HS:HS + 1], 1.0))
        va_ck = P.ck("vector")
        vt_used[hk] = va_ck
        last_rmm_of_head = None
        for g in range(NG):
            gq0 = g * 512
            for j in range(4):
                t = g * 4 + j
                for sl in range(8):
                    k = sl % 4
                    mw = (q_cks[h], ka_ck[hk]) + \
                         ((pss_used[k],) if pss_used[k] else ())
                    mck = P.op("tensor", (lambda h=h, t=t, k=k, hk=hk, sl=sl:
                                          lambda e: e.matmul(
                        out=ps_s[k][:],
                        lhsT=qh[h][:, t * 128:(t + 1) * 128],
                        rhs=kaug[hk][0:64, sl * 512:(sl + 1) * 512],
                        start=True, stop=True))(), waits=mw)
                    P.op("scalar", (lambda sl=sl, k=k: lambda e: e.activation(
                        out=S1[:, sl * 512:(sl + 1) * 512], in_=ps_s[k][:],
                        func=AF.Copy))(),
                        waits=(mck, Ck("@vector", P.cnt["@vector"])))
                    pss_used[k] = P.ck("scalar")
                s_ck = P.ck("scalar")
                src, dst = S1, S2
                rr_ck = s_ck
                for r in range(8):
                    P.op("vector", (lambda src=src: lambda e: e.max(
                        out=m8a[:], in_=src[:]))(), waits=(rr_ck,))
                    if r < 7:
                        P.op("vector", (lambda src=src, dst=dst:
                                        lambda e: e.match_replace(
                            out=dst[:], in_to_replace=m8a[:],
                            in_values=src[:], imm_value=-3e38))(),
                            waits=(P.ck("vector"),))
                        rr_ck = P.ck("vector")
                        src, dst = dst, src
                P.op("vector", lambda e: e.tensor_copy(
                    tsp[:, 0:1], m8a[:, 7:8]), waits=(P.ck("vector"),))
                P.op("vector", lambda e: e.tensor_tensor(
                    out=tf[:], in0=m8a[:, 7:8], in1=tsp[:, 0:1],
                    op=ALU.subtract), waits=(P.ck("vector"),))
                P.op("vector", lambda e: e.tensor_copy(
                    tsp[:, 1:2], tf[:]), waits=(P.ck("vector"),))
                tsp_ck = P.ck("vector")
                qaw = (tsp_ck,) + ((prev_rmm,) if prev_rmm and j == 0 else ())
                tp.t(tsp[:], qaug[64:66, j * 128:(j + 1) * 128], waits=qaw)
            P.op("vector", (lambda h=h, gq0=gq0: lambda e: e.tensor_copy(
                qaug[0:64, :], qh[h][:, gq0:gq0 + 512]))(),
                waits=(q_cks[h],) + ((prev_rmm,) if prev_rmm else ()))
            qa_ck = (P.ck("vector"), P.ck("scalar"))
            kpv = g % 2
            pvw = ((pv_used[kpv],) if pv_used[kpv] else ())
            last_pv = None
            for kc in range(NKC):
                k2 = kc % 2
                mw = qa_ck + (ka_ck[hk],) + \
                     ((pss_used[k2],) if pss_used[k2] else ())
                rmck = P.op("tensor", (lambda hk=hk, kc=kc, k2=k2:
                                       lambda e: e.matmul(
                    out=ps_s[k2][:],
                    lhsT=kaug[hk][0:66, kc * 128:(kc + 1) * 128],
                    rhs=qaug[:], start=True, stop=True))(), waits=mw)
                P.op("scalar", (lambda k2=k2: lambda e: e.activation(
                    out=em[:], in_=ps_s[k2][:], func=AF.Exp))(),
                    waits=(rmck, Ck("@vector", P.cnt["@vector"])))
                P.op("vector", (lambda k2=k2: lambda e:
                                e.scalar_tensor_tensor(
                    out=pmat[:], in0=ps_s[k2][:], scalar=0.0,
                    in1=em[:], op0=ALU.is_ge, op1=ALU.mult))(),
                    waits=(P.ck("scalar"),) +
                          ((last_pv,) if last_pv else ()))
                pm_ck = P.ck("vector")
                pss_used[k2] = pm_ck
                pvw2 = (pm_ck, va_ck) + (pvw if kc == 0 else ())
                last_pv = P.op("tensor", (lambda kc=kc, kpv=kpv:
                                          lambda e: e.matmul(
                    out=ps_pv[kpv][:], lhsT=va3[:, kc, :], rhs=pmat[:],
                    start=(kc == 0), stop=(kc == NKC - 1)))(),
                    waits=pvw2, inc=1)
                prev_rmm = last_pv
            last_rmm_of_head = last_pv
            P.op("vector", (lambda kpv=kpv: lambda e: e.reciprocal(
                out=rb32[:], in_=ps_pv[kpv][64:65, :]))(), waits=(last_pv,))
            P.op("vector", lambda e: e.tensor_copy(rb[:], rb32[:]),
                 waits=(P.ck("vector"),))
            rb_ck = P.ck("vector")
            bcw = (rb_ck, ones_ck) + \
                  ((pss_used[0],) if pss_used[0] else ())
            bcck = P.op("tensor", (lambda: lambda e: e.matmul(
                out=ps_s[0][0:64, :], lhsT=ones1[:], rhs=rb[:],
                start=True, stop=True))(), waits=bcw)
            P.op("vector", lambda e: e.tensor_copy(
                dnr[0:64, :], ps_s[0][0:64, :]), waits=(bcck,))
            dnr_ck = P.ck("vector")
            pss_used[0] = dnr_ck
            hb = (h % 2) * 64
            P.op("vector", (lambda h=h, hb=hb, gq0=gq0, kpv=kpv:
                            lambda e: e.tensor_tensor(
                out=attn[h // 2][hb:hb + 64, gq0:gq0 + 512],
                in0=ps_pv[kpv][0:64, :], in1=dnr[0:64, :],
                op=ALU.mult))(), waits=(dnr_ck,))
            pv_used[kpv] = P.ck("vector")
        ka_used[hk] = last_rmm_of_head
    attn_ck = (P.ck("vector"),)

    wo_t = [P.sbuf(f"e2_wo{c}", [128, D], BF16) for c in range(NC)]
    wo_cks = [P.dma("gpsimd", wo_t[c][:], wo_hbm[c * 128:(c + 1) * 128, :])
              for c in range(NC)]
    bo_bc, bo_ck = bcast_row(P, consts, bo_hbm, D, "e2_bo", dtype=F32,
                             ps=ps_s[0][:],
                             ps_free=(pss_used[0],) + attn_ck)
    x_s = [P.sbuf(f"e2_x{i}", [128, D], F32) for i in range(2)]
    x_used = [None, None]
    psf_used = [bo_ck, attn_ck[0]]
    for t in range(NT):
        k = t % 2
        xck = P.dma("sync", x_s[k][:], xres_in[t * 128:(t + 1) * 128, :],
                    waits=((x_used[k],) if x_used[k] else ()))
        for half in range(2):
            mw = tuple(wo_cks) + attn_ck + \
                 ((psf_used[half],) if psf_used[half] else ())
            mck = None
            for c in range(NC):
                mck = P.op("tensor", (lambda c=c, t=t, half=half:
                                      lambda e: e.matmul(
                    out=ps_s[half][:],
                    lhsT=attn[c][:, t * 128:(t + 1) * 128],
                    rhs=wo_t[c][:, half * 512:(half + 1) * 512],
                    start=(c == 0), stop=(c == NC - 1)))(),
                    waits=mw if c == 0 else (), inc=(1 if c == NC - 1 else 0))
            P.op("vector", (lambda k=k, half=half: lambda e: e.tensor_tensor(
                out=x_s[k][:, half * 512:(half + 1) * 512],
                in0=x_s[k][:, half * 512:(half + 1) * 512],
                in1=ps_s[half][:], op=ALU.add))(), waits=(mck, xck))
            P.op("vector", (lambda k=k, half=half: lambda e: e.tensor_tensor(
                out=x_s[k][:, half * 512:(half + 1) * 512],
                in0=x_s[k][:, half * 512:(half + 1) * 512],
                in1=bo_bc[:, half * 512:(half + 1) * 512], op=ALU.add))(),
                waits=(bo_ck,))
            psf_used[half] = P.ck("vector")
        x_used[k] = P.dma("sync", xres_out[t * 128:(t + 1) * 128, :],
                          x_s[k][:], waits=(P.ck("vector"),))
    P.wait("sync", (x_used[0], x_used[1]))


import jax

_BUILT = {}


def _get_prog(name, fn):
    if name not in _BUILT:
        _BUILT[name] = fn()
    return _BUILT[name]


def _tonp(a):
    return np.asarray(jax.device_get(a))


def _shard_x(x, c):
    b, p = c // 2, c % 2
    blocks = [x[b, j * 512:(j + 1) * 512] for j in range(8) if j % 2 == p]
    return np.concatenate(blocks, axis=0)


def _unshard(parts):
    out = np.empty((4, 4096, 1024), np.float32)
    for b in range(4):
        for j in range(8):
            p, i = j % 2, j // 2
            out[b, j * 512:(j + 1) * 512] = parts[2 * b + p][i * 512:(i + 1) * 512]
    return out


def _interleave_full(parts, blk=512):
    out = []
    for j in range(8):
        p, i = j % 2, j // 2
        out.append(parts[p][..., i * blk:(i + 1) * blk])
    return np.concatenate(out, axis=-1)


def _interleave_lm(l0, l1):
    out = np.zeros((l0.shape[0], 64), l0.dtype)
    for p, l in ((0, l0), (1, l1)):
        for j in range(4):
            g = (p + 2 * j) * 8
            out[:, g:g + 8] = l[:, j * 8:(j + 1) * 8]
    return out


def _kb_a1():
    P = Prog()
    consts = Consts(P)
    tp = Transposer(P, consts)
    x_in = P.dram("x_in", [T, D], F32, kind="ExternalInput")
    wk = P.dram("wk", [D, D], F32, kind="ExternalInput")
    wq = P.dram("wq", [D, D], F32, kind="ExternalInput")
    wv = P.dram("wv", [D, D], F32, kind="ExternalInput")
    outs = {n: P.dram(f"o{n}", [D, T], BF16, kind="ExternalOutput")
            for n in "qkv"}
    ql = P.dram("oql", [D, ML], F32, kind="ExternalOutput")
    kl = P.dram("okl", [D, ML], F32, kind="ExternalOutput")
    emit_l0attnA_1(P, consts, tp, wk, wq, wv, x_in,
                   outs["q"], outs["k"], outs["v"], ql, kl)
    return P.build()


def _kb_a2():
    P = Prog()
    consts = Consts(P)
    tp = Transposer(P, consts)
    q = P.dram("q", [D, T], BF16, kind="ExternalInput")
    k = P.dram("k", [D, T], BF16, kind="ExternalInput")
    v = P.dram("v", [D, T], BF16, kind="ExternalInput")
    qlg = P.dram("qlg", [D, M], F32, kind="ExternalInput")
    klg = P.dram("klg", [D, M], F32, kind="ExternalInput")
    idt = P.dram("idt", [M, HT], BF16, kind="ExternalInput")
    s1T = P.dram("s1T", [D, T], BF16, kind="ExternalOutput")
    zsT = P.dram("zsT", [M, HT], F32, kind="ExternalOutput")
    mzs = P.dram("mzs", [M, NH * 66], F32, kind="ExternalOutput")
    emit_l0attnA_2(P, consts, tp, q, k, v, qlg, klg, idt, s1T, zsT, mzs)
    return P.build()


def _kb_a3():
    P = Prog()
    consts = Consts(P)
    s1T = P.dram("s1T", [D, T], BF16, kind="ExternalInput")
    zsT = P.dram("zsT", [M, HT], F32, kind="ExternalInput")
    mzo = P.dram("mzo", [M, NH * 66], F32, kind="ExternalInput")
    mzp = P.dram("mzp", [M, NH * 66], F32, kind="ExternalInput")
    wo = P.dram("wo", [D, D], F32, kind="ExternalInput")
    bo = P.dram("bo", [D], F32, kind="ExternalInput")
    x_in = P.dram("x_in", [T, D], F32, kind="ExternalInput")
    x_out = P.dram("x_out", [T, D], F32, kind="ExternalOutput")
    emit_l0attnA_3(P, None, None, s1T, zsT, mzo, mzp, wo, bo, x_in, x_out)
    return P.build()


def _kb_f():
    P = Prog()
    consts = Consts(P, f16=True)
    tp = Transposer(P, consts, dtype=F16)
    x_in = P.dram("x_in", [T, D], F32, kind="ExternalInput")
    wq = P.dram("wq", [D, 2 * 8 * 512], F32, kind="ExternalInput")
    keys = P.dram("keys", [8, 256, 2, 512], F32, kind="ExternalInput")
    wdn = P.dram("wdn", [NEXP, D], F32, kind="ExternalInput")
    wup = P.dram("wup", [NEXP, D], F32, kind="ExternalInput")
    x_out = P.dram("x_out", [T, D], F32, kind="ExternalOutput")
    emit_l0ffn(P, consts, tp, wq, keys, wdn, wup, x_in, x_out)
    return P.build()


def _kb_b1():
    P = Prog()
    consts = Consts(P)
    tp = Transposer(P, consts)
    x_in = P.dram("x_in", [T, D], F32, kind="ExternalInput")
    wqkv = P.dram("wqkv", [O3, D], F32, kind="ExternalInput")
    bqkv = P.dram("bqkv", [O3], F32, kind="ExternalInput")
    gq = P.dram("gq", [D], F32, kind="ExternalInput")
    gk = P.dram("gk", [D], F32, kind="ExternalInput")
    cos = P.dram("cos", [T, 512], F32, kind="ExternalInput")
    sin = P.dram("sin", [T, 512], F32, kind="ExternalInput")
    qfT = P.dram("qfT", [D, T], BF16, kind="ExternalOutput")
    vk = P.dram("vk", [64, 16 * 65], F32, kind="ExternalOutput")
    emit_l0attnB_1(P, consts, tp, wqkv, bqkv, gq, gk, cos, sin, x_in,
                   qfT, vk)
    return P.build()


def _kb_b2():
    P = Prog()
    consts = Consts(P)
    tp = Transposer(P, consts)
    qfT = P.dram("qfT", [D, T], BF16, kind="ExternalInput")
    vko = P.dram("vko", [64, 16 * 65], F32, kind="ExternalInput")
    vkp = P.dram("vkp", [64, 16 * 65], F32, kind="ExternalInput")
    wout = P.dram("wout", [D, D], F32, kind="ExternalInput")
    bout = P.dram("bout", [D], F32, kind="ExternalInput")
    gout = P.dram("gout", [D], F32, kind="ExternalInput")
    x_in = P.dram("x_in", [T, D], F32, kind="ExternalInput")
    x_out = P.dram("x_out", [T, D], F32, kind="ExternalOutput")
    emit_l0attnB_2(P, consts, tp, qfT, vko, vkp, wout, bout, gout,
                   x_in, x_out)
    return P.build()


def _kb_c1():
    P = Prog()
    consts = Consts(P)
    tp = Transposer(P, consts)
    x_in = P.dram("x_in", [T, D], F32, kind="ExternalInput")
    ws = [P.dram(f"w{n}", [D, D], F32, kind="ExternalInput") for n in "qkv"]
    bs = [P.dram(f"b{n}", [D], F32, kind="ExternalInput") for n in "qkv"]
    outs = [P.dram(f"o{n}", [D, T], BF16, kind="ExternalOutput") for n in "qkv"]
    vt = P.dram("ovt", [T, D], BF16, kind="ExternalOutput")
    emit_qkv_heads(P, consts, tp, x_in, ws, bs, outs, vt_idx=2, vt_hbm=vt)
    return P.build()


def _kb_c2():
    P = Prog()
    consts = Consts(P)
    tp = Transposer(P, consts)
    q = P.dram("q", [D, T], BF16, kind="ExternalInput")
    k = P.dram("k", [D, LFULL], BF16, kind="ExternalInput")
    vt = P.dram("vt", [LFULL, D], BF16, kind="ExternalInput")
    cm = P.dram("cm", [128, 2048], BF16, kind="ExternalInput")
    wo = P.dram("wo", [D, D], F32, kind="ExternalInput")
    wob = P.dram("wob", [D], F32, kind="ExternalInput")
    x_in = P.dram("x_in", [T, D], F32, kind="ExternalInput")
    x_out = P.dram("x_out", [T, D], F32, kind="ExternalOutput")
    emit_l1attnA_core(P, consts, tp, 1, q, k, vt, cm, wo, wob,
                      x_in, x_out)
    return P.build()


def _kb_dw():
    P = Prog()
    consts = Consts(P)
    tp = Transposer(P, consts)
    w_gate = P.dram("w_gate", [I_FF // 8, D], F32, kind="ExternalInput")
    w_up = P.dram("w_up", [I_FF // 8, D], F32, kind="ExternalInput")
    w_down = P.dram("w_down", [D // 8, I_FF], F32, kind="ExternalInput")
    o_gT = P.dram("o_gT", [D, I_FF // 8], BF16, kind="ExternalOutput")
    o_uT = P.dram("o_uT", [D, I_FF // 8], BF16, kind="ExternalOutput")
    o_dT = P.dram("o_dT", [I_FF, D // 8], BF16, kind="ExternalOutput")
    emit_l1ffn_wprep(P, consts, tp, w_gate, w_up, w_down, o_gT, o_uT, o_dT,
                     I_FF // 8, D // 8)
    return P.build()


def _kb_d():
    P = Prog()
    consts = Consts(P)
    tp = Transposer(P, consts)
    x_in = P.dram("x_in", [T, D], F32, kind="ExternalInput")
    wgT = P.dram("wgT", [D, I_FF], BF16, kind="ExternalInput")
    wuT = P.dram("wuT", [D, I_FF], BF16, kind="ExternalInput")
    wdT = P.dram("wdT", [I_FF, D], BF16, kind="ExternalInput")
    x_out = P.dram("x_out", [T, D], F32, kind="ExternalOutput")
    emit_l1ffn(P, consts, tp, x_in, wgT, wuT, wdT, x_out)
    return P.build()


def _kb_e1():
    P = Prog()
    consts = Consts(P)
    tp = Transposer(P, consts)
    x_in = P.dram("x_in", [T, D], F32, kind="ExternalInput")
    ws = [P.dram(f"w{n}", [D, D], F32, kind="ExternalInput") for n in "qkv"]
    bs = [P.dram(f"b{n}", [D], F32, kind="ExternalInput") for n in "qkv"]
    oq = P.dram("oq", [D, T], BF16, kind="ExternalOutput")
    ok = P.dram("ok", [D, T], BF16, kind="ExternalOutput")
    vt = P.dram("ovt", [T, D], BF16, kind="ExternalOutput")
    emit_l1attnB_1(P, consts, tp, ws, bs, [oq, ok, None], vt, x_in)
    return P.build()


def _kb_e2():
    P = Prog()
    consts = Consts(P)
    tp = Transposer(P, consts)
    q = P.dram("q", [D, T], BF16, kind="ExternalInput")
    kf = P.dram("kf", [D, LF], BF16, kind="ExternalInput")
    vtf = P.dram("vtf", [LF, D], BF16, kind="ExternalInput")
    wo = P.dram("wo", [D, D], F32, kind="ExternalInput")
    bo = P.dram("bo", [D], F32, kind="ExternalInput")
    x_in = P.dram("x_in", [T, D], F32, kind="ExternalInput")
    x_out = P.dram("x_out", [T, D], F32, kind="ExternalOutput")
    emit_l1attnB_2(P, consts, tp, q, kf, vtf, wo, bo, x_in, x_out)
    return P.build()


def kernel(**inp):
    x = np.asarray(inp["L0_attn_a_x"], np.float32)
    ch = Consts.host_arrays()
    idt = ident_tiled_host()
    cmask = causal_masks_host()
    tabs = [rope_tables_host(p) for p in range(2)]
    cores = list(range(8))
    xs = [_shard_x(x, c) for c in cores]

    nc = _get_prog("a1", _kb_a1)
    ins = [{"x_in": xs[c], "wk": inp["L0_attn_a_Wk"],
            "wq": inp["L0_attn_a_Wq"], "wv": inp["L0_attn_a_Wv"], **ch}
           for c in cores]
    r1 = run_bass_kernel_spmd(nc, ins, core_ids=cores)

    nc = _get_prog("a2", _kb_a2)
    ins = []
    for c in cores:
        b = c // 2
        pair = [2 * b, 2 * b + 1]
        qlg = _interleave_lm(_tonp(r1.results[pair[0]]["oql"]),
                             _tonp(r1.results[pair[1]]["oql"]))
        klg = _interleave_lm(_tonp(r1.results[pair[0]]["okl"]),
                             _tonp(r1.results[pair[1]]["okl"]))
        ins.append({"q": _tonp(r1.results[c]["oq"]),
                    "k": _tonp(r1.results[c]["ok"]),
                    "v": _tonp(r1.results[c]["ov"]),
                    "qlg": qlg, "klg": klg, "idt": idt, **ch})
    r2 = run_bass_kernel_spmd(nc, ins, core_ids=cores)

    nc = _get_prog("a3", _kb_a3)
    ins = []
    for c in cores:
        pair = c + 1 if c % 2 == 0 else c - 1
        ins.append({"s1T": _tonp(r2.results[c]["s1T"]),
                    "zsT": _tonp(r2.results[c]["zsT"]),
                    "mzo": _tonp(r2.results[c]["mzs"]),
                    "mzp": _tonp(r2.results[pair]["mzs"]),
                    "wo": inp["L0_attn_a_Wo"], "bo": inp["L0_attn_a_bo"],
                    "x_in": xs[c], **ch})
    r3 = run_bass_kernel_spmd(nc, ins, core_ids=cores)
    xs = [_tonp(r3.results[c]["x_out"]) for c in cores]

    nc = _get_prog("f", _kb_f)
    ins = [{"x_in": xs[c], "wq": inp["L0_ffn_w_q"],
            "keys": inp["L0_ffn_keys"], "wdn": inp["L0_ffn_w_down"],
            "wup": inp["L0_ffn_w_up"], **ch} for c in cores]
    rf = run_bass_kernel_spmd(nc, ins, core_ids=cores)
    xs = [_tonp(rf.results[c]["x_out"]) for c in cores]

    nc = _get_prog("b1", _kb_b1)
    ins = [{"x_in": xs[c], "wqkv": inp["L0_attn_b_Wqkv"],
            "bqkv": inp["L0_attn_b_bqkv"], "gq": inp["L0_attn_b_gq"],
            "gk": inp["L0_attn_b_gk"], "cos": tabs[c % 2][0],
            "sin": tabs[c % 2][1], **ch} for c in cores]
    rb1 = run_bass_kernel_spmd(nc, ins, core_ids=cores)

    nc = _get_prog("b2", _kb_b2)
    ins = []
    for c in cores:
        pair = c + 1 if c % 2 == 0 else c - 1
        ins.append({"qfT": _tonp(rb1.results[c]["qfT"]),
                    "vko": _tonp(rb1.results[c]["vk"]),
                    "vkp": _tonp(rb1.results[pair]["vk"]),
                    "wout": inp["L0_attn_b_Wout"],
                    "bout": inp["L0_attn_b_bout"],
                    "gout": inp["L0_attn_b_gout"],
                    "x_in": xs[c], **ch})
    rb2 = run_bass_kernel_spmd(nc, ins, core_ids=cores)
    xs = [_tonp(rb2.results[c]["x_out"]) for c in cores]

    nc = _get_prog("f", _kb_f)
    ins = [{"x_in": xs[c], "wq": inp["L0_ffn_w_q"],
            "keys": inp["L0_ffn_keys"], "wdn": inp["L0_ffn_w_down"],
            "wup": inp["L0_ffn_w_up"], **ch} for c in cores]
    rf = run_bass_kernel_spmd(nc, ins, core_ids=cores)
    xs = [_tonp(rf.results[c]["x_out"]) for c in cores]

    nc = _get_prog("c1", _kb_c1)
    ins = [{"x_in": xs[c], "wq": inp["L1_attn_a_wq_w"],
            "wk": inp["L1_attn_a_wk_w"], "wv": inp["L1_attn_a_wv_w"],
            "bq": inp["L1_attn_a_wq_b"], "bk": inp["L1_attn_a_wk_b"],
            "bv": inp["L1_attn_a_wv_b"], **ch} for c in cores]
    rc1 = run_bass_kernel_spmd(nc, ins, core_ids=cores)

    nc = _get_prog("c2", _kb_c2)
    ins = []
    for c in cores:
        b, p = c // 2, c % 2
        pair = [2 * b, 2 * b + 1]
        k_full = _interleave_full([_tonp(rc1.results[pair[0]]["ok"]),
                                   _tonp(rc1.results[pair[1]]["ok"])])
        vt_full = _interleave_full(
            [_tonp(rc1.results[pair[0]]["ovt"]).T,
             _tonp(rc1.results[pair[1]]["ovt"]).T]).T
        vt_full = np.ascontiguousarray(vt_full)
        if p == 0:
            k_full = np.concatenate(
                [np.zeros_like(k_full[:, :512]), k_full[:, :3584]], axis=1)
            vt_full = np.concatenate(
                [np.zeros_like(vt_full[:512]), vt_full[:3584]], axis=0)
        ins.append({"q": _tonp(rc1.results[c]["oq"]), "k": k_full,
                    "vt": vt_full, "cm": cmask,
                    "wo": inp["L1_attn_a_wo_w"], "wob": inp["L1_attn_a_wo_b"],
                    "x_in": xs[c], **ch})
    rc2 = run_bass_kernel_spmd(nc, ins, core_ids=cores)
    xs = [_tonp(rc2.results[c]["x_out"]) for c in cores]

    nc = _get_prog("dw", _kb_dw)
    ins = []
    for c in cores:
        ins.append({"w_gate": inp["L1_ffn_w_gate"][c * 352:(c + 1) * 352],
                    "w_up": inp["L1_ffn_w_up"][c * 352:(c + 1) * 352],
                    "w_down": inp["L1_ffn_w_down"][c * 128:(c + 1) * 128],
                    **ch})
    rw = run_bass_kernel_spmd(nc, ins, core_ids=cores)
    wgT = np.concatenate([_tonp(rw.results[c]["o_gT"]) for c in cores], axis=1)
    wuT = np.concatenate([_tonp(rw.results[c]["o_uT"]) for c in cores], axis=1)
    wdT = np.concatenate([_tonp(rw.results[c]["o_dT"]) for c in cores], axis=1)

    nc = _get_prog("d", _kb_d)
    ins = [{"x_in": xs[c], "wgT": wgT, "wuT": wuT, "wdT": wdT, **ch}
           for c in cores]
    rd = run_bass_kernel_spmd(nc, ins, core_ids=cores)
    xs = [_tonp(rd.results[c]["x_out"]) for c in cores]

    nc = _get_prog("e1", _kb_e1)
    ins = [{"x_in": xs[c], "wq": inp["L1_attn_b_Wq"],
            "wk": inp["L1_attn_b_Wk"], "wv": inp["L1_attn_b_Wv"],
            "bq": inp["L1_attn_b_bq"], "bk": inp["L1_attn_b_bk"],
            "bv": inp["L1_attn_b_bv"], **ch} for c in cores]
    re1 = run_bass_kernel_spmd(nc, ins, core_ids=cores)

    nc = _get_prog("e2", _kb_e2)
    ins = []
    for c in cores:
        b = c // 2
        pair = [2 * b, 2 * b + 1]
        k_full = _interleave_full([_tonp(re1.results[pair[0]]["ok"]),
                                   _tonp(re1.results[pair[1]]["ok"])])
        vt_full = _interleave_full(
            [_tonp(re1.results[pair[0]]["ovt"]).T,
             _tonp(re1.results[pair[1]]["ovt"]).T]).T
        vt_full = np.ascontiguousarray(vt_full)
        ins.append({"q": _tonp(re1.results[c]["oq"]), "kf": k_full,
                    "vtf": vt_full, "wo": inp["L1_attn_b_Wo"],
                    "bo": inp["L1_attn_b_bo"], "x_in": xs[c], **ch})
    re2 = run_bass_kernel_spmd(nc, ins, core_ids=cores)
    xs = [_tonp(re2.results[c]["x_out"]) for c in cores]

    nc = _get_prog("d", _kb_d)
    ins = [{"x_in": xs[c], "wgT": wgT, "wuT": wuT, "wdT": wdT, **ch}
           for c in cores]
    rd = run_bass_kernel_spmd(nc, ins, core_ids=cores)
    xs = [_tonp(rd.results[c]["x_out"]) for c in cores]

    return _unshard(xs)


# version 03 (final)
# speedup vs baseline: 1.1182x; 1.1182x over previous
import numpy as np
from contextlib import ExitStack
import concourse.bass as bass
import concourse.mybir as mybir
from concourse.bass_utils import run_bass_kernel_spmd
from concourse.alu_op_type import AluOpType as ALU

F32 = mybir.dt.float32
BF16 = mybir.dt.bfloat16
F16 = mybir.dt.float16
U32 = mybir.dt.uint32
I32 = mybir.dt.int32
I16 = mybir.dt.int16
AF = mybir.ActivationFunctionType
AX = mybir.AxisListType

ENGINES = ("sync", "vector", "scalar", "gpsimd", "tensor")


class Ck:
    __slots__ = ("sem", "val")

    def __init__(self, sem, val):
        self.sem = sem
        self.val = val


class Prog:
    def __init__(self):
        self.nc = bass.Bass(target_bir_lowering=False)
        self.ctx = ExitStack()
        self.records = {e: [] for e in ENGINES}
        self.cnt = {}
        self.sem_names = []
        self.watermark = {e: {} for e in ENGINES}
        self._built = False
        self._sbuf_bytes = 0
        for e in ENGINES:
            self._new_sem("@" + e)

    def _new_sem(self, name):
        if name not in self.cnt:
            self.cnt[name] = 0
            self.sem_names.append(name)
        return name

    def dram(self, name, shape, dtype, kind="Internal", **kw):
        return self.nc.dram_tensor(name, list(shape), dtype, kind=kind, **kw).ap()

    def sbuf(self, name, shape, dtype):
        t = self.ctx.enter_context(self.nc.sbuf_tensor(name, list(shape), dtype))
        free_elems = 1
        for s in shape[1:]:
            free_elems *= s
        self._sbuf_bytes += free_elems * mybir.dt.size(dtype)
        return t

    def psum(self, name, shape, dtype=F32):
        return self.ctx.enter_context(self.nc.psum_tensor(name, list(shape), dtype))

    def wait(self, eng, waits):
        wm = self.watermark[eng]
        best = {}
        for ck in waits:
            if ck is None:
                continue
            if ck.sem not in best or ck.val > best[ck.sem].val:
                best[ck.sem] = ck
        for sem, ck in best.items():
            if wm.get(sem, -1) >= ck.val:
                continue
            wm[sem] = ck.val
            self.records[eng].append(("wait", sem, ck.val))

    SERIAL = ("vector", "scalar", "gpsimd")

    def op(self, eng, fn, waits=(), inc=1, stream=None, _is_dma=False):
        w = tuple(waits)
        if not _is_dma and eng in self.SERIAL:
            assert inc or stream, "serialized engines need inc>0 compute ops"
            w = w + (Ck("@" + eng, self.cnt["@" + eng]),)
        self.wait(eng, w)
        sem = "@" + eng if stream is None else self._new_sem(stream)
        self.records[eng].append(("op", fn, inc, sem))
        if inc:
            self.cnt[sem] += inc
            return Ck(sem, self.cnt[sem])
        return None

    def dma(self, queue, out, in_, waits=(), stream=None, **kw):
        if stream is None:
            tname = None
            for ap in (out, in_):
                t = getattr(ap, "tensor", None)
                if t is not None and getattr(t, "name", None) is not None:
                    try:
                        sp = ap.space
                    except Exception:
                        sp = None
                    if str(sp).endswith("SBUF"):
                        tname = t.name
                        break
            if tname is None:
                tname = out.tensor.name
            stream = "dma_" + tname
        return self.op(queue, lambda e: e.dma_start(out=out, in_=in_, **kw),
                       waits, inc=16, stream=stream, _is_dma=True)

    def ck(self, eng_or_sem):
        sem = "@" + eng_or_sem if eng_or_sem in ENGINES else eng_or_sem
        return Ck(sem, self.cnt[sem])

    def ccol(self, val, dtype=F32):
        key = (float(val), dtype)
        if not hasattr(self, "_ccols"):
            self._ccols = {}
        if key not in self._ccols:
            t = self.sbuf(f"ccol{len(self._ccols)}", [128, 1], dtype)
            self.op("vector", (lambda t=t, val=val: lambda e: e.memset(
                t[:], float(val)))())
            self._ccols[key] = (t, self.ck("vector"))
        return self._ccols[key]

    def build(self):
        assert not self._built
        self._built = True
        assert len(self.sem_names) <= 100, \
            f"too many semaphores: {len(self.sem_names)}"
        with ExitStack() as sctx:
            sems = {}
            for name in self.sem_names:
                sems[name] = sctx.enter_context(self.nc.semaphore(name))
            block = sctx.enter_context(self.nc.Block())
            for e in ENGINES:
                recs = self.records[e]
                if not recs:
                    continue
                dec = getattr(block, e)

                def make_body(recs_):
                    def body(engobj):
                        for r in recs_:
                            if r[0] == "wait":
                                engobj.wait_ge(sems[r[1]], r[2])
                            else:
                                _, fn, inc, sem = r
                                inst = fn(engobj)
                                if inc:
                                    assert inst is not None
                                    inst.then_inc(sems[sem], inc)
                    return body

                dec(make_body(recs))
        return self.nc


def ceil_div(a, b):
    return (a + b - 1) // b


"""Layer emitters for the megakernel. All emit onto a Prog (mkcore)."""
import numpy as np

T = 2048
D = 1024
NT = T // 128
NC = D // 128


class Consts:

    @staticmethod
    def host_arrays():
        import numpy as _np
        import ml_dtypes
        zlo = _np.zeros((128, 128), _np.float32)
        for i in range(64):
            zlo[64 + i, i] = 1.0
        return {"c_ident": _np.eye(128, dtype=ml_dtypes.bfloat16),
                "c_identh": _np.eye(128, dtype=_np.float16),
                "c_identlo": zlo.astype(ml_dtypes.bfloat16)}

    def __init__(self, P, f16=False):
        self.ident = P.sbuf("c_ident_s", [128, 128], BF16)
        self.ident_hbm = P.dram("c_ident", [128, 128], BF16, kind="ExternalInput")
        self.ck_ident = P.dma("sync", self.ident[:], self.ident_hbm)
        if f16:
            self.identh = P.sbuf("c_identh_s", [128, 128], F16)
            self.identh_hbm = P.dram("c_identh", [128, 128], F16, kind="ExternalInput")
            self.ck_identh = P.dma("sync", self.identh[:], self.identh_hbm)
        self.identlo = None

    def load_identlo(self, P):
        if self.identlo is None:
            self.identlo = P.sbuf("c_identlo_s", [128, 128], BF16)
            self.identlo_hbm = P.dram("c_identlo", [128, 128], BF16,
                                      kind="ExternalInput")
            self.ck_identlo = P.dma("sync", self.identlo[:], self.identlo_hbm)
        return self.identlo, self.ck_identlo


def mm_accum(P, psum_ap, pairs, waits=(), tail_waits=()):
    n = len(pairs)
    ck = None
    for i, (l, r) in enumerate(pairs):
        w = tuple(waits) if i == 0 else ()
        if i == n - 1:
            w = w + tuple(tail_waits)
        ck = P.op("tensor",
                  (lambda l=l, r=r, i=i: lambda e: e.matmul(
                      out=psum_ap, lhsT=l, rhs=r, start=(i == 0), stop=(i == n - 1)))(),
                  waits=w, inc=(1 if i == n - 1 else 0))
    return ck


class Transposer:

    def __init__(self, P, consts, name="tp", dtype=BF16):
        self.P = P
        self.c = consts
        self.dtype = dtype
        self.ident = consts.identh if dtype == F16 else consts.ident
        self.ck_id = consts.ck_identh if dtype == F16 else consts.ck_ident
        self.ps = [P.psum(f"{name}_ps{i}", [128, 128], dtype) for i in range(2)]
        self.evac_ck = [None, None]
        self.i = 0

    def t(self, src_ap, dst_ap, waits=(), evac_eng="scalar"):
        P = self.P
        k = self.i % 2
        self.i += 1
        sp, sf = src_ap.shape[0], src_ap.shape[1]
        w = tuple(waits) + ((self.ck_id,) if self.i <= 2 else ())
        if self.evac_ck[k] is not None:
            w = w + (self.evac_ck[k],)
        try:
            base = src_ap.base_partition()
        except Exception:
            base = 0
        if base == 64:
            idl, idl_ck = self.c.load_identlo(P)
            ident_ap = idl[64:128, 0:sp]
            w = w + (idl_ck,)
        else:
            assert base == 0, f"transpose src base {base}"
            ident_ap = self.ident[0:sp, 0:sp]
        tck = P.op("tensor", (lambda k=k, src_ap=src_ap, sp=sp, sf=sf,
                              ident_ap=ident_ap: lambda e: e.transpose(
            out=self.ps[k][0:sf, 0:sp], in_=src_ap,
            identity=ident_ap))(), waits=w)
        if evac_eng == "scalar":
            eck = P.op("scalar", (lambda k=k, dst_ap=dst_ap, sp=sp, sf=sf:
                                  lambda e: e.activation(
                out=dst_ap, in_=self.ps[k][0:sf, 0:sp], func=AF.Copy))(),
                waits=(tck,))
        else:
            eck = P.op("vector", (lambda k=k, dst_ap=dst_ap, sp=sp, sf=sf:
                                  lambda e: e.tensor_copy(
                dst_ap, self.ps[k][0:sf, 0:sp]))(), waits=(tck,))
        self.evac_ck[k] = eck
        return eck


def transpose_tiles(P, tp, src_tiles, dst_chunks, waits_per_tile=None,
                    nt=NT, nch=NC):
    cks = []
    for c in range(nch):
        last = None
        for t in range(nt):
            w = ()
            if waits_per_tile is not None and waits_per_tile[t] is not None:
                w = (waits_per_tile[t],)
            last = tp.t(src_tiles[t][:, c * 128:(c + 1) * 128],
                        dst_chunks[c][:, t * 128:(t + 1) * 128], waits=w)
        cks.append(last)
    return cks


def emit_ln_stats_tile(P, x_ap, dump_ap, stat, t, waits=(), nt=NT):
    P.op("vector", lambda e: e.tensor_scalar(
        out=dump_ap, in0=x_ap, scalar1=1.0, scalar2=0.0,
        op0=ALU.mult, op1=ALU.add, accum_out=stat[:, t:t + 1]),
        waits=tuple(waits))
    P.op("vector", lambda e: e.scalar_tensor_tensor(
        out=dump_ap, in0=x_ap, scalar=1.0, in1=x_ap,
        op0=ALU.bypass, op1=ALU.mult, accum_out=stat[:, nt + t:nt + t + 1]))
    return P.ck("vector")


def emit_ln_finalize(P, stat, nt=NT, dvar=D, eps=1e-5):
    sumx = stat[:, 0:nt]
    sumx2 = stat[:, nt:2 * nt]
    mu = stat[:, 2 * nt:3 * nt]
    rstd = stat[:, 3 * nt:4 * nt]
    P.op("vector", lambda e: e.tensor_scalar(
        out=mu, in0=sumx, scalar1=1.0 / dvar, scalar2=None, op0=ALU.mult),
        waits=(P.ck("vector"),))
    P.op("vector", lambda e: e.tensor_tensor(
        out=rstd, in0=mu, in1=mu, op=ALU.mult), waits=(P.ck("vector"),))
    P.op("vector", lambda e: e.scalar_tensor_tensor(
        out=rstd, in0=sumx2, scalar=1.0 / dvar, in1=rstd,
        op0=ALU.mult, op1=ALU.subtract), waits=(P.ck("vector"),))
    epst, epsck = P.ccol(float(eps))
    P.op("scalar", lambda e: e.activation(
        out=rstd, in_=rstd, func=AF.Sqrt, bias=epst[:, 0:1], scale=1.0),
        waits=(P.ck("vector"), epsck))
    P.op("vector", lambda e: e.reciprocal(out=rstd, in_=rstd),
         waits=(P.ck("scalar"),))
    return P.ck("vector")


def ln_affine(P, x_ap, out_ap, stat, t, waits=(), nt=NT):
    mu = stat[:, 2 * nt + t:2 * nt + t + 1]
    rstd = stat[:, 3 * nt + t:3 * nt + t + 1]
    P.op("vector", lambda e: e.tensor_scalar(
        out=out_ap, in0=x_ap, scalar1=mu, scalar2=rstd,
        op0=ALU.subtract, op1=ALU.mult), waits=tuple(waits))
    return P.ck("vector")


def emit_ln(P, x_tiles, xhat_tiles, stat, x_cks=None, nt=NT, dvar=D, eps=1e-5):
    sumx = stat[:, 0:nt]
    sumx2 = stat[:, nt:2 * nt]
    mu = stat[:, 2 * nt:3 * nt]
    rstd = stat[:, 3 * nt:4 * nt]
    for t in range(nt):
        w = (x_cks[t],) if x_cks else ()
        P.op("vector", (lambda t=t: lambda e: e.tensor_scalar(
            out=xhat_tiles[t][:], in0=x_tiles[t][:], scalar1=1.0, scalar2=0.0,
            op0=ALU.mult, op1=ALU.add, accum_out=sumx[:, t:t + 1]))(), waits=w)
        P.op("vector", (lambda t=t: lambda e: e.scalar_tensor_tensor(
            out=xhat_tiles[t][:], in0=x_tiles[t][:], scalar=1.0, in1=x_tiles[t][:],
            op0=ALU.bypass, op1=ALU.mult, accum_out=sumx2[:, t:t + 1]))())
    ck0 = P.ck("vector")
    P.op("vector", lambda e: e.tensor_scalar(
        out=mu, in0=sumx, scalar1=1.0 / dvar, scalar2=None, op0=ALU.mult),
        waits=(ck0,))
    ckmu = P.ck("vector")
    P.op("vector", lambda e: e.tensor_tensor(
        out=rstd, in0=mu, in1=mu, op=ALU.mult), waits=(ckmu,))
    P.op("vector", lambda e: e.scalar_tensor_tensor(
        out=rstd, in0=sumx2, scalar=1.0 / dvar, in1=rstd,
        op0=ALU.mult, op1=ALU.subtract), waits=(P.ck("vector"),))
    ckv = P.ck("vector")
    P.op("scalar", lambda e: e.activation(
        out=rstd, in_=rstd, func=AF.Sqrt, bias=float(eps), scale=1.0),
        waits=(ckv,))
    cksq = P.ck("scalar")
    P.op("vector", lambda e: e.reciprocal(out=rstd, in_=rstd), waits=(cksq,))
    ckr = P.ck("vector")
    tile_cks = []
    for t in range(nt):
        P.op("vector", (lambda t=t: lambda e: e.tensor_scalar(
            out=xhat_tiles[t][:], in0=x_tiles[t][:],
            scalar1=mu[:, t:t + 1], scalar2=rstd[:, t:t + 1],
            op0=ALU.subtract, op1=ALU.mult))(), waits=(ckr,))
        tile_cks.append(P.ck("vector"))
    return tile_cks, P.ck("vector")


QUANT_MIDS = [0.25, 0.75, 1.25, 1.75, 2.5, 3.5, 5.0]


def emit_quant(P, src_ap, dst_ap, scr, width, waits=(), sign_eng="scalar"):
    nb = width // 16
    np_ = src_ap.shape[0]
    mx = scr["mx"][0:np_, 0:nb]
    inv = scr["inv"][0:np_, 0:nb]
    a = scr["a"][0:np_, 0:width]
    q = scr["q"][0:np_, 0:width]
    sgn = scr["sgn"][0:np_, 0:width]
    src3 = src_ap.rearrange("p (w c) -> p w c", c=16)
    P.op("vector", lambda e: e.tensor_reduce(
        out=mx, in_=src3, axis=AX.X, op=ALU.max,
        apply_absolute_value=True), waits=tuple(waits))
    P.op("vector", lambda e: e.tensor_scalar(
        out=mx, in0=mx, scalar1=1.0 / 6.0, scalar2=1e-12,
        op0=ALU.mult, op1=ALU.max), waits=(P.ck("vector"),))
    P.op("vector", lambda e: e.reciprocal(out=inv, in_=mx),
         waits=(P.ck("vector"),))
    ck_inv = P.ck("vector")
    P.op(sign_eng, lambda e: e.activation(out=sgn, in_=src_ap, func=AF.Sign),
         waits=tuple(waits))
    ck_sgn = P.ck(sign_eng)
    inv3 = inv.rearrange("p (w o) -> p w o", o=1)
    from concourse.bass import broadcast_tensor_aps
    xn3, invb = broadcast_tensor_aps(src3, inv3)
    P.op("vector", (lambda: lambda e: e.tensor_tensor(
        out=a.rearrange("p (w c) -> p w c", c=16), in0=xn3, in1=invb,
        op=ALU.mult))(), waits=(ck_inv,))
    P.op("vector", lambda e: e.scalar_tensor_tensor(
        out=a, in0=a, scalar=-1.0, in1=a, op0=ALU.mult, op1=ALU.max),
        waits=(P.ck("vector"),))
    cka = P.ck("vector")
    P.op("vector", lambda e: e.tensor_scalar(
        out=q, in0=a, scalar1=0.25, scalar2=None, op0=ALU.is_gt), waits=(cka,))
    for m in (0.75, 1.25, 1.75):
        P.op("vector", (lambda m=m: lambda e: e.scalar_tensor_tensor(
            out=q, in0=a, scalar=m, in1=q, op0=ALU.is_gt, op1=ALU.add))(),
            waits=(P.ck("vector"),))
    u = dst_ap
    P.op("vector", lambda e: e.tensor_scalar(
        out=u, in0=a, scalar1=2.5, scalar2=None, op0=ALU.is_gt), waits=(cka,))
    for m in (3.5, 5.0, 5.0):
        P.op("vector", (lambda m=m: lambda e: e.scalar_tensor_tensor(
            out=u, in0=a, scalar=m, in1=u, op0=ALU.is_gt, op1=ALU.add))(),
            waits=(P.ck("vector"),))
    P.op("vector", lambda e: e.scalar_tensor_tensor(
        out=q, in0=q, scalar=0.5, in1=u, op0=ALU.mult, op1=ALU.add),
        waits=(P.ck("vector"),))
    q3 = q.rearrange("p (w c) -> p w c", c=16)
    mx3 = mx.rearrange("p (w o) -> p w o", o=1)
    q3b, mxb = broadcast_tensor_aps(q3, mx3)
    P.op("vector", (lambda: lambda e: e.tensor_tensor(
        out=q3, in0=q3b, in1=mxb, op=ALU.mult))(), waits=(P.ck("vector"),))
    P.op("vector", lambda e: e.tensor_tensor(
        out=dst_ap, in0=q, in1=sgn, op=ALU.mult),
        waits=(P.ck("vector"), ck_sgn))
    return P.ck("vector")


def load_bias_cols(P, tp, hbm_vec, width, name, scr_f32=None):
    ncol = width // 128
    row = P.sbuf(f"{name}_row", [1, width], F32)
    rowb = P.sbuf(f"{name}_rowb", [1, width], BF16)
    colsb = P.sbuf(f"{name}_colsb", [128, ncol], BF16)
    cols = P.sbuf(f"{name}_cols", [128, ncol], F32)
    ck = P.dma("sync", row[:], hbm_vec.rearrange("(o w) -> o w", o=1))
    P.op("vector", lambda e: e.tensor_copy(rowb[:], row[:]), waits=(ck,))
    ckb = P.ck("vector")
    last = None
    for c in range(ncol):
        last = tp.t(rowb[0:1, c * 128:(c + 1) * 128], colsb[:, c:c + 1], waits=(ckb,))
    P.op("vector", lambda e: e.tensor_copy(cols[:], colsb[:]), waits=(last,))
    return cols, P.ck("vector")


def bcast_row(P, consts, hbm_vec, width, name, dtype=BF16, ps=None,
              ps_free=None):
    row = P.sbuf(f"{name}_row", [1, width], F32)
    rowb = P.sbuf(f"{name}_rowb", [1, width], BF16)
    ones = P.sbuf(f"{name}_ones", [1, 128], BF16)
    out = P.sbuf(f"{name}_bc", [128, width], dtype)
    ck = P.dma("sync", row[:], hbm_vec.rearrange("(o w) -> o w", o=1))
    P.op("vector", lambda e: e.tensor_copy(rowb[:], row[:]), waits=(ck,))
    P.op("vector", lambda e: e.memset(ones[:], 1.0))
    ckb = P.ck("vector")
    extra0 = tuple(c for c in (ps_free or ()) if c is not None)
    cks = []
    nps = ceil_div(width, 512)
    if ps is None:
        ps = P.psum(f"{name}_ps", [128, min(width, 512)], F32)
    last = None
    for i in range(nps):
        w0 = i * 512
        w1 = min(width, w0 + 512)
        mmck = P.op("tensor", (lambda w0=w0, w1=w1: lambda e: e.matmul(
            out=ps[:, 0:w1 - w0], lhsT=ones[:], rhs=rowb[:, w0:w1],
            start=True, stop=True))(), waits=((ckb,) + extra0) if i == 0 else (P.ck("vector"),))
        P.op("vector", (lambda w0=w0, w1=w1: lambda e: e.tensor_copy(
            out[:, w0:w1], ps[:, 0:w1 - w0]))(), waits=(mmck,))
        last = P.ck("vector")
    return out, last


"""Generic projection emitters (feature-major outputs)."""
import numpy as np


def emit_proj_rowsO(P, tp, name, xhatT, xhatT_ck, w_hbm, bias_col, bias_ck,
                    out_cb, O=D, scale=1.0, tok=T, extra_w=()):
    NO = O // 128
    NS = tok // 512
    wrow = P.sbuf(f"{name}_wrow", [128, D], BF16)
    wTc = P.sbuf(f"{name}_wTc", [128, D], BF16)
    stage = [P.sbuf(f"{name}_stg{i}", [128, 512], BF16) for i in range(2)]
    ps = [P.psum(f"{name}_ps{i}", [128, 512], F32) for i in range(2)]
    st_ck = [None, None]
    ps_ck = [None, None]
    prev_w = None
    prev_T = None
    si = 0
    for oc in range(NO):
        w = ((prev_T,) if prev_T else ()) + tuple(extra_w)
        ldck = P.dma("gpsimd", wrow[:], w_hbm[oc * 128:(oc + 1) * 128, :],
                     waits=w)
        tcks = []
        for c in range(NC):
            eck = tp.t(wrow[:, c * 128:(c + 1) * 128],
                       wTc[:, c * 128:(c + 1) * 128], waits=(ldck,))
            tcks.append(eck)
        prev_T = tcks[-1]
        for ts in range(NS):
            k = si % 2
            si += 1
            mw = (tcks[-1],) + tuple(xhatT_ck) + \
                 ((ps_ck[k],) if ps_ck[k] else ())
            mck = None
            for c in range(NC):
                mck = P.op("tensor", (lambda c=c, ts=ts, k=k: lambda e: e.matmul(
                    out=ps[k][:], lhsT=wTc[:, c * 128:(c + 1) * 128],
                    rhs=xhatT[c][:, ts * 512:(ts + 1) * 512],
                    start=(c == 0), stop=(c == NC - 1)))(),
                    waits=mw if c == 0 else (), inc=(1 if c == NC - 1 else 0))
            ew = (mck,) + ((st_ck[k],) if st_ck[k] else ())
            if bias_col is not None:
                ew = ew + ((bias_ck,) if bias_ck else ())
                P.op("vector", (lambda oc=oc, k=k: lambda e: e.tensor_scalar(
                    out=stage[k][:], in0=ps[k][:], scalar1=float(scale),
                    scalar2=bias_col[:, oc:oc + 1], op0=ALU.mult,
                    op1=ALU.add))(), waits=ew)
            elif scale != 1.0:
                P.op("vector", (lambda k=k: lambda e: e.tensor_scalar(
                    out=stage[k][:], in0=ps[k][:], scalar1=float(scale),
                    scalar2=None, op0=ALU.mult))(), waits=ew)
            else:
                P.op("vector", (lambda k=k: lambda e: e.tensor_copy(
                    stage[k][:], ps[k][:]))(), waits=ew)
            eck = P.ck("vector")
            ps_ck[k] = eck
            st_ck[k] = out_cb(oc, ts, stage[k][:], eck)
    return


def emit_proj_rowsD(P, name, xhatT, xhatT_ck, w_tiles, w_ck, bias_col, bias_ck,
                    out_cb, O=D, scale=1.0, tok=T):
    NO = O // 128
    NS = tok // 512
    stage = [P.sbuf(f"{name}_stg{i}", [128, 512], BF16) for i in range(2)]
    ps = [P.psum(f"{name}_ps{i}", [128, 512], F32) for i in range(2)]
    st_ck = [None, None]
    ps_ck = [None, None]
    si = 0
    for oc in range(NO):
        for ts in range(NS):
            k = si % 2
            si += 1
            mw = tuple(w_ck) + tuple(xhatT_ck) + \
                 ((ps_ck[k],) if ps_ck[k] else ())
            mck = None
            for c in range(NC):
                mck = P.op("tensor", (lambda c=c, ts=ts, oc=oc, k=k:
                                      lambda e: e.matmul(
                    out=ps[k][:], lhsT=w_tiles[c][:, oc * 128:(oc + 1) * 128],
                    rhs=xhatT[c][:, ts * 512:(ts + 1) * 512],
                    start=(c == 0), stop=(c == NC - 1)))(),
                    waits=mw if c == 0 else (), inc=(1 if c == NC - 1 else 0))
            prev_st = st_ck[k]
            if prev_st is not None and not isinstance(prev_st, tuple):
                prev_st = (prev_st,)
            ew = (mck,) + (prev_st if prev_st else ())
            if bias_col is not None:
                ew = ew + ((bias_ck,) if bias_ck else ())
                P.op("vector", (lambda oc=oc, k=k: lambda e: e.tensor_scalar(
                    out=stage[k][:], in0=ps[k][:], scalar1=float(scale),
                    scalar2=bias_col[:, oc:oc + 1], op0=ALU.mult,
                    op1=ALU.add))(), waits=ew)
            else:
                P.op("vector", (lambda k=k: lambda e: e.tensor_scalar(
                    out=stage[k][:], in0=ps[k][:], scalar1=float(scale),
                    scalar2=None, op0=ALU.mult))(), waits=ew)
            eck = P.ck("vector")
            ps_ck[k] = eck
            st_ck[k] = out_cb(oc, ts, stage[k][:], eck)
    return P.ck("tensor")


def emit_ln_xhatT(P, tp, name, xres_in, tok=T, keep_tiles=False,
                  skip_transpose=False, dtype=BF16):
    nt = tok // 128
    x_s = [P.sbuf(f"{name}_x{i}", [128, D], F32) for i in range(2)]
    dump = P.sbuf(f"{name}_dump", [128, D], BF16)
    stat = P.sbuf(f"{name}_stat", [128, 4 * nt], F32)
    if keep_tiles:
        xh_tiles = [P.sbuf(f"{name}_xh{t}", [128, D], dtype)
                    for t in range(nt)]
    else:
        xh_tiles = [P.sbuf(f"{name}_xh", [128, D], dtype)] * nt
    xhatT = [P.sbuf(f"{name}_xT{c}", [128, tok], dtype) for c in range(NC)] \
        if not skip_transpose else None
    used = [None, None]
    for t in range(nt):
        k = t % 2
        w = ((used[k],) if used[k] else ())
        ck = P.dma("sync", x_s[k][:], xres_in[t * 128:(t + 1) * 128, :],
                   waits=w)
        used[k] = emit_ln_stats_tile(P, x_s[k][:], dump[:], stat, t,
                                     waits=(ck,), nt=nt)
    ln_ck = emit_ln_finalize(P, stat, nt=nt)
    prev_q = None
    tile_cks = []
    for t in range(nt):
        k = t % 2
        w = ((used[k],) if used[k] else ())
        ck = P.dma("sync", x_s[k][:], xres_in[t * 128:(t + 1) * 128, :],
                   waits=w)
        aw = (ck, ln_ck)
        if not keep_tiles and prev_q is not None:
            aw = aw + (prev_q,)
        ack = ln_affine(P, x_s[k][:], xh_tiles[t][:], stat, t, waits=aw,
                        nt=nt)
        used[k] = ack
        tile_cks.append(ack)
        if not skip_transpose:
            tcks = []
            for c in range(NC):
                eck = tp.t(xh_tiles[t][:, c * 128:(c + 1) * 128],
                           xhatT[c][:, t * 128:(t + 1) * 128], waits=(ack,))
                tcks.append(eck)
            prev_q = tcks[-1]
    ready = (P.ck("scalar"),) if not skip_transpose else (P.ck("vector"),)
    if keep_tiles:
        return xhatT, ready, xh_tiles, tile_cks
    return xhatT, ready


"""L0_attn_a: Nystromformer attention (64 landmarks, Moore-Penrose pinv)."""
import numpy as np

NH = 16
HS = 64
M = 64
ML = 32
ITERS = 6
HT = NH * M


def ident_tiled_host():
    import ml_dtypes
    i64 = np.eye(64, dtype=np.float32)
    return np.tile(i64, (1, 16)).astype(ml_dtypes.bfloat16)


def make_sb_cb(P, q_sb, out_cb):
    def cb(oc, ts, src_ap, ck):
        P.op("vector", (lambda oc=oc, ts=ts, src_ap=src_ap:
                        lambda e: e.tensor_copy(
            q_sb[oc][:, ts * 512:(ts + 1) * 512], src_ap))(), waits=(ck,))
        cck = P.ck("vector")
        dck = out_cb(oc, ts, src_ap, ck)
        return (cck, dck)
    return cb


def emit_l0attnA_1(P, consts, tp, wk_hbm, wq_hbm, wv_hbm, xres_in,
                   q_out, k_out, v_out, ql_out, kl_out):
    xhatT, xt_ck = emit_ln_xhatT(P, tp, "na", xres_in)
    w_tiles = [P.sbuf(f"na_w{c}", [128, D], BF16) for c in range(NC)]
    q_sb = [P.sbuf(f"na_q{c}", [128, T], BF16) for c in range(NC)]
    lm = P.sbuf("na_lm", [128, ML], F32)
    projs = [("q", wq_hbm, q_out, 0.125, ql_out),
             ("k", wk_hbm, k_out, 1.0, kl_out),
             ("v", wv_hbm, v_out, 1.0, None)]
    prev_w = [None] * NC
    prev_proj = None
    all_stores = []
    for (nm, w_hbm, out_hbm, scale, l_out) in projs:
        w_cks = []
        for c in range(NC):
            w = ((prev_w[c],) if prev_w[c] else ()) + \
                ((prev_proj,) if prev_proj else ())
            prev_w[c] = P.dma("gpsimd", w_tiles[c][:],
                              w_hbm[c * 128:(c + 1) * 128, :], waits=w)
            w_cks.append(prev_w[c])

        def out_cb(oc, ts, src_ap, ck, out_hbm=out_hbm):
            dck = P.dma("sync", out_hbm[oc * 128:(oc + 1) * 128,
                                        ts * 512:(ts + 1) * 512],
                        src_ap, waits=(ck,))
            all_stores.append(dck)
            return dck

        cb = make_sb_cb(P, q_sb, out_cb) if l_out is not None else out_cb
        prev_proj = emit_proj_rowsD(P, f"na_{nm}", xhatT, xt_ck, w_tiles,
                                    tuple(w_cks), None, None, cb, scale=scale)
        if l_out is not None:
            lw = (P.ck("vector"),)
            prev_lm = None
            for c in range(NC):
                P.op("vector", (lambda c=c: lambda e: e.tensor_reduce(
                    out=lm[:, 0:ML],
                    in_=q_sb[c][:].rearrange("p (w c2) -> p w c2", c2=64),
                    axis=AX.X, op=ALU.add))(),
                    waits=lw + ((prev_lm,) if prev_lm else ()))
                P.op("vector", lambda e: e.tensor_scalar(
                    out=lm[:, 0:ML], in0=lm[:, 0:ML], scalar1=1.0 / 64,
                    scalar2=None, op0=ALU.mult))
                prev_lm = P.dma("sync", l_out[c * 128:(c + 1) * 128, :],
                                lm[:, 0:ML], waits=(P.ck("vector"),))
                all_stores.append(prev_lm)
    P.wait("sync", tuple(all_stores))


def emit_l0attnA_2(P, consts, tp, q_hbm, k_hbm, v_hbm, qlg_hbm, klg_hbm,
                   identt_hbm, s1T_out, zsT_out, mzs_out, stages=5):
    from concourse.bass import broadcast_tensor_aps
    qlg = P.sbuf("n2_qlg", [HS, HT], BF16)
    klg = P.sbuf("n2_klg", [HS, HT], BF16)
    qlf = P.sbuf("n2_qlf", [HS, M], F32)
    for h in range(NH):
        c1 = P.dma("sync", qlf[:], qlg_hbm[h * HS:(h + 1) * HS, :],
                   waits=(P.ck("vector"),))
        P.op("vector", (lambda h=h: lambda e: e.tensor_copy(
            qlg[:, h * M:(h + 1) * M], qlf[:]))(), waits=(c1,))
        c2 = P.dma("sync", qlf[:], klg_hbm[h * HS:(h + 1) * HS, :],
                   waits=(P.ck("vector"),))
        P.op("vector", (lambda h=h: lambda e: e.tensor_copy(
            klg[:, h * M:(h + 1) * M], qlf[:]))(), waits=(c2,))
    qlg_ck = P.ck("vector")
    identt = P.sbuf("n2_id", [M, HT], BF16)
    id_ck = P.dma("sync", identt[:], identt_hbm)

    ps_b = P.psum("n2_psb", [M, 512], F32)
    ps_p = P.psum("n2_psp", [M, 512], F32)
    s1ps = P.psum("n2_s1ps", [128, HT], F32)
    ps_s3 = [P.psum(f"n2_ps3{i}", [M, 512], F32) for i in range(2)]

    s2n = P.sbuf("n2_s2n", [M, HT], BF16)
    s2nT = P.sbuf("n2_s2nT", [M, HT], BF16)
    sm = P.sbuf("n2_sm", [M, 8], F32)
    prev_blk = None
    for blk in range(2):
        mck = None
        for hh in range(8):
            h = blk * 8 + hh
            w = ()
            if hh == 0:
                w = (qlg_ck,) + ((prev_blk,) if prev_blk else ())
            mck = P.op("tensor", (lambda h=h, hh=hh: lambda e: e.matmul(
                out=ps_b[:, hh * M:(hh + 1) * M],
                lhsT=qlg[:, h * M:(h + 1) * M],
                rhs=klg[:, h * M:(h + 1) * M],
                start=True, stop=True))(), waits=w)
        base = blk * 512
        ps3 = ps_b[:].rearrange("p (w c2) -> p w c2", c2=M)
        mx3 = sm[:, 0:8].rearrange("p (w o) -> p w o", o=1)
        P.op("vector", (lambda ps3=ps3: lambda e: e.tensor_reduce(
            out=sm[:, 0:8], in_=ps3, axis=AX.X, op=ALU.max))(), waits=(mck,))
        a3, b3 = broadcast_tensor_aps(ps3, mx3)
        P.op("vector", (lambda a3=a3, b3=b3: lambda e: e.tensor_tensor(
            out=ps_b[:], in0=a3, in1=b3, op=ALU.subtract))(),
            waits=(P.ck("vector"),))
        P.op("scalar", lambda e: e.activation(
            out=ps_b[:], in_=ps_b[:], func=AF.Exp), waits=(P.ck("vector"),))
        P.op("vector", (lambda ps3=ps3: lambda e: e.tensor_reduce(
            out=sm[:, 0:8], in_=ps3, axis=AX.X, op=ALU.add))(),
            waits=(P.ck("scalar"),))
        P.op("vector", lambda e: e.reciprocal(out=sm[:, 0:8],
                                              in_=sm[:, 0:8]),
             waits=(P.ck("vector"),))
        a3b, b3b = broadcast_tensor_aps(ps3, mx3)
        P.op("vector", (lambda base=base, a3b=a3b, b3b=b3b:
                        lambda e: e.tensor_tensor(
            out=s2n[:, base:base + 512], in0=a3b, in1=b3b, op=ALU.mult))(),
            waits=(P.ck("vector"),))
        s2_ck = P.ck("vector")
        for hh in range(8):
            o = base + hh * M
            tp.t(s2n[:, o:o + M], s2nT[:, o:o + M], waits=(s2_ck,))
        prev_blk = P.ck("scalar")
    s2nT_ck = P.ck("scalar")
    s2n_ck = P.ck("vector")

    z = P.sbuf("n2_z", [M, HT], BF16)
    zT = P.sbuf("n2_zT", [M, HT], BF16)
    t1 = P.sbuf("n2_t1", [M, HT], BF16)
    u = P.sbuf("n2_u", [M, HT], BF16)
    xz = P.sbuf("n2_xz", [M, HT], BF16)
    xzT = P.sbuf("n2_xzT", [M, HT], BF16)
    ones64 = P.sbuf("n2_ones", [M, 1], BF16)
    ones_row = P.sbuf("n2_onesr", [1, M], BF16)
    P.op("vector", lambda e: e.memset(ones64[:], 1.0))
    P.op("vector", lambda e: e.memset(ones_row[:], 1.0))
    ones_ck = P.ck("vector")
    rs = P.sbuf("n2_rs", [M, NH], F32)
    rsb = P.sbuf("n2_rsb", [M, NH], BF16)
    rsT = P.sbuf("n2_rsT", [1, M], BF16)
    cs = P.sbuf("n2_cs", [1, HT], F32)
    csm = P.sbuf("n2_csm", [1, 2 * NH], F32)
    csm_b = P.sbuf("n2_csmb", [1, NH], BF16)
    rec64 = P.sbuf("n2_rec", [M, NH], F32)
    P.op("vector", lambda e: e.tensor_reduce(
        out=rs[:], in_=s2n[:].rearrange("p (w c2) -> p w c2", c2=M),
        axis=AX.X, op=ALU.add), waits=(s2n_ck,))
    P.op("vector", lambda e: e.tensor_copy(rsb[:], rs[:]),
         waits=(P.ck("vector"),))
    rsb_ck = P.ck("vector")
    for half in range(2):
        csck = P.op("tensor", (lambda half=half: lambda e: e.matmul(
            out=ps_b[0:1, 0:512], lhsT=ones64[:],
            rhs=s2n[:, half * 512:(half + 1) * 512],
            start=True, stop=True))(),
            waits=(ones_ck, s2n_ck, P.ck("vector")))
        P.op("vector", (lambda half=half: lambda e: e.tensor_copy(
            cs[0:1, half * 512:(half + 1) * 512], ps_b[0:1, 0:512]))(),
            waits=(csck,))
    P.op("vector", lambda e: e.tensor_reduce(
        out=csm[0:1, 0:NH],
        in_=cs[0:1, :].rearrange("p (w c2) -> p w c2", c2=M),
        axis=AX.X, op=ALU.max), waits=(P.ck("vector"),))
    for h in range(NH):
        tp.t(rsb[:, h:h + 1], rsT[0:1, :], waits=(rsb_ck, P.ck("vector")))
        P.op("vector", (lambda h=h: lambda e: e.tensor_reduce(
            out=csm[0:1, NH + h:NH + h + 1], in_=rsT[0:1, :], axis=AX.X,
            op=ALU.max))(), waits=(P.ck("scalar"),))
    P.op("vector", lambda e: e.tensor_tensor(
        out=csm[0:1, 0:NH], in0=csm[0:1, 0:NH], in1=csm[0:1, NH:2 * NH],
        op=ALU.mult), waits=(P.ck("vector"),))
    P.op("vector", lambda e: e.reciprocal(
        out=csm[0:1, 0:NH], in_=csm[0:1, 0:NH]), waits=(P.ck("vector"),))
    P.op("vector", lambda e: e.tensor_copy(csm_b[:], csm[0:1, 0:NH]),
         waits=(P.ck("vector"),))
    cb_ck = P.ck("vector")
    for h in range(NH):
        bck = P.op("tensor", (lambda h=h: lambda e: e.matmul(
            out=ps_b[0:M, 0:1], lhsT=ones_row[:], rhs=csm_b[:, h:h + 1],
            start=True, stop=True))(),
            waits=(cb_ck,) if h == 0 else (P.ck("vector"),))
        P.op("vector", (lambda h=h: lambda e: e.tensor_copy(
            rec64[:, h:h + 1], ps_b[0:M, 0:1]))(), waits=(bck,))
    rec_ck = P.ck("vector")
    for (src, dst, wck) in ((s2nT, z, s2nT_ck), (s2n, zT, s2n_ck)):
        for h in range(NH):
            P.op("vector", (lambda src=src, dst=dst, h=h:
                            lambda e: e.tensor_scalar(
                out=dst[:, h * M:(h + 1) * M], in0=src[:, h * M:(h + 1) * M],
                scalar1=rec64[:, h:h + 1], scalar2=None, op0=ALU.mult))(),
                waits=(rec_ck, wck))
    z_ck = P.ck("vector")

    def heads_mm_evac(lhs_sb, rhs_sb, out_sb, waits, scale=None):
        last = None
        for blk in range(2):
            mck = None
            for hh in range(8):
                h = blk * 8 + hh
                w = ()
                if blk == 0 and hh == 0:
                    w = tuple(waits)
                if blk == 1 and hh == 0:
                    w = (P.ck("vector"),)
                mck = P.op("tensor", (lambda h=h, hh=hh, lhs_sb=lhs_sb,
                                      rhs_sb=rhs_sb: lambda e: e.matmul(
                    out=ps_p[:, hh * M:(hh + 1) * M],
                    lhsT=lhs_sb[:, h * M:(h + 1) * M],
                    rhs=rhs_sb[:, h * M:(h + 1) * M],
                    start=True, stop=True))(), waits=w)
            if scale is None:
                P.op("vector", (lambda blk=blk, out_sb=out_sb:
                                lambda e: e.tensor_copy(
                    out_sb[:, blk * 512:(blk + 1) * 512], ps_p[:]))(),
                    waits=(mck,))
            else:
                P.op("vector", (lambda blk=blk, out_sb=out_sb, scale=scale:
                                lambda e: e.tensor_scalar(
                    out=out_sb[:, blk * 512:(blk + 1) * 512], in0=ps_p[:],
                    scalar1=float(scale), scalar2=None, op0=ALU.mult))(),
                    waits=(mck,))
            last = P.ck("vector")
        return last

    def isub(dst, const, srcT):
        P.op("vector", (lambda dst=dst, const=const, srcT=srcT:
                        lambda e: e.scalar_tensor_tensor(
            out=dst[:], in0=identt[:], scalar=float(const), in1=srcT[:],
            op0=ALU.mult, op1=ALU.subtract))(),
            waits=(P.ck("vector"), id_ck))
        return P.ck("vector")

    for it in range(ITERS):
        w0 = (z_ck,) if it == 0 else (P.ck("vector"),)
        heads_mm_evac(s2nT, z, xz, w0)
        heads_mm_evac(z, s2nT, xzT, (P.ck("vector"),))
        isub(t1, 7.0, xz)
        heads_mm_evac(xzT, t1, u, (P.ck("vector"),))
        isub(t1, 15.0, u)
        heads_mm_evac(xzT, t1, u, (P.ck("vector"),))
        isub(t1, 13.0, u)
        heads_mm_evac(zT, t1, u, (P.ck("vector"),))
        heads_mm_evac(t1, zT, zT, (P.ck("vector"),), scale=0.25)
        P.op("vector", lambda e: e.tensor_scalar(
            out=z[:], in0=u[:], scalar1=0.25, scalar2=None, op0=ALU.mult),
            waits=(P.ck("vector"),))
    zs_f = P.sbuf("n2_zsf", [M, HT], F32)
    P.op("vector", lambda e: e.tensor_copy(zs_f[:], z[:]),
         waits=(P.ck("vector"),))
    zst = P.dma("sync", zsT_out, zs_f[:], waits=(P.ck("vector"),))

    s1sc = P.sbuf("n2_s1sc", [128, HT], F32)
    s1n = P.sbuf("n2_s1n", [128, HT], BF16)
    wm = P.sbuf("n2_wm", [128, NH], F32)
    qsl = [P.sbuf(f"n2_qsl{i}", [HS, 128], BF16) for i in range(4)]
    qsl_ck = [None] * 4
    qsl_used = [None] * 4
    stg = [P.sbuf(f"n2_stg{i}", [128, 128], BF16) for i in range(2)]
    stg_ck = [None, None]
    s1_store = []
    prev_s1ps = None
    qi = 0
    for t in range(NT):
        mcks = []
        for h in range(NH):
            qk = qi % 4
            qi += 1
            lw = ((qsl_used[qk],) if qsl_used[qk] else ())
            qsl_ck[qk] = P.dma("sync", qsl[qk][:],
                               q_hbm[h * HS:(h + 1) * HS,
                                     t * 128:(t + 1) * 128], waits=lw)
            w = (qsl_ck[qk], qlg_ck)
            if h == 0:
                w = w + ((prev_s1ps,) if prev_s1ps else ())
            mck = P.op("tensor", (lambda h=h, qk=qk: lambda e: e.matmul(
                out=s1ps[:, h * M:(h + 1) * M],
                lhsT=qsl[qk][:], rhs=klg[:, h * M:(h + 1) * M],
                start=True, stop=True))(), waits=w)
            qsl_used[qk] = mck
            mcks.append(mck)
        ps3 = s1ps[:].rearrange("p (w c2) -> p w c2", c2=M)
        sc3 = s1sc[:].rearrange("p (w c2) -> p w c2", c2=M)
        mx3 = wm[:].rearrange("p (w o) -> p w o", o=1)
        P.op("vector", (lambda ps3=ps3: lambda e: e.tensor_reduce(
            out=wm[:], in_=ps3, axis=AX.X, op=ALU.max))(), waits=(mcks[-1],))
        a3, b3 = broadcast_tensor_aps(ps3, mx3)
        P.op("vector", (lambda a3=a3, b3=b3, sc3=sc3:
                        lambda e: e.tensor_tensor(
            out=sc3, in0=a3, in1=b3, op=ALU.subtract))(),
            waits=(P.ck("vector"),))
        prev_s1ps = P.ck("vector")
        P.op("scalar", lambda e: e.activation(
            out=s1sc[:], in_=s1sc[:], func=AF.Exp), waits=(prev_s1ps,))
        P.op("vector", (lambda sc3=sc3: lambda e: e.tensor_reduce(
            out=wm[:], in_=sc3, axis=AX.X, op=ALU.add))(),
            waits=(P.ck("scalar"),))
        P.op("vector", lambda e: e.reciprocal(out=wm[:], in_=wm[:]),
             waits=(P.ck("vector"),))
        a3b, b3b = broadcast_tensor_aps(sc3, mx3)
        P.op("vector", (lambda a3b=a3b, b3b=b3b: lambda e: e.tensor_tensor(
            out=s1n[:], in0=a3b, in1=b3b, op=ALU.mult))(),
            waits=(P.ck("vector"),))
        s1n_ck = P.ck("vector")
        for c in range(8):
            k = (t * 8 + c) % 2
            ew = (s1n_ck,) + ((stg_ck[k],) if stg_ck[k] else ())
            eck = tp.t(s1n[:, c * 128:(c + 1) * 128], stg[k][:], waits=ew)
            stg_ck[k] = P.dma("sync",
                              s1T_out[c * 128:(c + 1) * 128,
                                      t * 128:(t + 1) * 128],
                              stg[k][:], waits=(eck,))
            s1_store.append(stg_ck[k])

    khs = [P.sbuf(f"n2_khs{i}", [HS, T], BF16) for i in range(2)]
    vhs = [P.sbuf(f"n2_vhs{i}", [HS, T], BF16) for i in range(2)]
    kh_ck = [None, None]
    vh_ck = [None, None]
    kh_used = [None, None]
    vh_used = [None, None]
    e3 = P.sbuf("n2_e3", [M, T], BF16)
    e3T = P.sbuf("n2_e3T", [128, NT * M], BF16)
    vT = P.sbuf("n2_vT", [128, NT * M], BF16)
    sm3 = P.sbuf("n2_sm3", [M, 8], F32)
    mzs = P.sbuf("n2_mzs", [M, NH * 66], F32)
    prev_e3 = None
    prev_z = None
    for h in range(NH):
        hk = h % 2
        kh_ck[hk] = P.dma("gpsimd", khs[hk][:],
                          k_hbm[h * HS:(h + 1) * HS, :],
                          waits=((kh_used[hk],) if kh_used[hk] else ()))
        vh_ck[hk] = P.dma("gpsimd", vhs[hk][:],
                          v_hbm[h * HS:(h + 1) * HS, :],
                          waits=((vh_used[hk],) if vh_used[hk] else ()))
        mk3 = [None, None]
        for sl in range(4):
            k2 = sl % 2
            w = (kh_ck[hk], qlg_ck)
            if sl >= 2:
                w = w + (Ck("@scalar", P.cnt["@scalar"]),)
            if sl == 0 and prev_z:
                w = w + (prev_z,)
            mk3[k2] = P.op("tensor", (lambda h=h, sl=sl, k2=k2, hk=hk:
                                      lambda e: e.matmul(
                out=ps_s3[k2][:], lhsT=qlg[:, h * M:(h + 1) * M],
                rhs=khs[hk][:, sl * 512:(sl + 1) * 512],
                start=True, stop=True))(), waits=w)
            P.op("vector", (lambda sl=sl, k2=k2: lambda e: e.tensor_reduce(
                out=sm3[:, sl:sl + 1], in_=ps_s3[k2][:], axis=AX.X,
                op=ALU.max))(), waits=(mk3[k2],))
            P.op("scalar", (lambda sl=sl, k2=k2, h=h: lambda e: e.activation(
                out=e3[:, sl * 512:(sl + 1) * 512], in_=ps_s3[k2][:],
                func=AF.Copy))(), waits=(P.ck("vector"),))
        kh_used[hk] = P.ck("scalar")
        P.op("vector", lambda e: e.tensor_reduce(
            out=sm3[:, 4:5], in_=sm3[:, 0:4], axis=AX.X, op=ALU.max),
            waits=(P.ck("vector"), P.ck("scalar")))
        P.op("vector", lambda e: e.tensor_scalar(
            out=sm3[:, 5:6], in0=sm3[:, 4:5], scalar1=-1.0, scalar2=None,
            op0=ALU.mult), waits=(P.ck("vector"),))
        negm_ck = P.ck("vector")
        for sl in range(4):
            P.op("scalar", (lambda sl=sl: lambda e: e.activation(
                out=e3[:, sl * 512:(sl + 1) * 512],
                in_=e3[:, sl * 512:(sl + 1) * 512],
                func=AF.Exp, bias=sm3[:, 5:6], scale=1.0,
                accum_out=sm3[:, 6:7]))(),
                waits=(negm_ck,) if sl == 0 else (P.ck("vector"),))
            if sl == 0:
                P.op("vector", lambda e: e.tensor_copy(
                    sm3[:, 7:8], sm3[:, 6:7]), waits=(P.ck("scalar"),))
            else:
                P.op("vector", lambda e: e.tensor_tensor(
                    out=sm3[:, 7:8], in0=sm3[:, 7:8], in1=sm3[:, 6:7],
                    op=ALU.add), waits=(P.ck("scalar"),))
        e3_ck = (P.ck("scalar"), P.ck("vector"))
        tcks = []
        for c in range(NT):
            ew = e3_ck + ((prev_e3,) if prev_e3 and c == 0 else ())
            tp.t(e3[:, c * 128:(c + 1) * 128],
                 e3T[:, c * M:(c + 1) * M], waits=ew)
            eck2 = tp.t(vhs[hk][:, c * 128:(c + 1) * 128],
                        vT[:, c * M:(c + 1) * M], waits=(vh_ck[hk],))
            tcks.append(eck2)
        tr_ck = tcks[-1]
        zmck = None
        for c in range(NT):
            zmck = P.op("tensor", (lambda c=c: lambda e: e.matmul(
                out=ps_b[0:M, 0:M], lhsT=e3T[:, c * M:(c + 1) * M],
                rhs=vT[:, c * M:(c + 1) * M],
                start=(c == 0), stop=(c == NT - 1)))(),
                waits=(tr_ck,) if c == 0 else (), inc=1)
        prev_e3 = zmck
        vh_used[hk] = zmck
        P.op("vector", (lambda h=h: lambda e: e.tensor_copy(
            mzs[:, h * 66:h * 66 + 64], ps_b[0:M, 0:M]))(), waits=(zmck,))
        P.op("vector", (lambda h=h: lambda e: e.tensor_copy(
            mzs[:, h * 66 + 64:h * 66 + 65], sm3[:, 4:5]))(),
            waits=(P.ck("vector"),))
        P.op("vector", (lambda h=h: lambda e: e.tensor_copy(
            mzs[:, h * 66 + 65:h * 66 + 66], sm3[:, 7:8]))(),
            waits=(P.ck("vector"),))
        prev_z = P.ck("vector")
    mzst = P.dma("sync", mzs_out, mzs[:],
                 waits=(prev_z,) if prev_z else (P.ck("vector"),))
    P.wait("sync", (mzst, zst) + tuple(s1_store[-2:]))


def emit_l0attnA_3(P, consts, tp, s1T_hbm, zsT_hbm, mzs_own_hbm,
                   mzs_pair_hbm, wo_hbm, bo_hbm, xres_in, xres_out):
    mo = P.sbuf("n3_mo", [M, NH * 66], F32)
    mp = P.sbuf("n3_mp", [M, NH * 66], F32)
    c1 = P.dma("sync", mo[:], mzs_own_hbm)
    c2 = P.dma("sync", mp[:], mzs_pair_hbm)
    zs = P.sbuf("n3_zs", [M, HT], F32)
    zsb = P.sbuf("n3_zsb", [M, HT], BF16)
    c3 = P.dma("sync", zs[:], zsT_hbm)
    P.op("vector", lambda e: e.tensor_copy(zsb[:], zs[:]), waits=(c3,))
    zs_ck = P.ck("vector")
    sm = P.sbuf("n3_sm", [M, 8], F32)
    z3 = P.sbuf("n3_z3", [M, HT], BF16)
    for h in range(NH):
        o = h * 66
        P.op("vector", (lambda o=o: lambda e: e.tensor_tensor(
            out=sm[:, 0:1], in0=mo[:, o + 64:o + 65],
            in1=mp[:, o + 64:o + 65], op=ALU.max))(), waits=(c1, c2))
        P.op("vector", lambda e: e.tensor_scalar(
            out=sm[:, 1:2], in0=sm[:, 0:1], scalar1=-1.0, scalar2=None,
            op0=ALU.mult), waits=(P.ck("vector"),))
        P.op("scalar", (lambda o=o: lambda e: e.activation(
            out=sm[:, 2:3], in_=mo[:, o + 64:o + 65], func=AF.Exp,
            bias=sm[:, 1:2], scale=1.0))(), waits=(P.ck("vector"),))
        P.op("scalar", (lambda o=o: lambda e: e.activation(
            out=sm[:, 3:4], in_=mp[:, o + 64:o + 65], func=AF.Exp,
            bias=sm[:, 1:2], scale=1.0))(), waits=(P.ck("scalar"),))
        P.op("vector", (lambda o=o: lambda e: e.tensor_tensor(
            out=sm[:, 4:5], in0=mo[:, o + 65:o + 66], in1=sm[:, 2:3],
            op=ALU.mult))(), waits=(P.ck("scalar"),))
        P.op("vector", (lambda o=o: lambda e: e.tensor_tensor(
            out=sm[:, 5:6], in0=mp[:, o + 65:o + 66], in1=sm[:, 3:4],
            op=ALU.mult))(), waits=(P.ck("vector"),))
        P.op("vector", lambda e: e.tensor_tensor(
            out=sm[:, 4:5], in0=sm[:, 4:5], in1=sm[:, 5:6], op=ALU.add),
            waits=(P.ck("vector"),))
        P.op("vector", lambda e: e.reciprocal(out=sm[:, 4:5],
                                              in_=sm[:, 4:5]),
             waits=(P.ck("vector"),))
        P.op("vector", (lambda o=o: lambda e: e.tensor_scalar(
            out=mo[:, o:o + 64], in0=mo[:, o:o + 64], scalar1=sm[:, 2:3],
            scalar2=None, op0=ALU.mult))(), waits=(P.ck("vector"),))
        P.op("vector", (lambda o=o: lambda e: e.scalar_tensor_tensor(
            out=mo[:, o:o + 64], in0=mp[:, o:o + 64], scalar=sm[:, 3:4],
            in1=mo[:, o:o + 64], op0=ALU.mult, op1=ALU.add))(),
            waits=(P.ck("vector"),))
        P.op("vector", (lambda o=o, h=h: lambda e: e.tensor_scalar(
            out=z3[:, h * M:(h + 1) * M], in0=mo[:, o:o + 64],
            scalar1=sm[:, 4:5], scalar2=None, op0=ALU.mult))(),
            waits=(P.ck("vector"),))
    z3_ck = P.ck("vector")
    ps_w = P.psum("n3_psw", [M, HT], F32)
    Wt = P.sbuf("n3_W", [M, HT], BF16)
    wmck = None
    for h in range(NH):
        wmck = P.op("tensor", (lambda h=h: lambda e: e.matmul(
            out=ps_w[:, h * M:(h + 1) * M], lhsT=zsb[:, h * M:(h + 1) * M],
            rhs=z3[:, h * M:(h + 1) * M], start=True, stop=True))(),
            waits=(zs_ck, z3_ck) if h == 0 else ())
    P.op("vector", lambda e: e.tensor_copy(Wt[:], ps_w[:]), waits=(wmck,))
    W_ck = P.ck("vector")
    s1h = [P.sbuf(f"n3_s1{i}", [M, T], BF16) for i in range(2)]
    s1_ck = [None, None]
    attn = [P.sbuf(f"n3_at{c}", [128, T], BF16) for c in range(NC)]
    ps_o = [P.psum(f"n3_pso{i}", [M, 512], F32) for i in range(2)]
    pso_used = [None, None]
    oi = 0
    for h in range(NH):
        hk = h % 2
        w = ((s1_ck[hk],) if s1_ck[hk] else ())
        lck = P.dma("sync", s1h[hk][:], s1T_hbm[h * M:(h + 1) * M, :],
                    waits=w)
        last = None
        for sl in range(4):
            ko = oi % 2
            oi += 1
            mck = P.op("tensor", (lambda h=h, sl=sl, hk=hk, ko=ko:
                                  lambda e: e.matmul(
                out=ps_o[ko][:], lhsT=Wt[:, h * M:(h + 1) * M],
                rhs=s1h[hk][:, sl * 512:(sl + 1) * 512],
                start=True, stop=True))(),
                waits=(lck, W_ck) + ((pso_used[ko],) if pso_used[ko]
                                     else ()))
            P.op("scalar", (lambda h=h, sl=sl, ko=ko: lambda e: e.activation(
                out=attn[h // 2][(h % 2) * 64:(h % 2) * 64 + 64,
                                 sl * 512:(sl + 1) * 512],
                in_=ps_o[ko][:], func=AF.Copy))(), waits=(mck,))
            pso_used[ko] = P.ck("scalar")
            last = pso_used[ko]
        s1_ck[hk] = last
    attn_ck = (P.ck("scalar"),)
    wo_t = [P.sbuf(f"n3_wo{c}", [128, D], BF16) for c in range(NC)]
    wo_cks = [P.dma("gpsimd", wo_t[c][:], wo_hbm[c * 128:(c + 1) * 128, :])
              for c in range(NC)]
    bo_bc, bo_ck = bcast_row(P, consts, bo_hbm, D, "n3_bo", dtype=F32)
    x_s = [P.sbuf(f"n3_x{i}", [128, D], F32) for i in range(2)]
    ps_f = [P.psum(f"n3_psf{i}", [128, 512], F32) for i in range(2)]
    x_used = [None, None]
    psf_used = [None, None]
    for t in range(NT):
        k = t % 2
        xck = P.dma("sync", x_s[k][:], xres_in[t * 128:(t + 1) * 128, :],
                    waits=((x_used[k],) if x_used[k] else ()))
        for half in range(2):
            mw = tuple(wo_cks) + attn_ck + \
                 ((psf_used[half],) if psf_used[half] else ())
            mck = None
            for c in range(NC):
                mck = P.op("tensor", (lambda c=c, t=t, half=half:
                                      lambda e: e.matmul(
                    out=ps_f[half][:],
                    lhsT=attn[c][:, t * 128:(t + 1) * 128],
                    rhs=wo_t[c][:, half * 512:(half + 1) * 512],
                    start=(c == 0), stop=(c == NC - 1)))(),
                    waits=mw if c == 0 else (), inc=(1 if c == NC - 1 else 0))
            P.op("vector", (lambda k=k, half=half: lambda e: e.tensor_tensor(
                out=x_s[k][:, half * 512:(half + 1) * 512],
                in0=x_s[k][:, half * 512:(half + 1) * 512],
                in1=ps_f[half][:], op=ALU.add))(), waits=(mck, xck))
            P.op("vector", (lambda k=k, half=half: lambda e: e.tensor_tensor(
                out=x_s[k][:, half * 512:(half + 1) * 512],
                in0=x_s[k][:, half * 512:(half + 1) * 512],
                in1=bo_bc[:, half * 512:(half + 1) * 512], op=ALU.add))(),
                waits=(bo_ck,))
            psf_used[half] = P.ck("vector")
        x_used[k] = P.dma("sync", xres_out[t * 128:(t + 1) * 128, :],
                          x_s[k][:], waits=(P.ck("vector"),))
    P.wait("sync", (x_used[0], x_used[1]))


"""L0_ffn: product-key memory FFN (top-1 expert per head via argmax+argmax)."""
import numpy as np
import concourse.bass as bass

HEADS = 8
NKEYS = 256
DKEY = 512
NEXP = 65536


def emit_l0ffn(P, consts, tp, wq_hbm, keys_hbm, wdown_hbm, wup_hbm,
               xres_in, xres_out, dbg=None):
    xhatT, xt_ck, xh_tiles, xh_cks = emit_ln_xhatT(
        P, tp, "pk", xres_in, keep_tiles=True, dtype=F16)

    qph = [P.sbuf(f"pk_qph{c}", [128, T], F16) for c in range(4)]
    wqs = [P.sbuf(f"pk_wqs{i}", [128, 512], F16) for i in range(NC)]
    keysb = P.sbuf("pk_keysb", [128, 2 * DKEY], F16)
    keysT = P.sbuf("pk_keysT", [128, 4 * NKEYS], F16)
    ps_q = [P.psum(f"pk_psq{i}", [128, 512], F32) for i in range(2)]
    ps_s = [P.psum(f"pk_pss{i}", [128, NKEYS], F32) for i in range(2)]
    sx = [P.sbuf(f"pk_sx{h}", [128, NT], F32) for h in range(HEADS)]
    sy = [P.sbuf(f"pk_sy{h}", [128, NT], F32) for h in range(HEADS)]
    ix = [P.sbuf(f"pk_ix{h}", [128, NT], F32) for h in range(HEADS)]
    iy = [P.sbuf(f"pk_iy{h}", [128, NT], F32) for h in range(HEADS)]
    m8 = P.sbuf("pk_m8", [128, 8], F32)
    i8 = P.sbuf("pk_i8", [128, 8], U32)
    i8f = P.sbuf("pk_i8f", [128, 8], F32)
    wq_ck = [None] * NC
    psq_used = [None, None]
    pss_used = [None, None]
    qph_used = None
    keys_used = None
    m8_used = None
    prev_qready = None
    for p in range(2):
        for h in range(HEADS):
            o0 = p * HEADS * DKEY + h * DKEY
            for c in range(NC):
                w = ((wq_ck[c],) if wq_ck[c] else ()) +                     ((prev_qready,) if prev_qready else ())
                wq_ck[c] = P.dma("gpsimd", wqs[c][:],
                                 wq_hbm[c * 128:(c + 1) * 128, o0:o0 + DKEY],
                                 waits=w)
            kw = ((keys_used,) if keys_used else ())
            kck = P.dma("gpsimd", keysb[:].rearrange("p (r f) -> p r f", r=2),
                        keys_hbm[h, :, p, :].rearrange(
                            "(k pp) f -> pp k f", pp=128), waits=kw)
            tk = []
            for kc in range(2):
                for dc in range(4):
                    eck = tp.t(keysb[:, kc * 512 + dc * 128:
                                     kc * 512 + (dc + 1) * 128],
                               keysT[:, dc * NKEYS + kc * 128:
                                     dc * NKEYS + (kc + 1) * 128],
                               waits=(kck,))
                    tk.append(eck)
            keysT_ck = tk[-1]
            for oc in range(4):
                for ts in range(4):
                    k = ts % 2
                    mw = tuple(wq_ck) + tuple(xt_ck) + \
                         ((psq_used[k],) if psq_used[k] else ())
                    mck = None
                    for c in range(NC):
                        mck = P.op("tensor", (lambda c=c, oc=oc, ts=ts, k=k:
                                              lambda e: e.matmul(
                            out=ps_q[k][:],
                            lhsT=wqs[c][:, oc * 128:(oc + 1) * 128],
                            rhs=xhatT[c][:, ts * 512:(ts + 1) * 512],
                            start=(c == 0), stop=(c == NC - 1)))(),
                            waits=mw if c == 0 else (),
                            inc=(1 if c == NC - 1 else 0))
                    ew = (mck,) + ((qph_used,) if qph_used else ())
                    P.op("scalar", (lambda oc=oc, ts=ts, k=k:
                                    lambda e: e.activation(
                        out=qph[oc][:, ts * 512:(ts + 1) * 512],
                        in_=ps_q[k][:], func=AF.Copy))(), waits=ew)
                    psq_used[k] = P.ck("scalar")
            qready = P.ck("scalar")
            prev_qready = qready
            sxa = sx[h] if p == 0 else sy[h]
            ixa = ix[h] if p == 0 else iy[h]
            for t in range(NT):
                k = t % 2
                mw = (qready, keysT_ck) + \
                     ((pss_used[k],) if pss_used[k] else ())
                mck = None
                for dc in range(4):
                    mck = P.op("tensor", (lambda dc=dc, t=t, k=k:
                                          lambda e: e.matmul(
                        out=ps_s[k][:],
                        lhsT=qph[dc][:, t * 128:(t + 1) * 128],
                        rhs=keysT[:, dc * NKEYS:(dc + 1) * NKEYS],
                        start=(dc == 0), stop=(dc == 3)))(),
                        waits=mw if dc == 0 else (), inc=(1 if dc == 3 else 0))
                vw = (mck,) + ((m8_used,) if m8_used else ())
                P.op("vector", (lambda k=k: lambda e: e.max(
                    out=m8[:], in_=ps_s[k][:]))(), waits=vw)
                P.op("vector", (lambda k=k: lambda e: e.max_index(
                    out=i8[:], in_max=m8[:], in_values=ps_s[k][:]))(),
                    waits=(P.ck("vector"),))
                P.op("vector", lambda e: e.tensor_copy(i8f[:], i8[:]),
                     waits=(P.ck("vector"),))
                P.op("vector", (lambda sxa=sxa, t=t: lambda e: e.tensor_copy(
                    sxa[:, t:t + 1], m8[:, 0:1]))(), waits=(P.ck("vector"),))
                P.op("vector", (lambda ixa=ixa, t=t: lambda e: e.tensor_copy(
                    ixa[:, t:t + 1], i8f[:, 0:1]))(), waits=(P.ck("vector"),))
                m8_used = P.ck("vector")
                pss_used[k] = m8_used
            qph_used = m8_used
            keys_used = keysT_ck
    idxu = [P.sbuf(f"pk_idxu{h}", [128, NT], U32) for h in range(HEADS)]
    sc = [P.sbuf(f"pk_sc{h}", [128, NT], F32) for h in range(HEADS)]
    idx_cks = []
    sc_cks = []
    for h in range(HEADS):
        P.op("vector", (lambda h=h: lambda e: e.tensor_tensor(
            out=sc[h][:], in0=sx[h][:], in1=sy[h][:], op=ALU.add))(),
            waits=(P.ck("vector"),))
        P.op("vector", (lambda h=h: lambda e: e.tensor_scalar(
            out=sc[h][:], in0=sc[h][:], scalar1=0.0, scalar2=None,
            op0=ALU.max))(), waits=(P.ck("vector"),))
        sc_cks.append(P.ck("vector"))
        P.op("vector", (lambda h=h: lambda e: e.scalar_tensor_tensor(
            out=ix[h][:], in0=ix[h][:], scalar=float(NKEYS), in1=iy[h][:],
            op0=ALU.mult, op1=ALU.add))(), waits=(P.ck("vector"),))
        P.op("vector", (lambda h=h: lambda e: e.tensor_copy(
            idxu[h][:], ix[h][:]))(), waits=(P.ck("vector"),))
        idx_cks.append(P.ck("vector"))

    gat = [P.sbuf(f"pk_gat{i}", [128, D], F32) for i in range(4)]
    gat_ck = [None] * 4
    gat_used = [None] * 4
    hv = [P.sbuf(f"pk_hv{h}", [128, NT], F32) for h in range(HEADS)]
    dmp = P.sbuf("pk_dmp", [128, D], F16)
    gi = 0
    for t in range(NT):
        for h in range(HEADS):
            g = gi % 4
            gi += 1
            gw = (idx_cks[h],) + ((gat_used[g],) if gat_used[g] else ())
            gat_ck[g] = P.op("gpsimd", (lambda g=g, h=h, t=t:
                                        lambda e: e.indirect_dma_start(
                out=gat[g][:], out_offset=None, in_=wdown_hbm,
                in_offset=bass.IndirectOffsetOnAxis(
                    ap=idxu[h][:, t:t + 1], axis=0)))(), waits=gw, inc=16,
                stream=f"gat{g}", _is_dma=True)
            P.op("vector", (lambda g=g, h=h, t=t: lambda e:
                            e.scalar_tensor_tensor(
                out=dmp[:], in0=gat[g][:], scalar=1.0,
                in1=xh_tiles[t][:], op0=ALU.bypass, op1=ALU.mult,
                accum_out=hv[h][:, t:t + 1]))(),
                waits=(gat_ck[g], xh_cks[t]))
            gat_used[g] = P.ck("vector")
    wgt = hv
    for h in range(HEADS):
        P.op("scalar", (lambda h=h: lambda e: e.activation(
            out=sx[h][:], in_=hv[h][:], func=AF.Gelu))(),
            waits=(P.ck("vector"), sc_cks[h]))
        P.op("vector", (lambda h=h: lambda e: e.tensor_tensor(
            out=wgt[h][:], in0=sx[h][:], in1=sc[h][:], op=ALU.mult))(),
            waits=(P.ck("scalar"),))
    wgt_ck = P.ck("vector")

    x_s = [P.sbuf(f"pk_xr{i}", [128, D], F32) for i in range(2)]
    x_used = [None, None]
    gat_used = [None] * 4
    last_store = None
    for t in range(NT):
        k = t % 2
        xck = P.dma("sync", x_s[k][:], xres_in[t * 128:(t + 1) * 128, :],
                    waits=((x_used[k],) if x_used[k] else ()))
        rck = xck
        for h in range(HEADS):
            g = gi % 4
            gi += 1
            gw = (idx_cks[h],) + ((gat_used[g],) if gat_used[g] else ())
            gck = P.op("gpsimd", (lambda g=g, h=h, t=t:
                                  lambda e: e.indirect_dma_start(
                out=gat[g][:], out_offset=None, in_=wup_hbm,
                in_offset=bass.IndirectOffsetOnAxis(
                    ap=idxu[h][:, t:t + 1], axis=0)))(), waits=gw, inc=16,
                stream=f"gat{g}", _is_dma=True)
            P.op("vector", (lambda g=g, h=h, t=t, k=k: lambda e:
                            e.scalar_tensor_tensor(
                out=x_s[k][:], in0=gat[g][:], scalar=wgt[h][:, t:t + 1],
                in1=x_s[k][:], op0=ALU.mult, op1=ALU.add))(),
                waits=(gck, rck, wgt_ck))
            rck = P.ck("vector")
            gat_used[g] = rck
        last_store = P.dma("sync", xres_out[t * 128:(t + 1) * 128, :],
                           x_s[k][:], waits=(rck,))
        x_used[k] = last_store
    dcks = []
    if dbg is not None:
        for h in range(HEADS):
            dcks.append(P.dma("sync", dbg["ix"][h], ix[h][:], waits=(P.ck("vector"),)))
            dcks.append(P.dma("sync", dbg["sc"][h], sc[h][:], waits=(P.ck("vector"),)))
            dcks.append(P.dma("sync", dbg["wgt"][h], wgt[h][:], waits=(P.ck("vector"),)))
    P.wait("sync", (last_store, x_used[0], x_used[1]) + tuple(dcks))


"""L0_attn_b: RoPE + relu linear attention with rmsnorms."""
import numpy as np

NH = 16
HS = 64
RMS_EPS = 1.1920929e-07
EPS_LIN = 1e-06
THETA = 10000.0
O3 = 3 * D


def rope_tables_host(parity):
    pos = np.concatenate([np.arange(512) + 512 * (parity + 2 * i)
                          for i in range(4)])
    inv_freq = 1.0 / THETA ** (np.arange(0, HS, 2, dtype=np.float64) / HS)
    fr = np.outer(pos, inv_freq)
    cos = np.cos(fr).astype(np.float32)
    sin = np.sin(fr).astype(np.float32)
    return (np.tile(cos, (1, NH)).astype(np.float32),
            np.tile(sin, (1, NH)).astype(np.float32))


def emit_l0attnB_1(P, consts, tp, wqkv_hbm, bqkv_hbm, gq_hbm, gk_hbm,
                   cos_hbm, sin_hbm, xres_in, qfT_hbm, vk_hbm, dbg=None):
    xhatT, xt_ck = emit_ln_xhatT(P, tp, "ab", xres_in)
    wT = [P.sbuf(f"ab_wT{c}", [128, O3], BF16) for c in range(NC)]
    wrow = P.sbuf("ab_wrow", [128, D], BF16)
    prev_T = None
    for oc in range(O3 // 128):
        w = ((prev_T,) if prev_T else ())
        ldck = P.dma("gpsimd", wrow[:], wqkv_hbm[oc * 128:(oc + 1) * 128, :],
                     waits=w)
        for c in range(NC):
            prev_T = tp.t(wrow[:, c * 128:(c + 1) * 128],
                          wT[c][:, oc * 128:(oc + 1) * 128], waits=(ldck,))
    wT_ck = (P.ck("scalar"),)
    ps_p = [P.psum(f"ab_psp{i}", [128, 512], F32) for i in range(2)]
    bbc, bbc_ck = bcast_row(P, consts, bqkv_hbm, O3, "ab_bb", dtype=F32,
                            ps=ps_p[0])
    gqbc, gq_ck = bcast_row(P, consts, gq_hbm, D, "ab_gq", dtype=F32,
                            ps=ps_p[0], ps_free=(bbc_ck,))
    gkbc, gk_ck = bcast_row(P, consts, gk_hbm, D, "ab_gk", dtype=F32,
                            ps=ps_p[0], ps_free=(gq_ck,))
    cosS = [P.sbuf(f"ab_cosS{i}", [128, 512], F32) for i in range(2)]
    sinS = [P.sbuf(f"ab_sinS{i}", [128, 512], F32) for i in range(2)]
    rope_used = [None, None]

    qkv = P.sbuf("ab_qkv", [128, O3], F32)
    qf = P.sbuf("ab_qf", [128, D], BF16)
    kf = P.sbuf("ab_kf", [128, D], BF16)
    vp = P.sbuf("ab_vp", [128, NH * (HS + 1)], BF16)
    rr = P.sbuf("ab_rr", [128, D], F32)
    stt = P.sbuf("ab_stt", [128, 8], F32)
    dmp = P.sbuf("ab_dmp", [128, D], BF16)
    vk_sb = P.sbuf("ab_vksb", [64, NH * (HS + 1)], F32)
    ps_vk = [P.psum(f"ab_psvk{i}", [64, HS + 1], F32) for i in range(2)]
    stage = [P.sbuf(f"ab_stg{i}", [128, 128], BF16) for i in range(2)]
    epsc, epsck = P.ccol(RMS_EPS)
    P.op("vector", lambda e: e.memset(vk_sb[:], 0.0))
    vk_zck = P.ck("vector")
    vk_used = vk_zck
    psp_used = [gk_ck, None]
    psvk_used = [None, None]
    st_used = [None, None]
    qkv_used = None
    si = 0
    vki = 0
    for t in range(NT):
        rk = t % 2
        rw = ((rope_used[rk],) if rope_used[rk] else ())
        cos_ck = P.dma("sync", cosS[rk][:], cos_hbm[t * 128:(t + 1) * 128, :],
                       waits=rw)
        sin_ck = P.dma("sync", sinS[rk][:],
                       sin_hbm[t * 128:(t + 1) * 128, :], waits=rw)
        rope_ck = (cos_ck, sin_ck)
        for s in range(6):
            k = s % 2
            mw = wT_ck + tuple(xt_ck) + \
                 ((psp_used[k],) if psp_used[k] else ())
            mck = None
            for c in range(NC):
                mck = P.op("tensor", (lambda c=c, s=s, t=t, k=k:
                                      lambda e: e.matmul(
                    out=ps_p[k][:], lhsT=xhatT[c][:, t * 128:(t + 1) * 128],
                    rhs=wT[c][:, s * 512:(s + 1) * 512],
                    start=(c == 0), stop=(c == NC - 1)))(),
                    waits=mw if c == 0 else (), inc=(1 if c == NC - 1 else 0))
            ew = (mck, bbc_ck)
            P.op("vector", (lambda s=s, k=k: lambda e: e.tensor_tensor(
                out=qkv[:, s * 512:(s + 1) * 512], in0=ps_p[k][:],
                in1=bbc[:, s * 512:(s + 1) * 512], op=ALU.add))(), waits=ew)
            psp_used[k] = P.ck("vector")
        qkv_ck = P.ck("vector")
        if dbg is not None and t == 0:
            P.dma("sync", dbg["qkv"], qkv[:], waits=(qkv_ck,))
        for j, (g_bc, g_ck) in enumerate(((gqbc, gq_ck), (gkbc, gk_ck))):
            seg = qkv[:, j * D:(j + 1) * D]
            P.op("vector", (lambda seg=seg, j=j: lambda e:
                            e.scalar_tensor_tensor(
                out=dmp[:], in0=seg, scalar=1.0, in1=seg,
                op0=ALU.bypass, op1=ALU.mult,
                accum_out=stt[:, j:j + 1]))(), waits=(qkv_ck,))
            P.op("scalar", (lambda j=j: lambda e: e.activation(
                out=stt[:, 4 + j:5 + j], in_=stt[:, j:j + 1], func=AF.Sqrt,
                bias=epsc[:, 0:1], scale=1.0 / D))(),
                waits=(P.ck("vector"), epsck))
            P.op("vector", (lambda j=j: lambda e: e.reciprocal(
                out=stt[:, 4 + j:5 + j], in_=stt[:, 4 + j:5 + j]))(),
                waits=(P.ck("scalar"),))
            P.op("vector", (lambda seg=seg, j=j: lambda e: e.tensor_scalar(
                out=seg, in0=seg, scalar1=stt[:, 4 + j:5 + j], scalar2=None,
                op0=ALU.mult))(), waits=(P.ck("vector"),))
            P.op("vector", (lambda seg=seg, g_bc=g_bc: lambda e:
                            e.tensor_tensor(
                out=seg, in0=seg, in1=g_bc[:], op=ALU.mult))(),
                waits=(P.ck("vector"), g_ck))
        rms_ck = P.ck("vector")
        if dbg is not None and t == 0:
            P.dma("sync", dbg["rms"], qkv[:], waits=(rms_ck,))
        qf_war = st_used[1] if si > 0 else None
        for j, dst in enumerate((qf, kf)):
            seg = qkv[:, j * D:(j + 1) * D]
            ev = seg.rearrange("p (w two) -> p w two", two=2)[:, :, 0:1] \
                .rearrange("p w o -> p (w o)")
            od = seg.rearrange("p (w two) -> p w two", two=2)[:, :, 1:2] \
                .rearrange("p w o -> p (w o)")
            r_ev = rr[:, 0:512]
            r_od = rr[:, 512:1024]
            P.op("vector", (lambda ev=ev, rk=rk, r_ev=r_ev: lambda e:
                            e.tensor_tensor(out=r_ev, in0=ev,
                                            in1=cosS[rk][:], op=ALU.mult))(),
                 waits=(rms_ck, P.ck("vector"), st_used[0],
                        st_used[1]) + rope_ck)
            P.op("vector", (lambda od=od, rk=rk, r_od=r_od: lambda e:
                            e.tensor_tensor(out=r_od, in0=od,
                                            in1=cosS[rk][:], op=ALU.mult))())
            dm1 = dmp[:, 0:512]
            P.op("vector", (lambda od=od, rk=rk, dm1=dm1: lambda e:
                            e.tensor_tensor(out=dm1, in0=od,
                                            in1=sinS[rk][:], op=ALU.mult))())
            P.op("vector", (lambda r_ev=r_ev, dm1=dm1: lambda e:
                            e.tensor_tensor(out=r_ev, in0=r_ev, in1=dm1,
                                            op=ALU.subtract))(),
                 waits=(P.ck("vector"),))
            P.op("vector", (lambda ev=ev, rk=rk, dm1=dm1: lambda e:
                            e.tensor_tensor(out=dm1, in0=ev,
                                            in1=sinS[rk][:], op=ALU.mult))(),
                 waits=(P.ck("vector"),))
            P.op("vector", (lambda r_od=r_od, dm1=dm1: lambda e:
                            e.tensor_tensor(out=r_od, in0=r_od, in1=dm1,
                                            op=ALU.add))(),
                 waits=(P.ck("vector"),))
            dev = dst[:].rearrange("p (w two) -> p w two", two=2)[:, :, 0:1] \
                .rearrange("p w o -> p (w o)")
            dod = dst[:].rearrange("p (w two) -> p w two", two=2)[:, :, 1:2] \
                .rearrange("p w o -> p (w o)")
            P.op("vector", (lambda dev=dev, r_ev=r_ev: lambda e:
                            e.tensor_scalar(out=dev, in0=r_ev, scalar1=0.0,
                                            scalar2=None, op0=ALU.max))(),
                 waits=(P.ck("vector"),))
            P.op("vector", (lambda dod=dod, r_od=r_od: lambda e:
                            e.tensor_scalar(out=dod, in0=r_od, scalar1=0.0,
                                            scalar2=None, op0=ALU.max))(),
                 waits=(P.ck("vector"),))
        qfkf_ck = P.ck("vector")
        rope_used[rk] = qfkf_ck
        if dbg is not None and t == 0:
            P.dma("sync", dbg["qf"], qf[:], waits=(qfkf_ck,))
            P.dma("sync", dbg["kf"], kf[:], waits=(qfkf_ck,))
        P.op("vector", (lambda t=t: lambda e: e.tensor_copy(
            vp[:].rearrange("p (h f) -> p h f", f=HS + 1)[:, :, 0:HS],
            qkv[:, 2 * D:3 * D].rearrange("p (h f) -> p h f", f=HS)))(),
            waits=(qkv_ck, P.ck("vector")))
        P.op("vector", lambda e: e.memset(
            vp[:].rearrange("p (h f) -> p h f", f=HS + 1)[:, :, HS:HS + 1],
            1.0), waits=(P.ck("vector"),))
        vp_ck = P.ck("vector")
        for h in range(NH):
            kv = vki % 2
            vki += 1
            mw = (qfkf_ck, vp_ck) + \
                 ((psvk_used[kv],) if psvk_used[kv] else ())
            mck = P.op("tensor", (lambda h=h, kv=kv: lambda e: e.matmul(
                out=ps_vk[kv][:], lhsT=kf[:, h * HS:(h + 1) * HS],
                rhs=vp[:, h * (HS + 1):(h + 1) * (HS + 1)],
                start=True, stop=True))(), waits=mw)
            P.op("vector", (lambda h=h, kv=kv: lambda e: e.tensor_tensor(
                out=vk_sb[:, h * (HS + 1):(h + 1) * (HS + 1)],
                in0=vk_sb[:, h * (HS + 1):(h + 1) * (HS + 1)],
                in1=ps_vk[kv][:], op=ALU.add))(),
                waits=(mck, vk_used))
            vk_used = P.ck("vector")
            psvk_used[kv] = vk_used
        for c in range(NC):
            k = si % 2
            si += 1
            ew = (qfkf_ck,) + ((st_used[k],) if st_used[k] else ())
            eck = tp.t(qf[:, c * 128:(c + 1) * 128], stage[k][:], waits=ew)
            st_used[k] = P.dma("sync",
                               qfT_hbm[c * 128:(c + 1) * 128,
                                       t * 128:(t + 1) * 128],
                               stage[k][:], waits=(eck,))
        qkv_used = None
    vkst = P.dma("sync", vk_hbm, vk_sb[:], waits=(vk_used,))
    P.wait("sync", (vkst,) + tuple(st_used))


def emit_l0attnB_2(P, consts, tp, qfT_hbm, vk_own_hbm, vk_pair_hbm,
                   wout_hbm, bout_hbm, gout_hbm, xres_in, xres_out):
    qfT_h = [P.sbuf(f"b2_qf{h}", [HS, T], BF16) for h in range(NH)]
    qf_cks = [P.dma("sync", qfT_h[h][:], qfT_hbm[h * HS:(h + 1) * HS, :])
              for h in range(NH)]
    vko = P.sbuf("b2_vko", [64, NH * (HS + 1)], F32)
    vkp = P.sbuf("b2_vkp", [64, NH * (HS + 1)], F32)
    vkb = P.sbuf("b2_vkb", [64, NH * (HS + 1)], BF16)
    c1 = P.dma("sync", vko[:], vk_own_hbm)
    c2 = P.dma("sync", vkp[:], vk_pair_hbm)
    P.op("vector", lambda e: e.tensor_tensor(
        out=vko[:], in0=vko[:], in1=vkp[:], op=ALU.add), waits=(c1, c2))
    P.op("vector", lambda e: e.tensor_copy(vkb[:], vko[:]),
         waits=(P.ck("vector"),))
    vk_ck = P.ck("vector")
    woT = [P.sbuf(f"b2_woT{c}", [128, D], BF16) for c in range(NC)]
    wrow = P.sbuf("b2_wrow", [128, D], BF16)
    prev_T = None
    for oc in range(NC):
        w = ((prev_T,) if prev_T else ())
        ldck = P.dma("gpsimd", wrow[:], wout_hbm[oc * 128:(oc + 1) * 128, :],
                     waits=w)
        for c in range(NC):
            prev_T = tp.t(wrow[:, c * 128:(c + 1) * 128],
                          woT[c][:, oc * 128:(oc + 1) * 128], waits=(ldck,))
    woT_ck = (P.ck("scalar"),)
    bout_bc, bout_ck = bcast_row(P, consts, bout_hbm, D, "b2_bo", dtype=F32)
    gout_bc, gout_ck = bcast_row(P, consts, gout_hbm, D, "b2_go", dtype=F32)
    epsc, epsck = P.ccol(RMS_EPS)
    attn = P.sbuf("b2_attn", [128, D], BF16)
    attnT = P.sbuf("b2_attnT", [128, D], BF16)
    res = P.sbuf("b2_res", [128, NH * (HS + 1)], F32)
    z = P.sbuf("b2_z", [128, D], F32)
    dmp = P.sbuf("b2_dmp", [128, D], BF16)
    stt = P.sbuf("b2_stt", [128, 4], F32)
    x_s = [P.sbuf(f"b2_x{i}", [128, D], F32) for i in range(2)]
    ps_r = [P.psum(f"b2_psr{i}", [128, HS + 1], F32) for i in range(2)]
    ps_w = [P.psum(f"b2_psw{i}", [128, 512], F32) for i in range(2)]
    x_used = [None, None]
    psr_used = [None, None]
    psw_used = [None, None]
    attnT_used = None
    res_used = None
    last_store = None
    ri = 0
    for t in range(NT):
        k = t % 2
        xck = P.dma("sync", x_s[k][:], xres_in[t * 128:(t + 1) * 128, :],
                    waits=((x_used[k],) if x_used[k] else ()))
        for h in range(NH):
            kr = ri % 2
            ri += 1
            mw = (qf_cks[h], vk_ck) + \
                 ((psr_used[kr],) if psr_used[kr] else ())
            mck = P.op("tensor", (lambda h=h, t=t, kr=kr: lambda e: e.matmul(
                out=ps_r[kr][:], lhsT=qfT_h[h][:, t * 128:(t + 1) * 128],
                rhs=vkb[:, h * (HS + 1):(h + 1) * (HS + 1)],
                start=True, stop=True))(), waits=mw)
            P.op("vector", (lambda h=h, kr=kr: lambda e: e.tensor_copy(
                res[:, h * (HS + 1):(h + 1) * (HS + 1)], ps_r[kr][:]))(),
                waits=(mck,) + ((res_used,) if res_used and h == 0 else ()))
            psr_used[kr] = P.ck("vector")
        res_ck = P.ck("vector")
        for h in range(NH):
            P.op("vector", (lambda h=h: lambda e: e.tensor_scalar(
                out=stt[:, 0:1], in0=res[:, h * 65 + 64:h * 65 + 65],
                scalar1=float(EPS_LIN), scalar2=None, op0=ALU.add))(),
                waits=(res_ck, P.ck("vector")))
            P.op("vector", lambda e: e.reciprocal(
                out=stt[:, 1:2], in_=stt[:, 0:1]), waits=(P.ck("vector"),))
            P.op("vector", (lambda h=h: lambda e: e.tensor_scalar(
                out=attn[:, h * 64:(h + 1) * 64],
                in0=res[:, h * 65:h * 65 + 64],
                scalar1=stt[:, 1:2], scalar2=None, op0=ALU.mult))(),
                waits=(P.ck("vector"),))
        attn_ck = P.ck("vector")
        res_used = attn_ck
        tcks = []
        for c in range(NC):
            ew = (attn_ck,) + ((attnT_used,) if attnT_used and c == 0 else ())
            eck = tp.t(attn[:, c * 128:(c + 1) * 128],
                       attnT[:, c * 128:(c + 1) * 128], waits=ew)
            tcks.append(eck)
        for half in range(2):
            mw = woT_ck + (tcks[-1],) + \
                 ((psw_used[half],) if psw_used[half] else ())
            mck = None
            for c in range(NC):
                mck = P.op("tensor", (lambda c=c, half=half:
                                      lambda e: e.matmul(
                    out=ps_w[half][:], lhsT=attnT[:, c * 128:(c + 1) * 128],
                    rhs=woT[c][:, half * 512:(half + 1) * 512],
                    start=(c == 0), stop=(c == NC - 1)))(),
                    waits=mw if c == 0 else (), inc=(1 if c == NC - 1 else 0))
            P.op("vector", (lambda half=half: lambda e: e.tensor_tensor(
                out=z[:, half * 512:(half + 1) * 512], in0=ps_w[half][:],
                in1=bout_bc[:, half * 512:(half + 1) * 512], op=ALU.add))(),
                waits=(mck, bout_ck))
            psw_used[half] = P.ck("vector")
        attnT_used = P.ck("tensor")
        z_ck = P.ck("vector")
        P.op("vector", lambda e: e.scalar_tensor_tensor(
            out=dmp[:], in0=z[:], scalar=1.0, in1=z[:],
            op0=ALU.bypass, op1=ALU.mult, accum_out=stt[:, 2:3]),
            waits=(z_ck,))
        P.op("scalar", lambda e: e.activation(
            out=stt[:, 3:4], in_=stt[:, 2:3], func=AF.Sqrt,
            bias=epsc[:, 0:1], scale=1.0 / D),
            waits=(P.ck("vector"), epsck))
        P.op("vector", lambda e: e.reciprocal(
            out=stt[:, 3:4], in_=stt[:, 3:4]), waits=(P.ck("scalar"),))
        P.op("vector", lambda e: e.tensor_scalar(
            out=z[:], in0=z[:], scalar1=stt[:, 3:4], scalar2=None,
            op0=ALU.mult), waits=(P.ck("vector"),))
        P.op("vector", (lambda: lambda e: e.tensor_tensor(
            out=z[:], in0=z[:], in1=gout_bc[:], op=ALU.mult))(),
            waits=(P.ck("vector"), gout_ck))
        P.op("vector", (lambda k=k: lambda e: e.tensor_tensor(
            out=x_s[k][:], in0=x_s[k][:], in1=z[:], op=ALU.add))(),
            waits=(P.ck("vector"), xck))
        last_store = P.dma("sync", xres_out[t * 128:(t + 1) * 128, :],
                           x_s[k][:], waits=(P.ck("vector"),))
        x_used[k] = last_store
    P.wait("sync", (x_used[0], x_used[1]))


"""L1_attn_a: causal masked-score attention (no softmax)."""
import numpy as np

NH = 16
HS = 64
LFULL = 4096


def causal_masks_host():
    import ml_dtypes
    m = np.zeros((128, 4, 512), np.float32)
    for j in range(4):
        for kk in range(128):
            m[kk, j, :] = (np.arange(512) >= kk + 128 * j)
    return m.reshape(128, 2048).astype(ml_dtypes.bfloat16)


def emit_qkv_heads(P, consts, tp, xres_in, w_hbms, b_hbms, out_hbms,
                   vt_idx=None, vt_hbm=None, scale=1.0, scale_idx=None):
    xhatT, xt_ck = emit_ln_xhatT(P, tp, "qk", xres_in)
    tst = [P.sbuf(f"qk_tst{i}", [128, 128], BF16) for i in range(2)]
    tst_ck = [None, None]
    state = {"si": 0}
    all_store_cks = []
    for i, (w_hbm, b_hbm, out_hbm) in enumerate(zip(w_hbms, b_hbms, out_hbms)):
        if b_hbm is not None:
            bias_col, bias_ck = load_bias_cols(P, tp, b_hbm, D, f"qk_b{i}")
        else:
            bias_col, bias_ck = None, None
        sc = scale if (scale_idx is None or i in scale_idx) else 1.0

        def out_cb(oc, ts, src_ap, ck, i=i, out_hbm=out_hbm):
            dck = P.dma("sync", out_hbm[oc * 128:(oc + 1) * 128,
                                        ts * 512:(ts + 1) * 512],
                        src_ap, waits=(ck,))
            all_store_cks.append(dck)
            if vt_idx is not None and i == vt_idx:
                for j in range(4):
                    k = state["si"] % 2
                    state["si"] += 1
                    ew = (ck,) + ((tst_ck[k],) if tst_ck[k] else ())
                    eck = tp.t(src_ap[:, j * 128:(j + 1) * 128],
                               tst[k][:], waits=ew)
                    tst_ck[k] = P.dma("sync",
                                      vt_hbm[ts * 512 + j * 128:
                                             ts * 512 + (j + 1) * 128,
                                             oc * 128:(oc + 1) * 128],
                                      tst[k][:], waits=(eck,))
                    all_store_cks.append(tst_ck[k])
            return dck

        emit_proj_rowsO(P, tp, f"qk_p{i}", xhatT, xt_ck, w_hbm, bias_col,
                        bias_ck, out_cb, scale=sc)
    P.wait("sync", tuple(all_store_cks))


def emit_l1attnA_core(P, consts, tp, parity, q_hbm, k_hbm, vt_hbm, cmask_hbm,
                      wo_hbm, wob_hbm, xres_in, xres_out):
    q_sb = [P.sbuf(f"aa_q{h}", [64, T], BF16) for h in range(NH)]
    q_cks = [P.dma("sync", q_sb[h][:], q_hbm[h * HS:(h + 1) * HS, :])
             for h in range(NH)]
    cmask = P.sbuf("aa_cmask", [128, 2048], BF16)
    cm_ck = P.dma("sync", cmask[:], cmask_hbm)
    attn = [P.sbuf(f"aa_at{c}", [128, T], BF16) for c in range(NC)]
    kh = [P.sbuf(f"aa_kh{i}", [64, LFULL], BF16) for i in range(2)]
    vth = [P.sbuf(f"aa_vth{i}", [128, 32 * 64], BF16) for i in range(2)]
    s_bf = [P.sbuf(f"aa_sbf{i}", [128, 512], BF16) for i in range(2)]
    ps_s = [P.psum(f"aa_pss{i}", [128, 512], F32) for i in range(2)]
    ps_o = [P.psum(f"aa_pso{i}", [64, 512], F32) for i in range(2)]
    kh_ck = [None, None]
    kh_used = [None, None]
    vth_ck = [None, None]
    vth_used = [None, None]
    s_ck = [None, None]
    s_used = [None, None]
    pss_used = [None, None]
    pso_used = [None, None]
    si = 0
    oi = 0
    vt3 = [v[:].rearrange("p (kc f) -> p kc f", f=64) for v in vth]
    for h in range(NH):
        hk = h % 2
        w = ((kh_used[hk],) if kh_used[hk] else ())
        kh_ck[hk] = P.dma("gpsimd", kh[hk][:],
                          k_hbm[h * HS:(h + 1) * HS, :], waits=w)
        w = ((vth_used[vk],) if vth_used[(vk := h % 2)] else ())
        vth_ck[hk] = P.dma("gpsimd", vt3[hk],
                           vt_hbm.rearrange("(kc p) f -> p kc f", p=128)
                           [:, :, h * 64:(h + 1) * 64], waits=w)
        hc, hr = h // 2, (h % 2) * 64
        for bq in range(4):
            gb = parity + 2 * bq
            ko = oi % 2
            oi += 1
            nkc = 4 * (gb + 1)
            last_pv = None
            for kc in range(nkc):
                k = si % 2
                si += 1
                mw = (kh_ck[hk], q_cks[h]) + \
                     ((pss_used[k],) if pss_used[k] else ())
                mck = P.op("tensor", (lambda hk=hk, kc=kc, bq=bq, k=k, h=h:
                                      lambda e: e.matmul(
                    out=ps_s[k][:], lhsT=kh[hk][:, kc * 128:(kc + 1) * 128],
                    rhs=q_sb[h][:, bq * 512:(bq + 1) * 512],
                    start=True, stop=True))(), waits=mw)
                ew = (mck,) + ((s_used[k],) if s_used[k] else ())
                j = kc - 4 * gb
                if j >= 0:
                    P.op("vector", (lambda k=k, j=j: lambda e: e.tensor_tensor(
                        out=s_bf[k][:], in0=ps_s[k][:],
                        in1=cmask[:, j * 512:(j + 1) * 512], op=ALU.mult))(),
                        waits=ew + (cm_ck,))
                    s_ck[k] = P.ck("vector")
                    pss_used[k] = s_ck[k]
                else:
                    P.op("scalar", (lambda k=k: lambda e: e.activation(
                        out=s_bf[k][:], in_=ps_s[k][:], func=AF.Copy))(),
                        waits=ew)
                    s_ck[k] = P.ck("scalar")
                    pss_used[k] = s_ck[k]
                pw = (s_ck[k], vth_ck[hk]) + \
                     (((pso_used[ko],) if pso_used[ko] else ()) if kc == 0 else ())
                pvck = P.op("tensor", (lambda hk=hk, kc=kc, k=k, ko=ko,
                                       nkc=nkc: lambda e: e.matmul(
                    out=ps_o[ko][:], lhsT=vt3[hk][:, kc, :],
                    rhs=s_bf[k][:], start=(kc == 0),
                    stop=(kc == nkc - 1)))(), waits=pw, inc=1)
                s_used[k] = pvck
                last_pv = pvck
            P.op("scalar", (lambda hc=hc, hr=hr, bq=bq, ko=ko:
                            lambda e: e.activation(
                out=attn[hc][hr:hr + 64, bq * 512:(bq + 1) * 512],
                in_=ps_o[ko][:], func=AF.Copy))(), waits=(last_pv,))
            pso_used[ko] = P.ck("scalar")
        kh_used[hk] = last_pv
        vth_used[hk] = last_pv
    attn_ck = (P.ck("scalar"), P.ck("vector"))

    woT = [P.sbuf(f"aa_woT{c}", [128, D], BF16) for c in range(NC)]
    wrow = P.sbuf("aa_wrow", [128, D], BF16)
    prev_T = None
    for oc in range(NC):
        w = ((prev_T,) if prev_T else ())
        ldck = P.dma("gpsimd", wrow[:], wo_hbm[oc * 128:(oc + 1) * 128, :],
                     waits=w)
        for c in range(NC):
            prev_T = tp.t(wrow[:, c * 128:(c + 1) * 128],
                          woT[c][:, oc * 128:(oc + 1) * 128], waits=(ldck,))
    woT_ck = (P.ck("scalar"),)
    wob_bc, wob_ck = bcast_row(P, consts, wob_hbm, D, "aa_wob", dtype=F32,
                               ps=ps_s[1], ps_free=(pss_used[1],))
    x_s = [P.sbuf(f"aa_x{i}", [128, D], F32) for i in range(2)]
    ps_w = ps_s
    psw_used = [pss_used[0], wob_ck]
    x_used = [None, None]
    last_store = None
    for t in range(NT):
        k = t % 2
        xck = P.dma("sync", x_s[k][:], xres_in[t * 128:(t + 1) * 128, :],
                    waits=((x_used[k],) if x_used[k] else ()))
        for half in range(2):
            mw = woT_ck + attn_ck + \
                 ((psw_used[half],) if psw_used[half] else ())
            mck = None
            for c in range(NC):
                mck = P.op("tensor", (lambda c=c, t=t, half=half:
                                      lambda e: e.matmul(
                    out=ps_w[half][:],
                    lhsT=attn[c][:, t * 128:(t + 1) * 128],
                    rhs=woT[c][:, half * 512:(half + 1) * 512],
                    start=(c == 0), stop=(c == NC - 1)))(),
                    waits=mw if c == 0 else (), inc=(1 if c == NC - 1 else 0))
            P.op("vector", (lambda k=k, half=half: lambda e: e.tensor_tensor(
                out=x_s[k][:, half * 512:(half + 1) * 512],
                in0=x_s[k][:, half * 512:(half + 1) * 512],
                in1=ps_w[half][:], op=ALU.add))(),
                waits=(mck, xck, wob_ck))
            P.op("vector", (lambda k=k, half=half: lambda e: e.tensor_tensor(
                out=x_s[k][:, half * 512:(half + 1) * 512],
                in0=x_s[k][:, half * 512:(half + 1) * 512],
                in1=wob_bc[:, half * 512:(half + 1) * 512], op=ALU.add))())
            psw_used[half] = P.ck("vector")
        last_store = P.dma("sync", xres_out[t * 128:(t + 1) * 128, :],
                           x_s[k][:], waits=(P.ck("vector"),))
        x_used[k] = last_store
    P.wait("sync", (x_used[0], x_used[1]))


"""L1_ffn (fp4 quant-dequant SwiGLU) emitters."""
import numpy as np

I_FF = 2816
N_SL = 6
SLICES = [(s * 512, min(I_FF, (s + 1) * 512)) for s in range(N_SL)]
NCH_FF = I_FF // 128


def emit_l1ffn_wprep(P, consts, tp, w_gate, w_up, w_down, o_gT, o_uT, o_dT,
                     gu_rows, d_rows):
    scr = {
        "mx": P.sbuf("wq_mx", [128, I_FF // 16], F32),
        "inv": P.sbuf("wq_inv", [128, I_FF // 16], F32),
        "a": P.sbuf("wq_a", [128, I_FF], BF16),
        "q": P.sbuf("wq_q", [128, I_FF], BF16),
        "sgn": P.sbuf("wq_sgn", [128, I_FF], BF16),
    }
    wt = P.sbuf("wq_wt", [128, I_FF], BF16)
    wq = P.sbuf("wq_wq", [128, I_FF], BF16)
    stage = [P.sbuf(f"wq_st{i}", [128, 128], BF16) for i in range(2)]
    st_ck = [None, None]
    si = 0
    prev = None
    for (w_hbm, o_hbm, rows, width) in ((w_gate, o_gT, gu_rows, D),
                                        (w_up, o_uT, gu_rows, D),
                                        (w_down, o_dT, d_rows, I_FF)):
        for r0 in range(0, rows, 128):
            r1 = min(rows, r0 + 128)
            nr = r1 - r0
            w = ((prev,) if prev else ())
            ldck = P.dma("gpsimd", wt[0:nr, 0:width], w_hbm[r0:r1, :],
                         waits=w)
            qck = emit_quant(P, wt[0:nr, 0:width], wq[0:nr, 0:width], scr,
                             width, waits=(ldck,))
            prev = qck
            for c0 in range(0, width, 128):
                k = si % 2
                si += 1
                ew = (qck,) + ((st_ck[k],) if st_ck[k] else ())
                eck = tp.t(wq[0:nr, c0:c0 + 128], stage[k][0:128, 0:nr],
                           waits=ew)
                st_ck[k] = P.dma("sync", o_hbm[c0:c0 + 128, r0:r1],
                                 stage[k][0:128, 0:nr], waits=(eck,))
    P.op("sync", lambda e: e.nop(), waits=[c for c in st_ck if c], inc=0)


def emit_l1ffn(P, consts, tp, xres_in, wgT_h, wuT_h, wdT_h, xres_out,
               group=4):
    x_s = [P.sbuf(f"f1_x{i}", [128, D], F32) for i in range(2)]
    xhat = P.sbuf("f1_xh", [128, D], BF16)
    dump = P.sbuf("f1_dump", [128, D], BF16)
    stat = P.sbuf("f1_stat", [128, 4 * NT], F32)
    used = [None, None]
    for t in range(NT):
        k = t % 2
        w = ((used[k],) if used[k] else ())
        ck = P.dma("sync", x_s[k][:], xres_in[t * 128:(t + 1) * 128, :],
                   waits=w)
        used[k] = emit_ln_stats_tile(P, x_s[k][:], dump[:], stat, t,
                                     waits=(ck,))
    ln_ck = emit_ln_finalize(P, stat)
    qscr = {
        "mx": P.sbuf("f1_qmx", [128, I_FF // 16], F32),
        "inv": P.sbuf("f1_qinv", [128, I_FF // 16], F32),
        "a": P.sbuf("f1_qa", [128, I_FF], BF16),
        "q": P.sbuf("f1_qq", [128, I_FF], BF16),
        "sgn": P.sbuf("f1_qsgn", [128, I_FF], BF16),
    }
    xq = P.sbuf("f1_xq", [128, D], BF16)
    xqT = [P.sbuf(f"f1_xqT{c}", [128, T], BF16) for c in range(NC)]
    prev_q = None
    for t in range(NT):
        k = t % 2
        w = ((used[k],) if used[k] else ())
        ck = P.dma("sync", x_s[k][:], xres_in[t * 128:(t + 1) * 128, :],
                   waits=w)
        ack = ln_affine(P, x_s[k][:], xhat[:], stat, t,
                        waits=(ck, ln_ck) + ((prev_q,) if prev_q else ()))
        used[k] = ack
        qck = emit_quant(P, xhat[:], xq[:], qscr, D, waits=(ack,))
        tcks = []
        for c in range(NC):
            eck = tp.t(xq[:, c * 128:(c + 1) * 128],
                       xqT[c][:, t * 128:(t + 1) * 128], waits=(qck,))
            tcks.append(eck)
        prev_q = tcks[-1]
    xqT_ready = (P.ck("scalar"),)

    wg_s = [P.sbuf(f"f1_wg{i}", [128, 512], BF16) for i in range(NC)]
    wu_s = [P.sbuf(f"f1_wu{i}", [128, 512], BF16) for i in range(NC)]
    wd_res = [P.sbuf(f"f1_wd{j}", [128, D], BF16) for j in range(NCH_FF)]
    wd_cks = [P.dma("gpsimd", wd_res[j][:], wdT_h[j * 128:(j + 1) * 128, :])
              for j in range(NCH_FF)]
    h_g = [P.sbuf(f"f1_h{i}", [128, I_FF], BF16) for i in range(group)]
    sg = P.sbuf("f1_sg", [128, 512], F32)
    hq = P.sbuf("f1_hq", [128, I_FF], BF16)
    hqT = P.sbuf("f1_hqT", [128, I_FF], BF16)
    ps_g = P.psum("f1_psg", [128, 512], F32)
    ps_u = P.psum("f1_psu", [128, 512], F32)
    ps_d = [P.psum(f"f1_psd{i}", [128, 512], F32) for i in range(2)]
    hq_scr = {
        "mx": qscr["mx"], "inv": qscr["inv"], "a": qscr["a"],
        "q": qscr["q"], "sgn": qscr["sgn"],
    }
    prev_slice_w = None
    prev_hq = None
    prev_sck = None
    prev_hck = None
    prev_resd = [None, None]
    last_store = None
    for g0 in range(0, NT, group):
        gtiles = range(g0, min(NT, g0 + group))
        for s, (o0, o1) in enumerate(SLICES):
            ow = o1 - o0
            lw = []
            for c in range(NC):
                w = (prev_slice_w,) if prev_slice_w else ()
                lw.append(P.dma("gpsimd", wg_s[c][:, 0:ow],
                                wgT_h[c * 128:(c + 1) * 128, o0:o1], waits=w))
                lw.append(P.dma("gpsimd", wu_s[c][:, 0:ow],
                                wuT_h[c * 128:(c + 1) * 128, o0:o1], waits=w))
            for t in gtiles:
                ti = t - g0
                gw = tuple(lw) + xqT_ready + ((prev_sck,) if prev_sck else ())
                ck_g = mm_accum(P, ps_g[:, 0:ow],
                                [(xqT[c][:, t * 128:(t + 1) * 128],
                                  wg_s[c][:, 0:ow]) for c in range(NC)],
                                waits=gw)
                uw = ((prev_hck,) if prev_hck else ())
                ck_u = mm_accum(P, ps_u[:, 0:ow],
                                [(xqT[c][:, t * 128:(t + 1) * 128],
                                  wu_s[c][:, 0:ow]) for c in range(NC)],
                                waits=uw)
                sw = (ck_g,) + ((prev_hck,) if prev_hck else ())
                sck = P.op("scalar", (lambda ow=ow: lambda e: e.activation(
                    out=sg[:, 0:ow], in_=ps_g[:, 0:ow], func=AF.Silu))(),
                    waits=sw)
                hck = P.op("vector", (lambda ti=ti, o0=o0, o1=o1, ow=ow:
                                      lambda e: e.tensor_tensor(
                    out=h_g[ti][:, o0:o1], in0=sg[:, 0:ow],
                    in1=ps_u[:, 0:ow], op=ALU.mult))(), waits=(sck, ck_u))
                prev_sck, prev_hck = sck, hck
            prev_slice_w = prev_hck
        for t in gtiles:
            ti = t - g0
            k = t % 2
            xw = ((used[k],) if used[k] else ())
            xck = P.dma("sync", x_s[k][:], xres_in[t * 128:(t + 1) * 128, :],
                        waits=xw)
            w = ((prev_hq,) if prev_hq else ()) + (prev_hck,)
            qck = emit_quant(P, h_g[ti][:], hq[:], hq_scr, I_FF, waits=w)
            tcks = []
            for j in range(NCH_FF):
                eck = tp.t(hq[:, j * 128:(j + 1) * 128],
                           hqT[:, j * 128:(j + 1) * 128], waits=(qck,))
                tcks.append(eck)
            prev_hq = tcks[-1]
            for half in range(2):
                dw = (tcks[-1],) + tuple(wd_cks) + \
                     ((prev_resd[half],) if prev_resd[half] else ())
                ck_d = mm_accum(
                    P, ps_d[half][:],
                    [(hqT[:, j * 128:(j + 1) * 128],
                      wd_res[j][:, half * 512:(half + 1) * 512])
                     for j in range(NCH_FF)],
                    waits=dw)
                P.op("vector", (lambda k=k, half=half: lambda e: e.tensor_tensor(
                    out=x_s[k][:, half * 512:(half + 1) * 512],
                    in0=x_s[k][:, half * 512:(half + 1) * 512],
                    in1=ps_d[half][:], op=ALU.add))(), waits=(ck_d, xck))
                prev_resd[half] = P.ck("vector")
            rck = P.ck("vector")
            last_store = P.dma("sync", xres_out[t * 128:(t + 1) * 128, :],
                               x_s[k][:], waits=(rck,))
            used[k] = last_store
    P.wait("sync", (last_store, used[0], used[1]))


"""L1_attn_b: top-64-threshold softmax attention."""
import numpy as np

NH = 16
HS = 64
LF = 4096
NKC = LF // 128
NG = 4


def emit_l1attnB_1(P, consts, tp, w_hbms, b_hbms, out_hbms, vt_hbm,
                   xres_in):
    xhatT, xt_ck = emit_ln_xhatT(P, tp, "eb", xres_in)
    w_tiles = [P.sbuf(f"eb_w{c}", [128, D], BF16) for c in range(NC)]
    tst = [P.sbuf(f"eb_tst{i}", [128, 128], BF16) for i in range(2)]
    tst_ck = [None, None]
    state = {"si": 0}
    prev_w = [None] * NC
    prev_proj = None
    all_stores = []
    for i, (w_hbm, b_hbm, out_hbm, scale) in enumerate(
            zip(w_hbms, b_hbms, out_hbms, (0.125, 1.0, 1.0))):
        w_cks = []
        for c in range(NC):
            w = ((prev_w[c],) if prev_w[c] else ()) + \
                ((prev_proj,) if prev_proj else ())
            prev_w[c] = P.dma("gpsimd", w_tiles[c][:],
                              w_hbm[c * 128:(c + 1) * 128, :], waits=w)
            w_cks.append(prev_w[c])
        bias_col, bias_ck = load_bias_cols(P, tp, b_hbm, D, f"eb_b{i}")
        if scale != 1.0:
            P.op("vector", (lambda bias_col=bias_col, scale=scale:
                            lambda e: e.tensor_scalar(
                out=bias_col[:], in0=bias_col[:], scalar1=float(scale),
                scalar2=None, op0=ALU.mult))(), waits=(bias_ck,))
            bias_ck = P.ck("vector")

        def out_cb(oc, ts, src_ap, ck, i=i, out_hbm=out_hbm):
            dcks = []
            if out_hbm is not None:
                dck = P.dma("sync", out_hbm[oc * 128:(oc + 1) * 128,
                                            ts * 512:(ts + 1) * 512],
                            src_ap, waits=(ck,))
                all_stores.append(dck)
                dcks.append(dck)
            if i == 2:
                for j in range(4):
                    k = state["si"] % 2
                    state["si"] += 1
                    ew = (ck,) + ((tst_ck[k],) if tst_ck[k] else ())
                    eck = tp.t(src_ap[:, j * 128:(j + 1) * 128],
                               tst[k][:], waits=ew)
                    tst_ck[k] = P.dma("sync",
                                      vt_hbm[ts * 512 + j * 128:
                                             ts * 512 + (j + 1) * 128,
                                             oc * 128:(oc + 1) * 128],
                                      tst[k][:], waits=(eck,))
                    all_stores.append(tst_ck[k])
                    dcks.append(tst_ck[k])
            return tuple(dcks)

        prev_proj = emit_proj_rowsD(P, f"eb_p{i}", xhatT, xt_ck, w_tiles,
                                    tuple(w_cks), bias_col, bias_ck, out_cb,
                                    scale=scale)
    P.wait("sync", tuple(all_stores))


def emit_l1attnB_2(P, consts, tp, q_hbm, kf_hbm, vtf_hbm, wo_hbm, bo_hbm,
                   xres_in, xres_out):
    qh = [P.sbuf(f"e2_q{i}", [HS, T], BF16) for i in range(2)]
    qh_ck = [None, None]
    qh_used = [(), ()]
    kaug = [P.sbuf(f"e2_ka{i}", [66, LF], BF16) for i in range(2)]
    ka_ck = [None, None]
    ka_used = [None, None]
    neg_init = []
    for i in range(2):
        P.op("vector", (lambda i=i: lambda e: e.memset(
            kaug[i][64:66, :], -1.0))())
        neg_init.append(P.ck("vector"))
    vth = [P.sbuf(f"e2_vt{i}", [128, NKC * HS], BF16) for i in range(2)]
    vt_ck = [None, None]
    vt_used = [None, None]
    vaug = P.sbuf("e2_va", [128, NKC * (HS + 1)], BF16)
    va3 = vaug[:].rearrange("p (k f) -> p k f", f=HS + 1)
    S1 = [P.sbuf(f"e2_S1{i}", [128, LF], F32) for i in range(4)]
    S2 = P.sbuf("e2_S2", [128, LF], F32)
    s1_free = [None, None, None, None]
    m8a = P.sbuf("e2_m8", [128, 8], F32)
    tsp = P.sbuf("e2_tsp", [128, 2], BF16)
    tf = P.sbuf("e2_tf", [128, 1], F32)
    qaug = P.sbuf("e2_qa", [66, 512], BF16)
    em = [P.sbuf(f"e2_em{i}", [128, 512], BF16) for i in range(2)]
    pmat = [P.sbuf(f"e2_pm{i}", [128, 512], BF16) for i in range(2)]
    rb32 = P.sbuf("e2_rb32", [1, 512], F32)
    rb = P.sbuf("e2_rb", [1, 512], BF16)
    dnr = P.sbuf("e2_dnr", [128, 512], BF16)
    ones1 = P.sbuf("e2_o1", [1, 64], BF16)
    P.op("vector", lambda e: e.memset(ones1[:], 1.0))
    ones_ck = P.ck("vector")
    attn = [P.sbuf(f"e2_at{c}", [128, T], BF16) for c in range(NC)]
    psA = [P.psum(f"e2_psA{i}", [128, 512], F32) for i in range(2)]
    psB = [P.psum(f"e2_psB{i}", [128, 512], F32) for i in range(2)]
    ps_pv = [P.psum(f"e2_pv{i}", [65, 512], F32) for i in range(2)]
    psA_used = [None, None]
    psB_used = [None, None]
    em_used = [None, None]
    pm_used = [None, None]
    pv_used = [None, None]
    prev_rmm = None
    for h in range(NH):
        hk = h % 2
        qh_ck[hk] = P.dma("sync", qh[hk][:], q_hbm[h * HS:(h + 1) * HS, :],
                          waits=qh_used[hk])
        w = ((ka_used[hk],) if ka_used[hk] else ()) + (neg_init[hk],)
        ka_ck[hk] = P.dma("gpsimd", kaug[hk][0:64, :],
                          kf_hbm[h * HS:(h + 1) * HS, :], waits=w)
        w = ((vt_used[hk],) if vt_used[hk] else ())
        vt_ck[hk] = P.dma("gpsimd",
                          vth[hk][:].rearrange("p (k f) -> p k f", f=HS),
                          vtf_hbm.rearrange("(k p) f -> p k f", p=128)
                          [:, :, h * HS:(h + 1) * HS], waits=w)
        P.op("vector", (lambda hk=hk: lambda e: e.tensor_copy(
            va3[:, :, 0:HS],
            vth[hk][:].rearrange("p (k f) -> p k f", f=HS)))(),
            waits=(vt_ck[hk],) + tuple(c for c in pv_used if c))
        P.op("vector", lambda e: e.memset(va3[:, :, HS:HS + 1], 1.0))
        va_ck = P.ck("vector")
        vt_used[hk] = va_ck
        last_score = None
        qa_vck = None
        for g in range(NG):
            gq0 = g * 512
            fill_last = [None] * 4
            for j in range(4):
                t = g * 4 + j
                par = j
                for sl in range(8):
                    k = sl % 2
                    mw = (qh_ck[hk], ka_ck[hk]) + \
                         ((psA_used[k],) if psA_used[k] else ())
                    mck = P.op("tensor", (lambda t=t, k=k, hk=hk, sl=sl:
                                          lambda e: e.matmul(
                        out=psA[k][:],
                        lhsT=qh[hk][:, t * 128:(t + 1) * 128],
                        rhs=kaug[hk][0:64, sl * 512:(sl + 1) * 512],
                        start=True, stop=True))(), waits=mw)
                    last_score = mck
                    fw = (mck,)
                    if sl == 0 and s1_free[par] is not None:
                        fw = fw + (s1_free[par],)
                    P.op("scalar", (lambda par=par, sl=sl, k=k:
                                    lambda e: e.activation(
                        out=S1[par][:, sl * 512:(sl + 1) * 512],
                        in_=psA[k][:], func=AF.Copy))(), waits=fw)
                    psA_used[k] = P.ck("scalar")
                fill_last[j] = psA_used[1]
            for j in range(4):
                t = g * 4 + j
                par = j
                src, dst = S1[par], S2
                rr_ck = fill_last[j]
                for r in range(8):
                    P.op("vector", (lambda src=src: lambda e: e.max(
                        out=m8a[:], in_=src[:]))(), waits=(rr_ck,))
                    if r < 7:
                        P.op("vector", (lambda src=src, dst=dst:
                                        lambda e: e.match_replace(
                            out=dst[:], in_to_replace=m8a[:],
                            in_values=src[:], imm_value=-3e38))(),
                            waits=(P.ck("vector"),))
                        rr_ck = P.ck("vector")
                        if r == 6:
                            s1_free[par] = rr_ck
                        src, dst = dst, src
                P.op("vector", lambda e: e.tensor_copy(
                    tsp[:, 0:1], m8a[:, 7:8]), waits=(P.ck("vector"),))
                P.op("vector", lambda e: e.tensor_tensor(
                    out=tf[:], in0=m8a[:, 7:8], in1=tsp[:, 0:1],
                    op=ALU.subtract), waits=(P.ck("vector"),))
                P.op("vector", lambda e: e.tensor_copy(
                    tsp[:, 1:2], tf[:]), waits=(P.ck("vector"),))
                tsp_ck = P.ck("vector")
                qaw = (tsp_ck,) + ((prev_rmm,) if prev_rmm and j == 0 else ())
                tp.t(tsp[:], qaug[64:66, j * 128:(j + 1) * 128], waits=qaw)
            P.op("vector", (lambda hk=hk, gq0=gq0: lambda e: e.tensor_copy(
                qaug[0:64, :], qh[hk][:, gq0:gq0 + 512]))(),
                waits=(qh_ck[hk],) + ((prev_rmm,) if prev_rmm else ()))
            qa_vck = P.ck("vector")
            qa_ck = (qa_vck, P.ck("scalar"))
            kpv = g % 2
            pvw = ((pv_used[kpv],) if pv_used[kpv] else ())
            last_pv = None

            def emit_rm(kc):
                k2 = kc % 2
                mw = qa_ck + (ka_ck[hk],) + \
                     ((psB_used[k2],) if psB_used[k2] else ())
                return P.op("tensor", (lambda hk=hk, kc=kc, k2=k2:
                                       lambda e: e.matmul(
                    out=psB[k2][:],
                    lhsT=kaug[hk][0:66, kc * 128:(kc + 1) * 128],
                    rhs=qaug[:], start=True, stop=True))(), waits=mw)

            rm_ck = emit_rm(0)
            for kc in range(NKC):
                k2 = kc % 2
                ew = (rm_ck,) + ((em_used[k2],) if em_used[k2] else ())
                P.op("scalar", (lambda k2=k2: lambda e: e.activation(
                    out=em[k2][:], in_=psB[k2][:], func=AF.Exp))(), waits=ew)
                e_ck = P.ck("scalar")
                sw = (e_ck,) + ((pm_used[k2],) if pm_used[k2] else ())
                P.op("vector", (lambda k2=k2: lambda e:
                                e.scalar_tensor_tensor(
                    out=pmat[k2][:], in0=psB[k2][:], scalar=0.0,
                    in1=em[k2][:], op0=ALU.is_ge, op1=ALU.mult))(),
                    waits=sw)
                pm_ck = P.ck("vector")
                psB_used[k2] = pm_ck
                em_used[k2] = pm_ck
                if kc + 1 < NKC:
                    rm_ck = emit_rm(kc + 1)
                pvw2 = (pm_ck, va_ck) + (pvw if kc == 0 else ())
                last_pv = P.op("tensor", (lambda kc=kc, kpv=kpv, k2=k2:
                                          lambda e: e.matmul(
                    out=ps_pv[kpv][:], lhsT=va3[:, kc, :], rhs=pmat[k2][:],
                    start=(kc == 0), stop=(kc == NKC - 1)))(),
                    waits=pvw2, inc=1)
                pm_used[k2] = last_pv
                prev_rmm = last_pv
            P.op("vector", (lambda kpv=kpv: lambda e: e.reciprocal(
                out=rb32[:], in_=ps_pv[kpv][64:65, :]))(), waits=(last_pv,))
            P.op("vector", lambda e: e.tensor_copy(rb[:], rb32[:]),
                 waits=(P.ck("vector"),))
            rb_ck = P.ck("vector")
            bcw = (rb_ck, ones_ck) + \
                  ((psA_used[0],) if psA_used[0] else ())
            bcck = P.op("tensor", (lambda: lambda e: e.matmul(
                out=psA[0][0:64, :], lhsT=ones1[:], rhs=rb[:],
                start=True, stop=True))(), waits=bcw)
            P.op("vector", lambda e: e.tensor_copy(
                dnr[0:64, :], psA[0][0:64, :]), waits=(bcck,))
            dnr_ck = P.ck("vector")
            psA_used[0] = dnr_ck
            hb = (h % 2) * 64
            P.op("vector", (lambda h=h, hb=hb, gq0=gq0, kpv=kpv:
                            lambda e: e.tensor_tensor(
                out=attn[h // 2][hb:hb + 64, gq0:gq0 + 512],
                in0=ps_pv[kpv][0:64, :], in1=dnr[0:64, :],
                op=ALU.mult))(), waits=(dnr_ck,))
            pv_used[kpv] = P.ck("vector")
        ka_used[hk] = prev_rmm
        qh_used[hk] = (last_score, qa_vck)
    attn_ck = (P.ck("vector"),)

    wo_t = [P.sbuf(f"e2_wo{c}", [128, D], BF16) for c in range(NC)]
    wo_cks = [P.dma("gpsimd", wo_t[c][:], wo_hbm[c * 128:(c + 1) * 128, :])
              for c in range(NC)]
    bo_bc, bo_ck = bcast_row(P, consts, bo_hbm, D, "e2_bo", dtype=F32,
                             ps=psA[0][:],
                             ps_free=(psA_used[0],) + attn_ck)
    x_s = [P.sbuf(f"e2_x{i}", [128, D], F32) for i in range(2)]
    x_used = [None, None]
    psf_used = [bo_ck, attn_ck[0]]
    for t in range(NT):
        k = t % 2
        xck = P.dma("sync", x_s[k][:], xres_in[t * 128:(t + 1) * 128, :],
                    waits=((x_used[k],) if x_used[k] else ()))
        for half in range(2):
            mw = tuple(wo_cks) + attn_ck + \
                 ((psf_used[half],) if psf_used[half] else ())
            mck = None
            for c in range(NC):
                mck = P.op("tensor", (lambda c=c, t=t, half=half:
                                      lambda e: e.matmul(
                    out=psA[half][:],
                    lhsT=attn[c][:, t * 128:(t + 1) * 128],
                    rhs=wo_t[c][:, half * 512:(half + 1) * 512],
                    start=(c == 0), stop=(c == NC - 1)))(),
                    waits=mw if c == 0 else (), inc=(1 if c == NC - 1 else 0))
            P.op("vector", (lambda k=k, half=half: lambda e: e.tensor_tensor(
                out=x_s[k][:, half * 512:(half + 1) * 512],
                in0=x_s[k][:, half * 512:(half + 1) * 512],
                in1=psA[half][:], op=ALU.add))(), waits=(mck, xck))
            P.op("vector", (lambda k=k, half=half: lambda e: e.tensor_tensor(
                out=x_s[k][:, half * 512:(half + 1) * 512],
                in0=x_s[k][:, half * 512:(half + 1) * 512],
                in1=bo_bc[:, half * 512:(half + 1) * 512], op=ALU.add))(),
                waits=(bo_ck,))
            psf_used[half] = P.ck("vector")
        x_used[k] = P.dma("sync", xres_out[t * 128:(t + 1) * 128, :],
                          x_s[k][:], waits=(P.ck("vector"),))
    P.wait("sync", (x_used[0], x_used[1]))


import jax

_BUILT = {}


def _get_prog(name, fn):
    if name not in _BUILT:
        _BUILT[name] = fn()
    return _BUILT[name]


def _tonp(a):
    return np.asarray(jax.device_get(a))


def _shard_x(x, c):
    b, p = c // 2, c % 2
    blocks = [x[b, j * 512:(j + 1) * 512] for j in range(8) if j % 2 == p]
    return np.concatenate(blocks, axis=0)


def _unshard(parts):
    out = np.empty((4, 4096, 1024), np.float32)
    for b in range(4):
        for j in range(8):
            p, i = j % 2, j // 2
            out[b, j * 512:(j + 1) * 512] = parts[2 * b + p][i * 512:(i + 1) * 512]
    return out


def _interleave_full(parts, blk=512):
    out = []
    for j in range(8):
        p, i = j % 2, j // 2
        out.append(parts[p][..., i * blk:(i + 1) * blk])
    return np.concatenate(out, axis=-1)


def _interleave_lm(l0, l1):
    out = np.zeros((l0.shape[0], 64), l0.dtype)
    for p, l in ((0, l0), (1, l1)):
        for j in range(4):
            g = (p + 2 * j) * 8
            out[:, g:g + 8] = l[:, j * 8:(j + 1) * 8]
    return out


def _kb_a1():
    P = Prog()
    consts = Consts(P)
    tp = Transposer(P, consts)
    x_in = P.dram("x_in", [T, D], F32, kind="ExternalInput")
    wk = P.dram("wk", [D, D], F32, kind="ExternalInput")
    wq = P.dram("wq", [D, D], F32, kind="ExternalInput")
    wv = P.dram("wv", [D, D], F32, kind="ExternalInput")
    outs = {n: P.dram(f"o{n}", [D, T], BF16, kind="ExternalOutput")
            for n in "qkv"}
    ql = P.dram("oql", [D, ML], F32, kind="ExternalOutput")
    kl = P.dram("okl", [D, ML], F32, kind="ExternalOutput")
    emit_l0attnA_1(P, consts, tp, wk, wq, wv, x_in,
                   outs["q"], outs["k"], outs["v"], ql, kl)
    return P.build()


def _kb_a2():
    P = Prog()
    consts = Consts(P)
    tp = Transposer(P, consts)
    q = P.dram("q", [D, T], BF16, kind="ExternalInput")
    k = P.dram("k", [D, T], BF16, kind="ExternalInput")
    v = P.dram("v", [D, T], BF16, kind="ExternalInput")
    qlg = P.dram("qlg", [D, M], F32, kind="ExternalInput")
    klg = P.dram("klg", [D, M], F32, kind="ExternalInput")
    idt = P.dram("idt", [M, HT], BF16, kind="ExternalInput")
    s1T = P.dram("s1T", [D, T], BF16, kind="ExternalOutput")
    zsT = P.dram("zsT", [M, HT], F32, kind="ExternalOutput")
    mzs = P.dram("mzs", [M, NH * 66], F32, kind="ExternalOutput")
    emit_l0attnA_2(P, consts, tp, q, k, v, qlg, klg, idt, s1T, zsT, mzs)
    return P.build()


def _kb_a3():
    P = Prog()
    consts = Consts(P)
    s1T = P.dram("s1T", [D, T], BF16, kind="ExternalInput")
    zsT = P.dram("zsT", [M, HT], F32, kind="ExternalInput")
    mzo = P.dram("mzo", [M, NH * 66], F32, kind="ExternalInput")
    mzp = P.dram("mzp", [M, NH * 66], F32, kind="ExternalInput")
    wo = P.dram("wo", [D, D], F32, kind="ExternalInput")
    bo = P.dram("bo", [D], F32, kind="ExternalInput")
    x_in = P.dram("x_in", [T, D], F32, kind="ExternalInput")
    x_out = P.dram("x_out", [T, D], F32, kind="ExternalOutput")
    emit_l0attnA_3(P, None, None, s1T, zsT, mzo, mzp, wo, bo, x_in, x_out)
    return P.build()


def _kb_f():
    P = Prog()
    consts = Consts(P, f16=True)
    tp = Transposer(P, consts, dtype=F16)
    x_in = P.dram("x_in", [T, D], F32, kind="ExternalInput")
    wq = P.dram("wq", [D, 2 * 8 * 512], F32, kind="ExternalInput")
    keys = P.dram("keys", [8, 256, 2, 512], F32, kind="ExternalInput")
    wdn = P.dram("wdn", [NEXP, D], F32, kind="ExternalInput")
    wup = P.dram("wup", [NEXP, D], F32, kind="ExternalInput")
    x_out = P.dram("x_out", [T, D], F32, kind="ExternalOutput")
    emit_l0ffn(P, consts, tp, wq, keys, wdn, wup, x_in, x_out)
    return P.build()


def _kb_b1():
    P = Prog()
    consts = Consts(P)
    tp = Transposer(P, consts)
    x_in = P.dram("x_in", [T, D], F32, kind="ExternalInput")
    wqkv = P.dram("wqkv", [O3, D], F32, kind="ExternalInput")
    bqkv = P.dram("bqkv", [O3], F32, kind="ExternalInput")
    gq = P.dram("gq", [D], F32, kind="ExternalInput")
    gk = P.dram("gk", [D], F32, kind="ExternalInput")
    cos = P.dram("cos", [T, 512], F32, kind="ExternalInput")
    sin = P.dram("sin", [T, 512], F32, kind="ExternalInput")
    qfT = P.dram("qfT", [D, T], BF16, kind="ExternalOutput")
    vk = P.dram("vk", [64, 16 * 65], F32, kind="ExternalOutput")
    emit_l0attnB_1(P, consts, tp, wqkv, bqkv, gq, gk, cos, sin, x_in,
                   qfT, vk)
    return P.build()


def _kb_b2():
    P = Prog()
    consts = Consts(P)
    tp = Transposer(P, consts)
    qfT = P.dram("qfT", [D, T], BF16, kind="ExternalInput")
    vko = P.dram("vko", [64, 16 * 65], F32, kind="ExternalInput")
    vkp = P.dram("vkp", [64, 16 * 65], F32, kind="ExternalInput")
    wout = P.dram("wout", [D, D], F32, kind="ExternalInput")
    bout = P.dram("bout", [D], F32, kind="ExternalInput")
    gout = P.dram("gout", [D], F32, kind="ExternalInput")
    x_in = P.dram("x_in", [T, D], F32, kind="ExternalInput")
    x_out = P.dram("x_out", [T, D], F32, kind="ExternalOutput")
    emit_l0attnB_2(P, consts, tp, qfT, vko, vkp, wout, bout, gout,
                   x_in, x_out)
    return P.build()


def _kb_c1():
    P = Prog()
    consts = Consts(P)
    tp = Transposer(P, consts)
    x_in = P.dram("x_in", [T, D], F32, kind="ExternalInput")
    ws = [P.dram(f"w{n}", [D, D], F32, kind="ExternalInput") for n in "qkv"]
    bs = [P.dram(f"b{n}", [D], F32, kind="ExternalInput") for n in "qkv"]
    outs = [P.dram(f"o{n}", [D, T], BF16, kind="ExternalOutput") for n in "qkv"]
    vt = P.dram("ovt", [T, D], BF16, kind="ExternalOutput")
    emit_qkv_heads(P, consts, tp, x_in, ws, bs, outs, vt_idx=2, vt_hbm=vt)
    return P.build()


def _kb_c2():
    P = Prog()
    consts = Consts(P)
    tp = Transposer(P, consts)
    q = P.dram("q", [D, T], BF16, kind="ExternalInput")
    k = P.dram("k", [D, LFULL], BF16, kind="ExternalInput")
    vt = P.dram("vt", [LFULL, D], BF16, kind="ExternalInput")
    cm = P.dram("cm", [128, 2048], BF16, kind="ExternalInput")
    wo = P.dram("wo", [D, D], F32, kind="ExternalInput")
    wob = P.dram("wob", [D], F32, kind="ExternalInput")
    x_in = P.dram("x_in", [T, D], F32, kind="ExternalInput")
    x_out = P.dram("x_out", [T, D], F32, kind="ExternalOutput")
    emit_l1attnA_core(P, consts, tp, 1, q, k, vt, cm, wo, wob,
                      x_in, x_out)
    return P.build()


def _kb_dw():
    P = Prog()
    consts = Consts(P)
    tp = Transposer(P, consts)
    w_gate = P.dram("w_gate", [I_FF // 8, D], F32, kind="ExternalInput")
    w_up = P.dram("w_up", [I_FF // 8, D], F32, kind="ExternalInput")
    w_down = P.dram("w_down", [D // 8, I_FF], F32, kind="ExternalInput")
    o_gT = P.dram("o_gT", [D, I_FF // 8], BF16, kind="ExternalOutput")
    o_uT = P.dram("o_uT", [D, I_FF // 8], BF16, kind="ExternalOutput")
    o_dT = P.dram("o_dT", [I_FF, D // 8], BF16, kind="ExternalOutput")
    emit_l1ffn_wprep(P, consts, tp, w_gate, w_up, w_down, o_gT, o_uT, o_dT,
                     I_FF // 8, D // 8)
    return P.build()


def _kb_d():
    P = Prog()
    consts = Consts(P)
    tp = Transposer(P, consts)
    x_in = P.dram("x_in", [T, D], F32, kind="ExternalInput")
    wgT = P.dram("wgT", [D, I_FF], BF16, kind="ExternalInput")
    wuT = P.dram("wuT", [D, I_FF], BF16, kind="ExternalInput")
    wdT = P.dram("wdT", [I_FF, D], BF16, kind="ExternalInput")
    x_out = P.dram("x_out", [T, D], F32, kind="ExternalOutput")
    emit_l1ffn(P, consts, tp, x_in, wgT, wuT, wdT, x_out)
    return P.build()


def _kb_e1():
    P = Prog()
    consts = Consts(P)
    tp = Transposer(P, consts)
    x_in = P.dram("x_in", [T, D], F32, kind="ExternalInput")
    ws = [P.dram(f"w{n}", [D, D], F32, kind="ExternalInput") for n in "qkv"]
    bs = [P.dram(f"b{n}", [D], F32, kind="ExternalInput") for n in "qkv"]
    oq = P.dram("oq", [D, T], BF16, kind="ExternalOutput")
    ok = P.dram("ok", [D, T], BF16, kind="ExternalOutput")
    vt = P.dram("ovt", [T, D], BF16, kind="ExternalOutput")
    emit_l1attnB_1(P, consts, tp, ws, bs, [oq, ok, None], vt, x_in)
    return P.build()


def _kb_e2():
    P = Prog()
    consts = Consts(P)
    tp = Transposer(P, consts)
    q = P.dram("q", [D, T], BF16, kind="ExternalInput")
    kf = P.dram("kf", [D, LF], BF16, kind="ExternalInput")
    vtf = P.dram("vtf", [LF, D], BF16, kind="ExternalInput")
    wo = P.dram("wo", [D, D], F32, kind="ExternalInput")
    bo = P.dram("bo", [D], F32, kind="ExternalInput")
    x_in = P.dram("x_in", [T, D], F32, kind="ExternalInput")
    x_out = P.dram("x_out", [T, D], F32, kind="ExternalOutput")
    emit_l1attnB_2(P, consts, tp, q, kf, vtf, wo, bo, x_in, x_out)
    return P.build()


def kernel(**inp):
    x = np.asarray(inp["L0_attn_a_x"], np.float32)
    ch = Consts.host_arrays()
    idt = ident_tiled_host()
    cmask = causal_masks_host()
    tabs = [rope_tables_host(p) for p in range(2)]
    cores = list(range(8))
    xs = [_shard_x(x, c) for c in cores]

    nc = _get_prog("a1", _kb_a1)
    ins = [{"x_in": xs[c], "wk": inp["L0_attn_a_Wk"],
            "wq": inp["L0_attn_a_Wq"], "wv": inp["L0_attn_a_Wv"], **ch}
           for c in cores]
    r1 = run_bass_kernel_spmd(nc, ins, core_ids=cores)

    nc = _get_prog("a2", _kb_a2)
    ins = []
    for c in cores:
        b = c // 2
        pair = [2 * b, 2 * b + 1]
        qlg = _interleave_lm(_tonp(r1.results[pair[0]]["oql"]),
                             _tonp(r1.results[pair[1]]["oql"]))
        klg = _interleave_lm(_tonp(r1.results[pair[0]]["okl"]),
                             _tonp(r1.results[pair[1]]["okl"]))
        ins.append({"q": _tonp(r1.results[c]["oq"]),
                    "k": _tonp(r1.results[c]["ok"]),
                    "v": _tonp(r1.results[c]["ov"]),
                    "qlg": qlg, "klg": klg, "idt": idt, **ch})
    r2 = run_bass_kernel_spmd(nc, ins, core_ids=cores)

    nc = _get_prog("a3", _kb_a3)
    ins = []
    for c in cores:
        pair = c + 1 if c % 2 == 0 else c - 1
        ins.append({"s1T": _tonp(r2.results[c]["s1T"]),
                    "zsT": _tonp(r2.results[c]["zsT"]),
                    "mzo": _tonp(r2.results[c]["mzs"]),
                    "mzp": _tonp(r2.results[pair]["mzs"]),
                    "wo": inp["L0_attn_a_Wo"], "bo": inp["L0_attn_a_bo"],
                    "x_in": xs[c], **ch})
    r3 = run_bass_kernel_spmd(nc, ins, core_ids=cores)
    xs = [_tonp(r3.results[c]["x_out"]) for c in cores]

    nc = _get_prog("f", _kb_f)
    ins = [{"x_in": xs[c], "wq": inp["L0_ffn_w_q"],
            "keys": inp["L0_ffn_keys"], "wdn": inp["L0_ffn_w_down"],
            "wup": inp["L0_ffn_w_up"], **ch} for c in cores]
    rf = run_bass_kernel_spmd(nc, ins, core_ids=cores)
    xs = [_tonp(rf.results[c]["x_out"]) for c in cores]

    nc = _get_prog("b1", _kb_b1)
    ins = [{"x_in": xs[c], "wqkv": inp["L0_attn_b_Wqkv"],
            "bqkv": inp["L0_attn_b_bqkv"], "gq": inp["L0_attn_b_gq"],
            "gk": inp["L0_attn_b_gk"], "cos": tabs[c % 2][0],
            "sin": tabs[c % 2][1], **ch} for c in cores]
    rb1 = run_bass_kernel_spmd(nc, ins, core_ids=cores)

    nc = _get_prog("b2", _kb_b2)
    ins = []
    for c in cores:
        pair = c + 1 if c % 2 == 0 else c - 1
        ins.append({"qfT": _tonp(rb1.results[c]["qfT"]),
                    "vko": _tonp(rb1.results[c]["vk"]),
                    "vkp": _tonp(rb1.results[pair]["vk"]),
                    "wout": inp["L0_attn_b_Wout"],
                    "bout": inp["L0_attn_b_bout"],
                    "gout": inp["L0_attn_b_gout"],
                    "x_in": xs[c], **ch})
    rb2 = run_bass_kernel_spmd(nc, ins, core_ids=cores)
    xs = [_tonp(rb2.results[c]["x_out"]) for c in cores]

    nc = _get_prog("f", _kb_f)
    ins = [{"x_in": xs[c], "wq": inp["L0_ffn_w_q"],
            "keys": inp["L0_ffn_keys"], "wdn": inp["L0_ffn_w_down"],
            "wup": inp["L0_ffn_w_up"], **ch} for c in cores]
    rf = run_bass_kernel_spmd(nc, ins, core_ids=cores)
    xs = [_tonp(rf.results[c]["x_out"]) for c in cores]

    nc = _get_prog("c1", _kb_c1)
    ins = [{"x_in": xs[c], "wq": inp["L1_attn_a_wq_w"],
            "wk": inp["L1_attn_a_wk_w"], "wv": inp["L1_attn_a_wv_w"],
            "bq": inp["L1_attn_a_wq_b"], "bk": inp["L1_attn_a_wk_b"],
            "bv": inp["L1_attn_a_wv_b"], **ch} for c in cores]
    rc1 = run_bass_kernel_spmd(nc, ins, core_ids=cores)

    nc = _get_prog("c2", _kb_c2)
    ins = []
    for c in cores:
        b, p = c // 2, c % 2
        pair = [2 * b, 2 * b + 1]
        k_full = _interleave_full([_tonp(rc1.results[pair[0]]["ok"]),
                                   _tonp(rc1.results[pair[1]]["ok"])])
        vt_full = _interleave_full(
            [_tonp(rc1.results[pair[0]]["ovt"]).T,
             _tonp(rc1.results[pair[1]]["ovt"]).T]).T
        vt_full = np.ascontiguousarray(vt_full)
        if p == 0:
            k_full = np.concatenate(
                [np.zeros_like(k_full[:, :512]), k_full[:, :3584]], axis=1)
            vt_full = np.concatenate(
                [np.zeros_like(vt_full[:512]), vt_full[:3584]], axis=0)
        ins.append({"q": _tonp(rc1.results[c]["oq"]), "k": k_full,
                    "vt": vt_full, "cm": cmask,
                    "wo": inp["L1_attn_a_wo_w"], "wob": inp["L1_attn_a_wo_b"],
                    "x_in": xs[c], **ch})
    rc2 = run_bass_kernel_spmd(nc, ins, core_ids=cores)
    xs = [_tonp(rc2.results[c]["x_out"]) for c in cores]

    nc = _get_prog("dw", _kb_dw)
    ins = []
    for c in cores:
        ins.append({"w_gate": inp["L1_ffn_w_gate"][c * 352:(c + 1) * 352],
                    "w_up": inp["L1_ffn_w_up"][c * 352:(c + 1) * 352],
                    "w_down": inp["L1_ffn_w_down"][c * 128:(c + 1) * 128],
                    **ch})
    rw = run_bass_kernel_spmd(nc, ins, core_ids=cores)
    wgT = np.concatenate([_tonp(rw.results[c]["o_gT"]) for c in cores], axis=1)
    wuT = np.concatenate([_tonp(rw.results[c]["o_uT"]) for c in cores], axis=1)
    wdT = np.concatenate([_tonp(rw.results[c]["o_dT"]) for c in cores], axis=1)

    nc = _get_prog("d", _kb_d)
    ins = [{"x_in": xs[c], "wgT": wgT, "wuT": wuT, "wdT": wdT, **ch}
           for c in cores]
    rd = run_bass_kernel_spmd(nc, ins, core_ids=cores)
    xs = [_tonp(rd.results[c]["x_out"]) for c in cores]

    nc = _get_prog("e1", _kb_e1)
    ins = [{"x_in": xs[c], "wq": inp["L1_attn_b_Wq"],
            "wk": inp["L1_attn_b_Wk"], "wv": inp["L1_attn_b_Wv"],
            "bq": inp["L1_attn_b_bq"], "bk": inp["L1_attn_b_bk"],
            "bv": inp["L1_attn_b_bv"], **ch} for c in cores]
    re1 = run_bass_kernel_spmd(nc, ins, core_ids=cores)

    nc = _get_prog("e2", _kb_e2)
    ins = []
    for c in cores:
        b = c // 2
        pair = [2 * b, 2 * b + 1]
        k_full = _interleave_full([_tonp(re1.results[pair[0]]["ok"]),
                                   _tonp(re1.results[pair[1]]["ok"])])
        vt_full = _interleave_full(
            [_tonp(re1.results[pair[0]]["ovt"]).T,
             _tonp(re1.results[pair[1]]["ovt"]).T]).T
        vt_full = np.ascontiguousarray(vt_full)
        ins.append({"q": _tonp(re1.results[c]["oq"]), "kf": k_full,
                    "vtf": vt_full, "wo": inp["L1_attn_b_Wo"],
                    "bo": inp["L1_attn_b_bo"], "x_in": xs[c], **ch})
    re2 = run_bass_kernel_spmd(nc, ins, core_ids=cores)
    xs = [_tonp(re2.results[c]["x_out"]) for c in cores]

    nc = _get_prog("d", _kb_d)
    ins = [{"x_in": xs[c], "wgT": wgT, "wuT": wuT, "wdT": wdT, **ch}
           for c in cores]
    rd = run_bass_kernel_spmd(nc, ins, core_ids=cores)
    xs = [_tonp(rd.results[c]["x_out"]) for c in cores]

    return _unshard(xs)
